# version 10
# baseline (speedup 1.0000x reference)
import numpy as np
import ml_dtypes
from contextlib import ExitStack

import concourse.bass as bass
import concourse.mybir as mybir
from concourse.bass_utils import run_bass_kernel_spmd

F32 = mybir.dt.float32
BF16 = mybir.dt.bfloat16
U8 = mybir.dt.uint8
AF = mybir.ActivationFunctionType
ALU = mybir.AluOpType

B, T, DM, DI, N, K = 4, 4096, 1024, 2048, 16, 4
NDT = 8
TCA = 512
TCS = 512
EPS = 1e-5
VC1 = float(2.0 ** 20 + 1)
MAGIC = float(3 * 2 ** 13)


class Ctx:
    ENGS = ("sync", "vector", "scalar", "tensor", "gpsimd")

    def __init__(self, nc, es):
        self.nc = nc
        self._es = es
        self.ops = {e: [] for e in self.ENGS}
        self.sems = {}
        self.cnt = {}
        for s in ("LD", "TE", "VE", "SE", "SY"):
            self.sem(s)

    def sem(self, name):
        if name not in self.sems:
            self.sems[name] = self._es.enter_context(self.nc.semaphore(name))
            self.cnt[name] = 0
        return self.sems[name]

    def em(self, eng, fn):
        self.ops[eng].append(fn)

    def wait(self, eng, sem, val=None):
        v = self.cnt[sem] if val is None else val
        if v > 0:
            self.em(eng, lambda e, s=self.sems[sem], v=v: e.wait_ge(s, v))
        return v

    def dma(self, eng, out, in_, sem="LD", **kw):
        self.sem(sem)
        self.cnt[sem] += 16
        s = self.sems[sem]
        self.em(eng, lambda e, o=out, i=in_, kw=kw, s=s:
                e.dma_start(out=o, in_=i, **kw).then_inc(s, 16))
        return self.cnt[sem]

    def op(self, eng, meth, *a, sem=None, **kw):
        if meth != "nop":
            self.em(eng, lambda e, a=a, kw=kw: getattr(e, meth)(*a, **kw))
        if eng in ("vector", "scalar") or (eng == "tensor" and sem is not None):
            if sem is not None:
                self.sem(sem)
                self.cnt[sem] += 1
                s = self.sems[sem]
                self.em(eng, lambda e, s=s: e.drain().then_inc(s, 1))
                return self.cnt[sem]
            self.em(eng, lambda e: e.drain())
            return None
        if sem is not None:
            self.sem(sem)
            self.cnt[sem] += 1
            s = self.sems[sem]
            self.em(eng, lambda e, s=s: e.nop().then_inc(s, 1))
            return self.cnt[sem]
        return None

    def barrier(self):
        self.cnt["SY"] += 5
        tgt = self.cnt["SY"]
        for eng in self.ENGS:
            if eng in ("vector", "scalar", "tensor", "gpsimd"):
                self.em(eng, lambda e: e.drain())
            self.em(eng, lambda e, s=self.sems["SY"]: e.sem_inc(s, 1))
        for eng in self.ENGS:
            self.em(eng, lambda e, s=self.sems["SY"], v=tgt: e.wait_ge(s, v))

    def replay(self, blk):
        for eng in self.ENGS:
            ops = self.ops[eng]
            getattr(blk, eng)(lambda e, ops=ops: [f(e) for f in ops])


def _ap(t, off, *dims):
    return bass.AP(t.tensor if isinstance(t, bass.AP) else t, off, [list(d) for d in dims])



def build_l1(debug=()):
    nc = bass.Bass("TRN2", target_bir_lowering=False, debug=False)
    xb = nc.dram_tensor("xb", [T, DM], F32, kind="ExternalInput").ap()
    w_in_s = nc.dram_tensor("w_in_s", [24 * 128, DM], F32, kind="ExternalInput").ap()
    b_in_s = nc.dram_tensor("b_in_s", [128, 24], F32, kind="ExternalInput").ap()
    w_conv_s = nc.dram_tensor("w_conv_s", [128, 16 * K], F32, kind="ExternalInput").ap()
    b_conv_s = nc.dram_tensor("b_conv_s", [128, 16], F32, kind="ExternalInput").ap()
    w_x_s = nc.dram_tensor("w_x_s", [9 * 128, DI], F32, kind="ExternalInput").ap()
    b_x_s = nc.dram_tensor("b_x_s", [128, 9], F32, kind="ExternalInput").ap()
    a_log_s = nc.dram_tensor("a_log_s", [128, NDT * N], F32, kind="ExternalInput").ap()
    d_s = nc.dram_tensor("d_s", [128, NDT], F32, kind="ExternalInput").ap()
    eye_in = nc.dram_tensor("eye_in", [128, 128], BF16, kind="ExternalInput").ap()
    yg = nc.dram_tensor("yg", [128, NDT, T], BF16, kind="ExternalOutput").ap()
    dbg_t = {}
    for name, shape, dt in debug:
        dbg_t[name] = nc.dram_tensor("dbg_" + name, shape, dt, kind="ExternalOutput").ap()
    xi_h = nc.dram_tensor("xi_h", [128, NDT, T], BF16).ap()
    dl_h = nc.dram_tensor("dl_h", [128, NDT, T], F32).ap()
    rs_h = nc.dram_tensor("rs_h", [128, NDT, T], F32).ap()
    bc_h = nc.dram_tensor("bc_h", [32, T], BF16).ap()

    es = ExitStack()
    with es:
        c = Ctx(nc, es)
        sb = lambda n, s, d=F32: es.enter_context(nc.sbuf_tensor(n, s, d))
        eye = sb("eye", [128, 128], BF16)
        b_in_t = sb("b_in_t", [128, 24])
        b_conv_t = sb("b_conv_t", [128, 16])
        b_x_t = sb("b_x_t", [128, 9])
        wconv_t = sb("wconv_t", [128, 16 * K])
        a_t = sb("a_t", [128, NDT * N])
        d_t = sb("d_t", [128, NDT])
        ceps = sb("ceps", [128, 1])
        states = sb("states", [128, NDT * N])
        es_ab = es.enter_context(ExitStack())
        sba = lambda n, s, d=F32: es_ab.enter_context(nc.sbuf_tensor(n, s, d))
        winT = sba("winT", [128, 8, 24 * 128], BF16)
        wxT = sba("wxT", [128, 16, 9 * 128], BF16)
        diag = sba("diag", [128, 16 * K, 128], BF16)

        c.op("vector", "memset", ceps[:], EPS)
        c.op("vector", "memset", states[:], 0.0)
        for dst, src in ((b_in_t, b_in_s), (b_conv_t, b_conv_s), (b_x_t, b_x_s),
                         (wconv_t, w_conv_s), (a_t, a_log_s), (d_t, d_s), (eye, eye_in)):
            c.dma("sync", dst[:], src)
        ld0 = c.cnt["LD"]
        c.wait("scalar", "LD", ld0)
        c.op("scalar", "activation", a_t[:], a_t[:], AF.Exp, sem="SE")
        c.wait("vector", "SE")
        c.op("vector", "tensor_scalar_mul", a_t[:], a_t[:], -1.0)
        c.wait("vector", "LD", ld0)
        for i in range(16 * K):
            c.op("vector", "tensor_scalar_mul", diag[:, i, :], eye[:], wconv_t[:, i:i + 1])
        c.op("vector", "nop", sem="VE")

        es_stg = ExitStack()
        stg_f = es_stg.enter_context(nc.sbuf_tensor("stg_f", [128, 2, DI], F32))
        stg_b = es_stg.enter_context(nc.sbuf_tensor("stg_b", [128, 2, DI], BF16))

        tr_done = [0, 0]

        def prep_w(src, n_ct, kdim, dstT):
            for ct in range(n_ct):
                p = ct % 2
                if tr_done[p]:
                    c.wait("sync", "WT%d" % p, tr_done[p])
                ld = c.dma("sync", stg_f[:, p, 0:kdim], src[ct * 128:(ct + 1) * 128, :],
                           sem="WL%d" % p)
                c.wait("vector", "WL%d" % p, ld)
                c.wait("vector", "WT%d" % p, tr_done[p])
                cd = c.op("vector", "tensor_copy", stg_b[:, p, 0:kdim],
                          stg_f[:, p, 0:kdim], sem="VE")
                c.wait("sync", "VE", cd)
                tr_done[p] = c.dma("sync", dstT[:, :, ct * 128:(ct + 1) * 128],
                                   stg_b[:, p, 0:kdim], transpose=True, sem="WT%d" % p)
        prep_w(w_in_s, 24, DM, winT)
        prep_w(w_x_s, 9, DI, wxT)
        c.wait("sync", "WT0")
        c.wait("sync", "WT1")
        c.wait("sync", "LD")
        c.barrier()
        es_stg.close()

        xtok = sba("xtok", [128, 1, 4, DM])
        scr = sba("scr", [128, DM])
        ssum = sba("ssum", [128, 1, 4])
        ssq = sba("ssq", [128, 1, 4])
        stat = sba("stat", [128, 1, 12])
        vtmp = sba("vtmp", [128, 4])
        xntok = sba("xntok", [128, 1, 4, DM], BF16)
        xnT = sba("xnT", [128, 1, 8, TCA], BF16)
        xipre = sba("xipre", [128, 16, 3 + TCA], BF16)
        xichk = sba("xichk", [128, 1, 16, TCA], BF16)
        reschk = sba("reschk", [128, 1, NDT, TCA])
        dlchk = sba("dlchk", [128, 1, NDT, TCA])
        bcchk = sba("bcchk", [32, 1, TCA], BF16)
        sps = es_ab.enter_context(nc.psum_tensor("sps", [128, 2, TCA], F32))
        cps = es_ab.enter_context(nc.psum_tensor("cps", [128, 2, TCA], F32))
        pps = es_ab.enter_context(nc.psum_tensor("pps", [128, 2, TCA], F32))

        c.op("vector", "memset", xipre[:, :, 0:3], 0.0)
        c.op("vector", "nop", sem="VE")

        for _sm in ("RS", "DS", "BS", "XS", "YS0", "YS1", "OS0", "OS1"):
            c.sem(_sm)
        NCH = 0 if "ab" in skip else T // TCA
        xn_ve = [0, 0]
        xr_mm = [0, 0]
        ev_ve = {}
        ev_se = {}
        conv_mm = 0
        xi_st = [0, 0]
        dl_st = [0, 0]
        rs_st = [0, 0]
        bc_st = [0, 0]
        for tc in range(NCH):
            pb = 0
            xin = _ap(xb, tc * TCA * DM, (DM, 128), (128 * DM, 4), (1, DM))
            c.wait("sync", "VE", xn_ve[pb])
            ldx = c.dma("sync", xtok[:, pb, :, :], xin, sem="XL")
            c.wait("scalar", "XL", ldx)
            for j in range(4):
                c.op("scalar", "activation", scr[:], xtok[:, pb, j, :], AF.Identity,
                     accum_out=ssum[:, pb, j:j + 1])
                c.op("scalar", "activation", scr[:], xtok[:, pb, j, :], AF.Square,
                     accum_out=ssq[:, pb, j:j + 1])
            st_se = c.op("scalar", "nop", sem="SE")
            c.wait("vector", "SE", st_se)
            c.op("vector", "tensor_scalar_mul", stat[:, pb, 0:4], ssum[:, pb, :], -1.0 / DM)
            c.op("vector", "tensor_scalar_mul", stat[:, pb, 4:8], ssq[:, pb, :], 1.0 / DM)
            c.op("vector", "tensor_tensor", vtmp[:], stat[:, pb, 0:4],
                 stat[:, pb, 0:4], ALU.mult)
            vw = c.op("vector", "tensor_sub", stat[:, pb, 4:8], stat[:, pb, 4:8],
                      vtmp[:], sem="VE")
            c.wait("scalar", "VE", vw)
            c.op("scalar", "activation", stat[:, pb, 4:8], stat[:, pb, 4:8],
                 AF.Ln, bias=ceps[:, 0:1])
            iv_se = c.op("scalar", "activation", stat[:, pb, 8:12], stat[:, pb, 4:8],
                         AF.Exp, scale=-0.5, sem="SE")
            c.wait("vector", "SE", iv_se)
            c.wait("vector", "XL", ldx)
            for j in range(4):
                c.op("vector", "tensor_scalar", xntok[:, pb, j, :], xtok[:, pb, j, :],
                     stat[:, pb, j:j + 1], stat[:, pb, 8 + j:9 + j], ALU.add, ALU.mult)
            xn_ve[pb] = c.op("vector", "nop", sem="VE")
            if "xn" in dbg_t and tc == 0:
                c.wait("gpsimd", "VE", xn_ve[pb])
                c.dma("gpsimd", dbg_t["xn"], xntok[:, 0, :, :])
            c.wait("sync", "VE", xn_ve[pb])
            c.wait("sync", "TE", xr_mm[pb])
            for j in range(4):
                trx = c.dma("sync", xnT[:, pb, :, j * 128:(j + 1) * 128],
                            xntok[:, pb, j, :], transpose=True, sem="XT")
            c.wait("tensor", "XT", trx)
            c.wait("tensor", "TE", conv_mm)
            for ct in range(24):
                pp = ct % 2
                c.wait("tensor", "VE", ev_ve.get(("xr", pp), 0))
                c.wait("tensor", "SE", ev_se.get(("xr", pp), 0))
                for kc in range(8):
                    c.op("tensor", "matmul", sps[:, pp, :],
                         winT[:, kc, ct * 128:(ct + 1) * 128],
                         xnT[:, pb, kc, :], start=(kc == 0), stop=(kc == 7))
                mm = c.op("tensor", "nop", sem="TE")
                if ct < 16:
                    c.wait("vector", "TE", mm)
                    ev_ve[("xr", pp)] = c.op(
                        "vector", "tensor_scalar_add", xipre[:, ct, 3:3 + TCA],
                        sps[:, pp, :], b_in_t[:, ct:ct + 1], sem="VE")
                else:
                    ct8 = ct - 16
                    c.wait("scalar", "TE", mm)
                    c.wait("scalar", "RS", rs_st[pb])
                    rs_st[pb] = 0
                    ev_se[("xr", pp)] = c.op(
                        "scalar", "activation", reschk[:, pb, ct8, :], sps[:, pp, :],
                        AF.Silu, bias=b_in_t[:, ct:ct + 1], sem="SE")
            xr_mm[pb] = c.cnt["TE"]
            c.wait("gpsimd", "SE", ev_se[("xr", 1)])
            c.wait("gpsimd", "SE", ev_se[("xr", 0)])
            rs_st[pb] = c.dma("gpsimd", rs_h[:, :, tc * TCA:(tc + 1) * TCA],
                              reschk[:, pb, :, :], sem="RS")
            if "xipre" in dbg_t and tc == 0:
                c.wait("gpsimd", "VE", ev_ve[("xr", 1)])
                c.wait("gpsimd", "VE", ev_ve[("xr", 0)])
                c.dma("gpsimd", dbg_t["xipre"], xipre[:, :, 3:3 + TCA])
            c.wait("tensor", "VE", ev_ve[("xr", 0)])
            c.wait("tensor", "VE", ev_ve[("xr", 1)])
            c.wait("scalar", "XS", xi_st[pb])
            for dt in range(16):
                pp = dt % 2
                c.wait("tensor", "SE", ev_se.get(("cv", pp), 0))
                for k in range(K):
                    c.op("tensor", "matmul", cps[:, pp, :], diag[:, dt * K + k, :],
                         xipre[:, dt, k:k + TCA], start=(k == 0), stop=(k == K - 1))
                mm = c.op("tensor", "nop", sem="TE")
                c.wait("scalar", "TE", mm)
                ev_se[("cv", pp)] = c.op(
                    "scalar", "activation", xichk[:, pb, dt, :], cps[:, pp, :],
                    AF.Silu, bias=b_conv_t[:, dt:dt + 1], sem="SE")
            conv_mm = c.cnt["TE"]
            c.wait("vector", "TE", conv_mm)
            c.op("vector", "tensor_copy", xipre[:, :, 0:3],
                 xipre[:, :, TCA:TCA + 3], sem="VE")
            c.wait("gpsimd", "SE", ev_se[("cv", 0)])
            c.wait("gpsimd", "SE", ev_se[("cv", 1)])
            xi_st[pb] = c.dma("gpsimd", xi_h[:, :, tc * TCA:(tc + 1) * TCA],
                              xichk[:, pb, 0:NDT, :], sem="XS")
            c.wait("tensor", "SE", ev_se[("cv", 0)])
            c.wait("tensor", "SE", ev_se[("cv", 1)])
            for ct in range(9):
                pp = ct % 2
                c.wait("tensor", "SE", ev_se.get(("pj", pp), 0))
                c.wait("tensor", "VE", ev_ve.get(("pj", pp), 0))
                for kc in range(16):
                    c.op("tensor", "matmul", pps[:, pp, :],
                         wxT[:, kc, ct * 128:(ct + 1) * 128],
                         xichk[:, pb, kc, :], start=(kc == 0), stop=(kc == 15))
                mm = c.op("tensor", "nop", sem="TE")
                if ct < 8:
                    c.wait("scalar", "TE", mm)
                    c.wait("scalar", "DS", dl_st[pb])
                    dl_st[pb] = 0
                    c.op("scalar", "activation", scr[:, 0:TCA], pps[:, pp, :],
                         AF.Exp, bias=b_x_t[:, ct:ct + 1])
                    ev_se[("pj", pp)] = c.op(
                        "scalar", "activation", dlchk[:, pb, ct, :], scr[:, 0:TCA],
                        AF.Ln, bias=1.0, sem="SE")
                else:
                    c.wait("vector", "TE", mm)
                    c.wait("vector", "BS", bc_st[pb])
                    bc_st[pb] = 0
                    ev_ve[("pj", pp)] = c.op(
                        "vector", "tensor_scalar_add", bcchk[:, pb, :],
                        pps[0:32, pp, :], b_x_t[0:32, ct:ct + 1], sem="VE")
            c.wait("gpsimd", "SE", ev_se[("pj", 0)])
            c.wait("gpsimd", "SE", ev_se[("pj", 1)])
            dl_st[pb] = c.dma("gpsimd", dl_h[:, :, tc * TCA:(tc + 1) * TCA],
                              dlchk[:, pb, :, :], sem="DS")
            c.wait("gpsimd", "VE", ev_ve[("pj", 0)])
            bc_st[pb] = c.dma("gpsimd", bc_h[:, tc * TCA:(tc + 1) * TCA],
                              bcchk[:, pb, :], sem="BS")
        for sm in ("RS", "DS", "BS", "XS"):
            c.wait("gpsimd", sm)
        c.barrier()
        es_ab.close()

        bcB = sb("bcB", [128, N, TCS], BF16)
        bcC = sb("bcC", [128, N, TCS], BF16)
        dls = sb("dls", [128, 2, TCS])
        xis = sb("xis", [128, 2, TCS], BF16)
        rss = sb("rss", [128, 2, TCS])
        wt = sb("wt", [128, 2, TCS], BF16)
        ut = sb("ut", [128, 2, TCS], BF16)
        dat = sb("dat", [128, 2, TCS])
        dbt = sb("dbt", [128, 2, TCS], BF16)
        ht = sb("ht", [128, 2, TCS], BF16)
        tnc = sb("tnc", [128, 4, TCS], BF16)
        g1 = sb("g1", [128, 2, TCS])
        ygt = sb("ygt", [128, 2, TCS], BF16)
        yps = es.enter_context(nc.psum_tensor("yps", [128, 2, TCS], F32))

        NSC = 0 if "scan" in skip else T // TCS
        last_cons = [0, 0]
        last_gate = [0, 0]
        last_ygdma = [0, 0]
        last_ymm = [0, 0, 0, 0]
        last_tnc = [0, 0]
        prev_chunk_ymm = 0
        for tc in range(NSC):
            c.wait("sync", "TE", prev_chunk_ymm)
            c.wait("sync", "BS")
            bcl = 0
            for n in range(N):
                c.dma("sync", bcB[:, n, :],
                      _ap(bc_h, n * T + tc * TCS, (0, 128), (1, TCS)), sem="BCL")
                bcl = c.dma("sync", bcC[:, n, :],
                            _ap(bc_h, (16 + n) * T + tc * TCS, (0, 128), (1, TCS)), sem="BCL")
            for dt in range(NDT):
                pd = dt % 2
                sl = slice(tc * TCS, (tc + 1) * TCS)
                c.wait("sync", "VE", last_cons[pd])
                c.dma("sync", dls[:, pd, :], dl_h[:, dt, sl], sem="SL%d" % pd)
                c.dma("sync", xis[:, pd, :], xi_h[:, dt, sl], sem="SL%d" % pd)
                sld = c.dma("sync", rss[:, pd, :], rs_h[:, dt, sl], sem="SL%d" % pd)
                c.wait("vector", "SL%d" % pd, sld)
                c.op("vector", "tensor_mul", wt[:, pd, :], dls[:, pd, :], xis[:, pd, :])
                c.op("vector", "tensor_scalar_mul", ut[:, pd, :], xis[:, pd, :],
                     d_t[:, dt:dt + 1], sem="VE")
                c.wait("scalar", "SL%d" % pd, sld)
                for n in range(N):
                    pn = n % 2
                    idx = dt * N + n
                    c.wait("scalar", "VE", last_tnc[pn])
                    da_se = c.op("scalar", "activation", dat[:, pn, :], dls[:, pd, :],
                                 AF.Exp, scale=a_t[:, idx:idx + 1], sem="SE")
                    c.wait("vector", "BCL", bcl)
                    c.wait("vector", "SE", da_se)
                    c.wait("vector", "TE", last_ymm[n % 4])
                    c.op("vector", "tensor_mul", dbt[:, pn, :], wt[:, pd, :], bcB[:, n, :])
                    c.op("vector", "tensor_tensor_scan", ht[:, pn, :], dat[:, pn, :],
                         dbt[:, pn, :], states[:, idx:idx + 1], ALU.mult, ALU.add)
                    c.op("vector", "tensor_copy", states[:, idx:idx + 1],
                         ht[:, pn, TCS - 1:TCS])
                    last_tnc[pn] = c.op("vector", "tensor_mul", tnc[:, n % 4, :],
                                        ht[:, pn, :], bcC[:, n, :], sem="VE")
                    c.wait("tensor", "VE", last_tnc[pn])
                    if n == 0:
                        c.wait("tensor", "VE", last_gate[pd])
                    for h2 in range(TCS // TCA):
                        c.op("tensor", "matmul", yps[:, pd, h2 * TCA:(h2 + 1) * TCA],
                             eye[:], tnc[:, n % 4, h2 * TCA:(h2 + 1) * TCA],
                             start=(n == 0), stop=(n == N - 1))
                    last_ymm[n % 4] = c.op("tensor", "nop", sem="TE")
                c.wait("vector", "TE", last_ymm[(N - 1) % 4])
                c.wait("vector", "YS%d" % pd, last_ygdma[pd])
                c.op("vector", "scalar_tensor_tensor", g1[:, pd, :], yps[:, pd, :],
                     1.0, ut[:, pd, :], ALU.mult, ALU.add)
                gv = c.op("vector", "tensor_mul", ygt[:, pd, :], g1[:, pd, :],
                          rss[:, pd, :], sem="VE")
                last_gate[pd] = gv
                last_cons[pd] = gv
                c.wait("gpsimd", "VE", gv)
                last_ygdma[pd] = c.dma("gpsimd", yg[:, dt, sl], ygt[:, pd, :],
                                       sem="YS%d" % pd)
            prev_chunk_ymm = c.cnt["TE"]
        c.wait("gpsimd", "YS0")
        c.wait("gpsimd", "YS1")
        c.replay(es.enter_context(nc.Block()))
    return nc



def build_l2(debug=()):
    TT2 = 2048
    NTT = TT2 // 128
    nc = bass.Bass("TRN2", target_bir_lowering=False, debug=False)
    ygb = nc.dram_tensor("ygb", [128, 16, TT2], BF16, kind="ExternalInput").ap()
    xb_t = nc.dram_tensor("xb_t", [TT2, DM], F32, kind="ExternalInput").ap()
    w_out = nc.dram_tensor("w_out", [DM, DI], F32, kind="ExternalInput").ap()
    b_out_r = nc.dram_tensor("b_out_r", [1, DM], F32, kind="ExternalInput").ap()
    kern = nc.dram_tensor("kern", [128, 8, DM], BF16, kind="ExternalInput").ap()
    s_in = nc.dram_tensor("s_in", [1, 1], F32, kind="ExternalInput").ap()
    s_k = nc.dram_tensor("s_k", [1, 1], F32, kind="ExternalInput").ap()
    fbias = nc.dram_tensor("fbias", [1, DM], BF16, kind="ExternalInput").ap()
    out_t = nc.dram_tensor("out_t", [TT2, DM], F32, kind="ExternalOutput").ap()
    dbg_t = {}
    for name, shape, dt in debug:
        dbg_t[name] = nc.dram_tensor("dbg_" + name, shape, dt, kind="ExternalOutput").ap()

    es = ExitStack()
    with es:
        c = Ctx(nc, es)
        sb = lambda n, s, d=F32: es.enter_context(nc.sbuf_tensor(n, s, d))

        ygs = sb("ygs", [128, 16, TT2], BF16)
        woT = sb("woT", [128, 16, DM], BF16)
        kq = sb("kq", [128, 8, DM], BF16)
        bout_bc = sb("bout_bc", [128, DM])
        fb_bc = sb("fb_bc", [128, DM], BF16)
        sinv = sb("sinv", [128, 3])
        ceps = sb("ceps", [128, 1])
        scr = sb("scr", [128, DM])
        scrq = sb("scrq", [128, 4, DM])
        msk = sb("msk", [128, DM], U8)
        qf = sb("qf", [128, DM])
        reck = sb("reck", [128, 1])
        reck2 = sb("reck2", [128, 1])
        ksrc = sb("ksrc", [128, DM])
        es_set = es.enter_context(ExitStack())
        sbs = lambda n, s2, d=F32: es_set.enter_context(nc.sbuf_tensor(n, s2, d))

        c.op("vector", "memset", ceps[:], EPS)
        kbf = sbs("kbf", [128, 8, DM], BF16)
        c.dma("sync", ygs[:], ygb)
        c.dma("sync", bout_bc[:], _ap(b_out_r, 0, (0, 128), (1, DM)))
        c.dma("sync", fb_bc[:], _ap(fbias, 0, (0, 128), (1, DM)))
        c.dma("sync", sinv[:, 0:1], _ap(s_in, 0, (0, 128), (1, 1)))
        c.dma("sync", sinv[:, 1:2], _ap(s_k, 0, (0, 128), (1, 1)))
        c.dma("sync", reck[:], _ap(s_k, 0, (0, 128), (1, 1)))
        c.dma("sync", kbf[:], kern)
        ld_all = c.cnt["LD"]
        stg_f = sbs("stg2_f", [128, 2, DI])
        stg_b = sbs("stg2_b", [128, 2, DI], BF16)
        tr_done = [0, 0]
        for ct in range(0 if "woprep" in skip else 8):
            p = ct % 2
            if tr_done[p]:
                c.wait("sync", "WT%d" % p, tr_done[p])
            ld = c.dma("sync", stg_f[:, p, :], w_out[ct * 128:(ct + 1) * 128, :],
                       sem="WL%d" % p)
            c.wait("vector", "WL%d" % p, ld)
            c.wait("vector", "WT%d" % p, tr_done[p])
            cd = c.op("vector", "tensor_copy", stg_b[:, p, :], stg_f[:, p, :], sem="VE")
            c.wait("sync", "VE", cd)
            tr_done[p] = c.dma("sync", woT[:, :, ct * 128:(ct + 1) * 128],
                               stg_b[:, p, :], transpose=True, sem="WT%d" % p)
        c.wait("vector", "LD", ld_all)
        c.op("vector", "tensor_mul", sinv[:, 1:2], sinv[:, 0:1], sinv[:, 1:2])
        c.op("vector", "reciprocal", sinv[:, 2:3], sinv[:, 0:1], sem="VE")
        c.op("vector", "reciprocal", reck2[:], reck[:], sem="VE")

        def quant_tile(dst_bf, src_f32, width):
            t1, t2, hi = scrq[:, 0, 0:width], scrq[:, 1, 0:width], scrq[:, 2, 0:width]
            sq = scrq[:, 3, 0:width]
            c.op("vector", "tensor_scalar_mul", t1, src_f32, VC1)
            c.op("vector", "tensor_sub", t2, t1, src_f32)
            c.op("vector", "tensor_sub", hi, t1, t2)
            c.op("vector", "tensor_scalar", qf[:, 0:width], src_f32, MAGIC, MAGIC,
                 ALU.add, ALU.subtract)
            c.op("vector", "scalar_tensor_tensor", sq, src_f32, 1.0, src_f32,
                 ALU.mult, ALU.mult)
            c.op("vector", "tensor_single_scalar", msk[:, 0:width], sq,
                 float(2.0 ** -10), ALU.is_ge)
            c.op("vector", "copy_predicated", qf[:, 0:width], msk[:, 0:width], hi)
            c.op("vector", "tensor_scalar", dst_bf, qf[:, 0:width], 448.0, -448.0,
                 ALU.min, ALU.max)

        for kc in range(0 if "kq" in skip else 8):
            c.op("vector", "tensor_copy", scr[:], kbf[:, kc, :])
            c.op("vector", "tensor_scalar_mul", ksrc[:], scr[:], reck2[:, 0:1])
            quant_tile(kq[:, kc, :], ksrc[:], DM)
        c.op("vector", "nop", sem="VE")
        c.wait("sync", "LD", ld_all)
        c.wait("sync", "WT0")
        c.wait("sync", "WT1")
        c.barrier()
        es_set.close()

        xts = sb("xts", [128, 2, DM])
        x2t = sb("x2t", [128, 2, DM])
        x2n = sb("x2n", [128, 2, DM], BF16)
        x2nf = sb("x2nf", [128, DM])
        qb = sb("qb", [128, DM], BF16)
        x2qT = sb("x2qT", [128, 2, 8, 128], BF16)
        ffnb = sb("ffnb", [128, 2, DM], BF16)
        outt = sb("outt", [128, 2, DM])
        stt = sb("stt", [128, 8])
        aps = es.enter_context(nc.psum_tensor("aps", [128, 2, DM], F32))
        fps = es.enter_context(nc.psum_tensor("fps", [128, 2, DM], F32))

        prev = {}
        for tt in range(0 if "l2" in skip else NTT):
            pb = tt % 2
            tsl = slice(tt * 128, (tt + 1) * 128)
            c.wait("sync", "VE", prev.get(("x2", tt - 2), 0))
            ldx = c.dma("sync", xts[:, pb, :],
                        _ap(xb_t, tt * 128 * DM, (DM, 128), (1, DM)), sem="XL%d" % pb)
            c.wait("tensor", "VE", prev.get(("aev", pb), 0))
            for kc in range(16):
                for hh in range(2):
                    c.op("tensor", "matmul", aps[:, pb, hh * 512:(hh + 1) * 512],
                         ygs[:, kc, tsl], woT[:, kc, hh * 512:(hh + 1) * 512],
                         start=(kc == 0), stop=(kc == 15))
            amm = c.op("tensor", "nop", sem="TE")
            c.wait("vector", "TE", amm)
            c.wait("vector", "XL%d" % pb, ldx)
            c.op("vector", "scalar_tensor_tensor", scr[:], aps[:, pb, :], 1.0,
                 bout_bc[:], ALU.mult, ALU.add)
            x2ve = c.op("vector", "tensor_add", x2t[:, pb, :], scr[:], xts[:, pb, :],
                        sem="VE")
            prev[("x2", tt)] = x2ve
            c.wait("scalar", "VE", x2ve)
            c.op("scalar", "activation", scr[:], x2t[:, pb, :], AF.Identity,
                 accum_out=stt[:, 0:1])
            sse = c.op("scalar", "activation", scr[:], x2t[:, pb, :], AF.Square,
                       accum_out=stt[:, 1:2], sem="SE")
            c.wait("vector", "SE", sse)
            c.op("vector", "tensor_scalar_mul", stt[:, 2:3], stt[:, 0:1], -1.0 / DM)
            c.op("vector", "tensor_scalar_mul", stt[:, 3:4], stt[:, 1:2], 1.0 / DM)
            c.op("vector", "tensor_tensor", stt[:, 4:5], stt[:, 2:3], stt[:, 2:3], ALU.mult)
            vve = c.op("vector", "tensor_sub", stt[:, 3:4], stt[:, 3:4], stt[:, 4:5],
                       sem="VE")
            c.wait("scalar", "VE", vve)
            c.op("scalar", "activation", stt[:, 3:4], stt[:, 3:4], AF.Ln, bias=ceps[:, 0:1])
            ivse = c.op("scalar", "activation", stt[:, 5:6], stt[:, 3:4], AF.Exp,
                        scale=-0.5, sem="SE")
            c.wait("vector", "SE", ivse)
            c.op("vector", "tensor_scalar", x2n[:, pb, :], x2t[:, pb, :],
                 stt[:, 2:3], stt[:, 5:6], ALU.add, ALU.mult)
            c.op("vector", "tensor_copy", x2nf[:], x2n[:, pb, :])
            c.op("vector", "tensor_scalar_mul", x2nf[:], x2nf[:], sinv[:, 2:3])
            quant_tile(qb[:], x2nf[:], DM)
            qve = c.op("vector", "nop", sem="VE")
            c.wait("sync", "VE", qve)
            c.wait("sync", "TE", prev.get(("fmm", pb), 0))
            trq = c.dma("sync", x2qT[:, pb, :, :], qb[:], transpose=True, sem="XT%d" % pb)
            c.wait("tensor", "XT%d" % pb, trq)
            c.wait("tensor", "VE", prev.get(("fev", pb), 0))
            for kc in range(8):
                for hh in range(2):
                    c.op("tensor", "matmul", fps[:, pb, hh * 512:(hh + 1) * 512],
                         x2qT[:, pb, kc, :], kq[:, kc, hh * 512:(hh + 1) * 512],
                         start=(kc == 0), stop=(kc == 7))
            fmm = c.op("tensor", "nop", sem="TE")
            prev[("fmm", pb)] = fmm
            c.wait("vector", "TE", fmm)
            c.op("vector", "scalar_tensor_tensor", ffnb[:, pb, :], fps[:, pb, :],
                 sinv[:, 1:2], fb_bc[:], ALU.mult, ALU.add)
            c.wait("vector", "OS%d" % pb, prev.get(("odma", tt - 2), 0))
            fev = c.op("vector", "tensor_add", outt[:, pb, :], x2t[:, pb, :],
                       ffnb[:, pb, :], sem="VE")
            prev[("fev", pb)] = fev
            prev[("aev", pb)] = fev
            c.wait("gpsimd", "VE", fev)
            c.wait("gpsimd", "OS%d" % pb, prev.get(("odma", tt - 2), 0))
            prev[("odma", tt)] = c.dma("gpsimd", out_t[tsl, :], outt[:, pb, :],
                                       sem="OS%d" % pb)
        c.wait("gpsimd", "OS0")
        c.wait("gpsimd", "OS1")
        c.replay(es.enter_context(nc.Block()))
    return nc



def _l1_inmaps(inputs):
    x = np.ascontiguousarray(inputs["L0_attn_x"])
    W_in = np.ascontiguousarray(inputs["L0_attn_W_in"])
    b_in = np.ascontiguousarray(inputs["L0_attn_b_in"])
    W_conv = np.ascontiguousarray(inputs["L0_attn_W_conv"])
    b_conv = np.ascontiguousarray(inputs["L0_attn_b_conv"])
    W_x = np.ascontiguousarray(inputs["L0_attn_W_x"])
    b_x = np.ascontiguousarray(inputs["L0_attn_b_x"])
    A_log = np.ascontiguousarray(inputs["L0_attn_A_log"])
    Dv = np.ascontiguousarray(inputs["L0_attn_D"])
    eye = np.eye(128, dtype=ml_dtypes.bfloat16)
    maps = []
    for cc in range(8):
        b, dh = cc // 2, cc % 2
        own = np.arange(1024 * dh, 1024 * dh + 1024)
        oth = np.arange(1024 * (1 - dh), 1024 * (1 - dh) + 1024)
        perm = np.concatenate([own, oth])
        w_in_s = np.concatenate([W_in[perm], W_in[DI + own]], axis=0)
        b_in_cat = np.concatenate([b_in[perm], b_in[DI + own]])
        b_in_s = np.ascontiguousarray(b_in_cat.reshape(24, 128).T)
        wc = W_conv[perm, 0, :]
        w_conv_s = np.ascontiguousarray(
            wc.reshape(16, 128, K).transpose(1, 0, 2).reshape(128, 16 * K))
        b_conv_s = np.ascontiguousarray(b_conv[perm].reshape(16, 128).T)
        wx_rows = np.concatenate([W_x[own], W_x[DI:DI + 32],
                                  np.zeros((96, DI), W_x.dtype)], axis=0)
        w_x_s = np.ascontiguousarray(wx_rows[:, perm])
        bx9 = np.zeros((128, 9), np.float32)
        bx9[:, 0:8] = b_x[own].reshape(8, 128).T
        bx9[0:32, 8] = b_x[DI:DI + 32]
        a_log_s = np.ascontiguousarray(
            A_log[own].reshape(8, 128, N).transpose(1, 0, 2).reshape(128, 8 * N))
        d_s = np.ascontiguousarray(Dv[own].reshape(8, 128).T)
        maps.append(dict(xb=x[b], w_in_s=w_in_s, b_in_s=b_in_s, w_conv_s=w_conv_s,
                         b_conv_s=b_conv_s, w_x_s=w_x_s, b_x_s=bx9,
                         a_log_s=a_log_s, d_s=d_s, eye_in=eye))
    return maps


def _l2_inmaps(inputs, yg_l1):
    x = np.ascontiguousarray(inputs["L0_attn_x"])
    W_out = np.ascontiguousarray(inputs["L0_attn_W_out"])
    b_out = np.ascontiguousarray(inputs["L0_attn_b_out"]).reshape(1, DM)
    kern = np.ascontiguousarray(inputs["L0_ffn_kernel"])
    s_in = np.ascontiguousarray(inputs["L0_ffn_input_scale"]).reshape(1, 1)
    s_k = np.ascontiguousarray(inputs["L0_ffn_kernel_scale"]).reshape(1, 1)
    fb = np.ascontiguousarray(inputs["L0_ffn_bias"]).reshape(1, DM)
    kern_r = np.ascontiguousarray(
        kern.reshape(8, 128, DM).transpose(1, 0, 2))
    maps = []
    for cc in range(8):
        b, th = cc // 2, cc % 2
        ygb = np.empty((128, 16, 2048), dtype=ml_dtypes.bfloat16)
        for g in range(16):
            src = yg_l1[2 * b + g // 8]
            ygb[:, g, :] = src[:, g % 8, th * 2048:(th + 1) * 2048]
        maps.append(dict(ygb=np.ascontiguousarray(ygb),
                         xb_t=np.ascontiguousarray(x[b, th * 2048:(th + 1) * 2048]),
                         w_out=W_out, b_out_r=b_out, kern=kern_r,
                         s_in=s_in, s_k=s_k, fbias=fb))
    return maps





def build_fused(debug=(), no_cc=False, no_branch=False, skip=()):
    from concourse.aot_env import allow_ici_all_to_all
    TT2 = 2048
    NTT = TT2 // 128
    nc = bass.Bass("TRN2", target_bir_lowering=False, debug=False)
    xb = nc.dram_tensor("xb", [T, DM], F32, kind="ExternalInput").ap()
    w_in_s = nc.dram_tensor("w_in_s", [24 * 128, DM], F32, kind="ExternalInput").ap()
    b_in_s = nc.dram_tensor("b_in_s", [128, 24], F32, kind="ExternalInput").ap()
    w_conv_s = nc.dram_tensor("w_conv_s", [128, 16 * K], F32, kind="ExternalInput").ap()
    b_conv_s = nc.dram_tensor("b_conv_s", [128, 16], F32, kind="ExternalInput").ap()
    w_x_s = nc.dram_tensor("w_x_s", [9 * 128, DI], F32, kind="ExternalInput").ap()
    b_x_s = nc.dram_tensor("b_x_s", [128, 9], F32, kind="ExternalInput").ap()
    a_log_s = nc.dram_tensor("a_log_s", [128, NDT * N], F32, kind="ExternalInput").ap()
    d_s = nc.dram_tensor("d_s", [128, NDT], F32, kind="ExternalInput").ap()
    eye_in = nc.dram_tensor("eye_in", [128, 128], BF16, kind="ExternalInput").ap()
    xb_t = nc.dram_tensor("xb_t", [TT2, DM], F32, kind="ExternalInput").ap()
    w_out = nc.dram_tensor("w_out", [DM, DI], F32, kind="ExternalInput").ap()
    b_out_r = nc.dram_tensor("b_out_r", [1, DM], F32, kind="ExternalInput").ap()
    kern = nc.dram_tensor("kern", [128, 8, DM], BF16, kind="ExternalInput").ap()
    s_in = nc.dram_tensor("s_in", [1, 1], F32, kind="ExternalInput").ap()
    s_k = nc.dram_tensor("s_k", [1, 1], F32, kind="ExternalInput").ap()
    fbias = nc.dram_tensor("fbias", [1, DM], BF16, kind="ExternalInput").ap()
    out_t = nc.dram_tensor("out_t", [TT2, DM], F32, kind="ExternalOutput").ap()
    xi_h = nc.dram_tensor("xi_h", [128, NDT, T], BF16).ap()
    dl_h = nc.dram_tensor("dl_h", [128, NDT, T], F32).ap()
    rs_h = nc.dram_tensor("rs_h", [128, NDT, T], F32).ap()
    bc_h = nc.dram_tensor("bc_h", [32, T], BF16).ap()
    w_in_b = nc.dram_tensor("w_in_b", [24 * 128, DM], BF16)
    w_x_b = nc.dram_tensor("w_x_b", [9 * 128, DI], BF16)
    yg = nc.dram_tensor("yg", [128, NDT, T], BF16).ap()
    agin = nc.dram_tensor("agin", [128, NDT, TT2], BF16)
    agout = nc.dram_tensor("agout", [8, 128, NDT, TT2], BF16)
    selc = nc.dram_tensor("selc", [1, 4], mybir.dt.uint32, kind="ExternalInput").ap()
    dbg_t = {}
    for name, shape, dt in debug:
        dbg_t[name] = nc.dram_tensor("dbg_" + name, shape, dt, kind="ExternalOutput").ap()

    es = ExitStack()
    with es:
        c = Ctx(nc, es)
        sb = lambda n, s, d=F32: es.enter_context(nc.sbuf_tensor(n, s, d))
        eye = sb("eye", [128, 128], BF16)
        b_in_t = sb("b_in_t", [128, 24])
        b_conv_t = sb("b_conv_t", [128, 16])
        b_x_t = sb("b_x_t", [128, 9])
        wconv_t = sb("wconv_t", [128, 16 * K])
        a_t = sb("a_t", [128, NDT * N])
        d_t = sb("d_t", [128, NDT])
        ceps = sb("ceps", [128, 1])
        states = sb("states", [128, NDT * N])
        selc_sb = sb("selc_sb", [1, 4], mybir.dt.uint32)
        es_ab = es.enter_context(ExitStack())
        sba = lambda n, s, d=F32: es_ab.enter_context(nc.sbuf_tensor(n, s, d))
        diag = sba("diag", [128, 16 * K, 128], BF16)

        c.dma("sync", selc_sb[:], selc)
        if "l1eng" not in skip:
            c.op("vector", "memset", ceps[:], EPS)
            c.op("vector", "memset", states[:], 0.0)
        if "l1dma" not in skip:
            for dst, src in ((b_in_t, b_in_s), (b_conv_t, b_conv_s), (b_x_t, b_x_s),
                             (wconv_t, w_conv_s), (a_t, a_log_s), (d_t, d_s), (eye, eye_in)):
                c.dma("sync", dst[:], src)
        ld0 = c.cnt["LD"]
        if "l1eng" not in skip:
            c.wait("scalar", "LD", ld0)
            c.op("scalar", "activation", a_t[:], a_t[:], AF.Exp, sem="SE")
            c.wait("vector", "SE")
            c.op("vector", "tensor_scalar_mul", a_t[:], a_t[:], -1.0)
            c.wait("vector", "LD", ld0)
            for i in range(16 * K):
                c.op("vector", "tensor_scalar_mul", diag[:, i, :], eye[:], wconv_t[:, i:i + 1])
            c.op("vector", "nop", sem="VE")

        xtok = sba("xtok", [128, 1, 4, DM])
        scr2 = sba("scr2", [128, TCA])
        ssum = sba("ssum", [128, 2, 4])
        ssq = sba("ssq", [128, 2, 4])
        stat = sba("stat", [128, 2, 12])
        vtmp = sba("vtmp", [128, 4])
        xntok = sba("xntok", [128, 2, DM], BF16)
        xnT = sba("xnT", [128, 1, 8, TCA], BF16)
        xipre = sba("xipre", [128, 16, 3 + TCA], BF16)
        xichk = sba("xichk", [128, 1, 16, TCA], BF16)
        reschk = sba("reschk", [128, 1, TCA])
        dlchk = sba("dlchk", [128, 1, TCA])
        bcchk = sba("bcchk", [32, 1, TCA], BF16)
        winTs = sba("winTs", [128, 8, 2, 128], BF16)
        wxTs = sba("wxTs", [128, 16, 2, 128], BF16)
        bcB = sba("bcB", [128, N, TCS], BF16)
        bcC = sba("bcC", [128, N, TCS], BF16)
        dls = sba("dls", [128, 2, TCS])
        xis = sba("xis", [128, 2, TCS], BF16)
        rss = sba("rss", [128, 8, TCS])
        wt = sba("wt", [128, 2, TCS], BF16)
        ut = sba("ut", [128, 8, TCS], BF16)
        dat = sba("dat", [128, 2, TCS])
        dbt = sba("dbt", [128, 2, TCS], BF16)
        ht = sba("ht", [128, 2, TCS], BF16)
        tnc = sba("tnc", [128, 2, N, TCS], BF16)
        g1 = sba("g1", [128, 2, TCS])
        ygt = sba("ygt", [128, 2, TCS], BF16)
        sps = es_ab.enter_context(nc.psum_tensor("sps", [128, 2, TCA], F32))
        cps = es_ab.enter_context(nc.psum_tensor("cps", [128, 2, TCA], F32))
        pps = es_ab.enter_context(nc.psum_tensor("pps", [128, 2, TCA], F32))
        yps = es_ab.enter_context(nc.psum_tensor("yps", [128, 2, TCS], F32))

        datf = bass.AP(dat, 0, [[2 * TCS, 128], [1, 2 * TCS]])
        c.op("vector", "memset", xipre[:, :, 0:3], 0.0)
        c.op("vector", "nop", sem="VE")
        setup_ve = c.cnt["VE"]
        c.wait("scalar", "VE", setup_ve)

        ldx0 = 0
        if "ab" not in skip:
            ldx0 = c.dma("gpsimd", xtok[:, 0, :, :],
                         _ap(xb, 0, (DM, 128), (128 * DM, 4), (1, DM)), sem="XL")
        cast_in = [None] * 24
        cast_x = [None] * 9
        if "wprep" not in skip:
            cast_prev = [0, 0]
            tiles = [("in", ct) for ct in range(24)] + [("x", ct) for ct in range(9)]
            for i, (fam, ct) in enumerate(tiles):
                p = i % 2
                if cast_prev[p]:
                    c.wait("gpsimd", "WC%d" % p, cast_prev[p])
                src, dst = ((w_in_s, w_in_b.ap()) if fam == "in"
                            else (w_x_s, w_x_b.ap()))
                cast_prev[p] = c.dma("gpsimd", dst[ct * 128:(ct + 1) * 128, :],
                                     src[ct * 128:(ct + 1) * 128, :], sem="WC%d" % p)
                if fam == "in":
                    cast_in[ct] = ("WC%d" % p, cast_prev[p])
                else:
                    cast_x[ct] = ("WC%d" % p, cast_prev[p])
        for _sm in ("WI0", "WI1", "WX0", "WX1", "WC0", "WC1", "RS0",
                    "DS0", "BS", "XS", "YS0", "YS1", "SL0", "SL1",
                    "BCL", "XT"):
            c.sem(_sm)

        NCH = 0 if "ab" in skip else T // TCA
        do_scan = ("scan" not in skip) and NCH > 0
        xn_ve = [0]
        xr_mm = [0, 0]
        ev_ve = {}
        ev_se = {}
        xi_st = [0, 0]
        dl_st = [0]
        rs_st = [0]
        bc_st = [0]
        xnk_tr = [0, 0]
        dl_rec = {}
        wi_mm = [0, 0]
        wx_mm = [0, 0]
        wi_tr = [0, 0]
        wx_tr = [0, 0]
        pj_se = {}
        pj_guard = {"se": 0, "ve": 0}
        st_rec = {}
        sl_ld = [0, 0]
        wtu_done = [0, 0]
        dat_done = [0, 0]
        gate_done = [0] * NDT
        gate_slot = [0, 0]
        ymm_rec = [0, 0]
        ym_of = {}
        last_tts = [0, 0]
        bc_used = [0]
        bcl_v = [0]
        last_ygdma = [0, 0]
        core_done = {}

        def ab_head(tc):
            pb = tc % 2
            if tc == 0:
                ldx = ldx0
            else:
                c.wait("sync", "VE", xn_ve[0])
                ldx = c.dma("sync", xtok[:, 0, :, :],
                            _ap(xb, tc * TCA * DM, (DM, 128), (128 * DM, 4), (1, DM)),
                            sem="XL")
            c.wait("scalar", "XL", ldx)
            c.wait("scalar", "VE", last_tts[0])
            c.wait("scalar", "VE", last_tts[1])
            for j in range(4):
                c.op("scalar", "activation", datf, xtok[:, 0, j, :], AF.Identity,
                     accum_out=ssum[:, pb, j:j + 1])
                c.op("scalar", "activation", datf, xtok[:, 0, j, :], AF.Square,
                     accum_out=ssq[:, pb, j:j + 1])
            st_se = c.op("scalar", "nop", sem="SE")
            c.wait("vector", "SE", st_se)
            c.op("vector", "tensor_scalar_mul", stat[:, pb, 0:4], ssum[:, pb, :],
                 -1.0 / DM)
            c.op("vector", "tensor_scalar_mul", stat[:, pb, 4:8], ssq[:, pb, :],
                 1.0 / DM)
            c.op("vector", "tensor_tensor", vtmp[:], stat[:, pb, 0:4],
                 stat[:, pb, 0:4], ALU.mult)
            vw = c.op("vector", "tensor_sub", stat[:, pb, 4:8], stat[:, pb, 4:8],
                      vtmp[:], sem="VE")
            c.wait("scalar", "VE", vw)
            c.op("scalar", "activation", stat[:, pb, 4:8], stat[:, pb, 4:8],
                 AF.Ln, bias=ceps[:, 0:1])
            iv_se = c.op("scalar", "activation", stat[:, pb, 8:12], stat[:, pb, 4:8],
                         AF.Exp, scale=-0.5, sem="SE")
            c.wait("vector", "SE", iv_se)
            c.wait("vector", "XL", ldx)
            c.wait("sync", "TE", xr_mm[(tc - 1) % 2])
            for j in range(4):
                sj = j % 2
                c.wait("vector", "XT", xnk_tr[sj])
                xv = c.op("vector", "tensor_scalar", xntok[:, sj, :], xtok[:, 0, j, :],
                          stat[:, pb, j:j + 1], stat[:, pb, 8 + j:9 + j],
                          ALU.add, ALU.mult, sem="VE")
                c.wait("sync", "VE", xv)
                xnk_tr[sj] = c.dma("sync", xnT[:, 0, :, j * 128:(j + 1) * 128],
                                   xntok[:, sj, :], transpose=True, sem="XT")
            xn_ve[0] = xv
            return xnk_tr[1]

        def scan_bc(j):
            rec = st_rec[j]
            c.wait("sync", "VE", bc_used[0])
            c.wait("sync", "BS", rec["bc"])
            for n in range(N):
                c.dma("sync", bcB[:, n, :],
                      _ap(bc_h, n * T + j * TCS, (0, 128), (1, TCS)), sem="BCL")
                bcl_v[0] = c.dma("sync", bcC[:, n, :],
                                 _ap(bc_h, (16 + n) * T + j * TCS, (0, 128), (1, TCS)),
                                 sem="BCL")

        def scan_loads(j, dt):
            pd = dt % 2
            sl = slice(j * TCS, (j + 1) * TCS)
            rec = st_rec[j]
            c.wait("sync", "VE", wtu_done[pd])
            c.wait("sync", "SE", dat_done[pd])
            c.wait("sync", "VE", gate_done[dt])
            c.wait("sync", "DS0", rec["dl"])
            c.wait("sync", "XS", rec["xi"])
            c.wait("sync", "RS0", rec["rs"])
            c.dma("sync", dls[:, pd, :], dl_h[:, dt, sl], sem="SL%d" % pd)
            c.dma("sync", xis[:, pd, :], xi_h[:, dt, sl], sem="SL%d" % pd)
            sl_ld[pd] = c.dma("sync", rss[:, dt, :], rs_h[:, dt, sl], sem="SL%d" % pd)

        def scan_core(j, dt):
            pd = dt % 2
            if dt == 0:
                scan_bc(j)
                scan_loads(j, 0)
                scan_loads(j, 1)
            c.wait("vector", "TE", ymm_rec[pd])
            c.wait("vector", "SL%d" % pd, sl_ld[pd])
            c.op("vector", "tensor_mul", wt[:, pd, :], dls[:, pd, :], xis[:, pd, :])
            wtu_done[pd] = c.op("vector", "tensor_scalar_mul", ut[:, dt, :],
                                xis[:, pd, :], d_t[:, dt:dt + 1], sem="VE")
            c.wait("scalar", "SL%d" % pd, sl_ld[pd])
            cd = 0
            da_se = 0
            for n in range(N):
                pn = n % 2
                idx = dt * N + n
                c.wait("scalar", "VE", last_tts[pn])
                da_se = c.op("scalar", "activation", dat[:, pn, :], dls[:, pd, :],
                             AF.Exp, scale=a_t[:, idx:idx + 1], sem="SE")
                c.wait("vector", "BCL", bcl_v[0])
                c.wait("vector", "SE", da_se)
                c.op("vector", "tensor_mul", dbt[:, pn, :], wt[:, pd, :], bcB[:, n, :])
                last_tts[pn] = c.op("vector", "tensor_tensor_scan", ht[:, pn, :],
                                    dat[:, pn, :], dbt[:, pn, :],
                                    states[:, idx:idx + 1], ALU.mult, ALU.add,
                                    sem="VE")
                c.op("vector", "tensor_copy", states[:, idx:idx + 1],
                     ht[:, pn, TCS - 1:TCS])
                cd = c.op("vector", "tensor_mul", tnc[:, pd, n, :],
                          ht[:, pn, :], bcC[:, n, :], sem="VE")
            dat_done[pd] = da_se
            core_done[(j, dt)] = cd
            if dt == NDT - 1:
                bc_used[0] = cd
            if 0 < dt < NDT - 2 + 1 and dt + 2 < NDT:
                scan_loads(j, dt + 2)
            elif dt == 0:
                scan_loads(j, 2)

        def scan_tail_t(j, dt):
            pd = dt % 2
            c.wait("tensor", "VE", core_done[(j, dt)])
            c.wait("tensor", "VE", gate_slot[pd])
            for n in range(N):
                c.op("tensor", "matmul", yps[:, pd, :], eye[:], tnc[:, pd, n, :],
                     start=(n == 0), stop=(n == N - 1))
            ym = c.op("tensor", "nop", sem="TE")
            ymm_rec[pd] = ym
            ym_of[(j, dt)] = ym

        def scan_tail_v(j, dt):
            pd = dt % 2
            sl = slice(j * TCS, (j + 1) * TCS)
            c.wait("vector", "TE", ym_of[(j, dt)])
            c.wait("vector", "YS%d" % pd, last_ygdma[pd])
            c.op("vector", "scalar_tensor_tensor", g1[:, pd, :], yps[:, pd, :],
                 1.0, ut[:, dt, :], ALU.mult, ALU.add)
            gv = c.op("vector", "tensor_mul", ygt[:, pd, :], g1[:, pd, :],
                      rss[:, dt, :], sem="VE")
            gate_done[dt] = gv
            gate_slot[pd] = gv
            c.wait("gpsimd", "VE", gv)
            last_ygdma[pd] = c.dma("gpsimd", yg[:, dt, sl], ygt[:, pd, :],
                                   sem="YS%d" % pd)

        slot_t = [(0, 0), (0, 1), (1, 0), (1, 1)]

        def ab_mid(tc, trx):
            pb = tc % 2
            c.wait("tensor", "XT", trx)
            for ct in range(24):
                pp = ct % 2
                c.wait("sync", "TE", wi_mm[pp])
                if tc == 0 and cast_in[ct] is not None:
                    c.wait("sync", cast_in[ct][0], cast_in[ct][1])
                wi_tr[pp] = c.dma("sync", winTs[:, :, pp, :],
                                  w_in_b.ap()[ct * 128:(ct + 1) * 128, :],
                                  transpose=True, sem="WI%d" % pp)
                c.wait("tensor", "WI%d" % pp, wi_tr[pp])
                c.wait("tensor", "VE", ev_ve.get(("xr", pp), 0))
                c.wait("tensor", "SE", ev_se.get(("xr", pp), 0))
                for kc in range(8):
                    c.op("tensor", "matmul", sps[:, pp, :], winTs[:, kc, pp, :],
                         xnT[:, 0, kc, :], start=(kc == 0), stop=(kc == 7))
                mm = c.op("tensor", "nop", sem="TE")
                wi_mm[pp] = mm
                if ct < 16:
                    c.wait("vector", "TE", mm)
                    ev_ve[("xr", pp)] = c.op(
                        "vector", "tensor_scalar_add", xipre[:, ct, 3:3 + TCA],
                        sps[:, pp, :], b_in_t[:, ct:ct + 1], sem="VE")
                else:
                    ct8 = ct - 16
                    c.wait("scalar", "TE", mm)
                    c.wait("scalar", "RS0", rs_st[0])
                    ev_se[("xr", pp)] = c.op(
                        "scalar", "activation", reschk[:, 0, :], sps[:, pp, :],
                        AF.Silu, bias=b_in_t[:, ct:ct + 1], sem="SE")
                    c.wait("gpsimd", "SE", ev_se[("xr", pp)])
                    rs_st[0] = c.dma("gpsimd",
                                     rs_h[:, ct8, tc * TCA:(tc + 1) * TCA],
                                     reschk[:, 0, :], sem="RS0")
            xr_mm[pb] = c.cnt["TE"]
            c.wait("tensor", "VE", ev_ve[("xr", 0)])
            c.wait("tensor", "VE", ev_ve[("xr", 1)])
            c.wait("scalar", "XS", xi_st[pb])
            c.wait("tensor", "VE", setup_ve)
            c.wait("tensor", "SE", pj_guard["se"])
            c.wait("tensor", "VE", pj_guard["ve"])
            for dt in range(16):
                pp = dt % 2
                c.wait("tensor", "SE", ev_se.get(("cv", pp), 0))
                for k in range(K):
                    c.op("tensor", "matmul", cps[:, pp, :], diag[:, dt * K + k, :],
                         xipre[:, dt, k:k + TCA], start=(k == 0), stop=(k == K - 1))
                mm = c.op("tensor", "nop", sem="TE")
                c.wait("scalar", "TE", mm)
                ev_se[("cv", pp)] = c.op(
                    "scalar", "activation", xichk[:, 0, dt, :], cps[:, pp, :],
                    AF.Silu, bias=b_conv_t[:, dt:dt + 1], sem="SE")
            conv_mm = c.cnt["TE"]
            c.wait("gpsimd", "SE", ev_se[("cv", 0)])
            c.wait("gpsimd", "SE", ev_se[("cv", 1)])
            xi_st[pb] = c.dma("gpsimd", xi_h[:, :, tc * TCA:(tc + 1) * TCA],
                              xichk[:, 0, 0:NDT, :], sem="XS")
            c.wait("tensor", "SE", ev_se[("cv", 0)])
            c.wait("tensor", "SE", ev_se[("cv", 1)])
            pj_mm = {}
            for ct in range(9):
                pw = ct % 2
                sN = ct % 4
                use_pps, si = slot_t[sN]
                tns = pps if use_pps else cps
                c.wait("sync", "TE", wx_mm[pw])
                if tc == 0 and cast_x[ct] is not None:
                    c.wait("sync", cast_x[ct][0], cast_x[ct][1])
                wx_tr[pw] = c.dma("sync", wxTs[:, :, pw, :],
                                  w_x_b.ap()[ct * 128:(ct + 1) * 128, :],
                                  transpose=True, sem="WX%d" % pw)
                c.wait("tensor", "WX%d" % pw, wx_tr[pw])
                c.wait("tensor", "SE", pj_se.get(sN, 0))
                if sN == 0:
                    c.wait("tensor", "VE", pj_guard["ve"])
                for kc in range(16):
                    c.op("tensor", "matmul", tns[:, si, :],
                         wxTs[:, kc, pw, :], xichk[:, 0, kc, :],
                         start=(kc == 0), stop=(kc == 15))
                mm = c.op("tensor", "nop", sem="TE")
                wx_mm[pw] = mm
                pj_mm[ct] = (mm, tns, si)
                if ct < 8:
                    c.wait("scalar", "TE", mm)
                    c.wait("scalar", "DS0", dl_st[0])
                    c.op("scalar", "activation", scr2[:], tns[:, si, :],
                         AF.Exp, bias=b_x_t[:, ct:ct + 1])
                    pj_se[sN] = c.op(
                        "scalar", "activation", dlchk[:, 0, :], scr2[:],
                        AF.Ln, bias=1.0, sem="SE")
                    c.wait("gpsimd", "SE", pj_se[sN])
                    dl_st[0] = c.dma("gpsimd",
                                     dl_h[:, ct, tc * TCA:(tc + 1) * TCA],
                                     dlchk[:, 0, :], sem="DS0")
            pj_guard["se"] = max(pj_se.get(0, 0), pj_se.get(1, 0))
            st_rec[tc] = {"xi": xi_st[pb], "dl": dl_st[0], "rs": rs_st[0], "bc": 0}
            return conv_mm, pj_mm[8][0]

        def ab_bc_halo(tc, conv_mm, bc_mm):
            c.wait("vector", "TE", bc_mm)
            c.wait("vector", "BS", bc_st[0])
            bv = c.op("vector", "tensor_scalar_add", bcchk[:, 0, :],
                      cps[0:32, 0, :], b_x_t[0:32, 8:9], sem="VE")
            pj_guard["ve"] = bv
            c.wait("gpsimd", "VE", bv)
            bc_st[0] = c.dma("gpsimd", bc_h[:, tc * TCA:(tc + 1) * TCA],
                             bcchk[:, 0, :], sem="BS")
            st_rec[tc]["bc"] = bc_st[0]
            c.wait("vector", "TE", conv_mm)
            c.op("vector", "tensor_copy", xipre[:, :, 0:3],
                 xipre[:, :, TCA:TCA + 3], sem="VE")

        for k in range(NCH + (1 if do_scan else 0)):
            j = k - 1
            em_ab = k < NCH
            em_sc = do_scan and 0 <= j < NCH
            if em_ab:
                trx = ab_head(k)
            if em_sc:
                scan_core(j, 0)
                scan_core(j, 1)
            if em_ab:
                conv_mm, bc_mm = ab_mid(k, trx)
            if em_sc:
                scan_tail_t(j, 0)
                scan_core(j, 2)
                scan_tail_t(j, 1)
                scan_tail_v(j, 0)
                if em_ab:
                    ab_bc_halo(k, conv_mm, bc_mm)
                for dt in range(3, NDT):
                    scan_core(j, dt)
                    scan_tail_t(j, dt - 1)
                    scan_tail_v(j, dt - 2)
                scan_tail_t(j, NDT - 1)
                scan_tail_v(j, NDT - 2)
                scan_tail_v(j, NDT - 1)
            elif em_ab:
                ab_bc_halo(k, conv_mm, bc_mm)

        for sm in ("RS0", "DS0", "BS", "XS"):
            c.wait("gpsimd", sm)
        c.wait("sync", "LD")
        c.wait("gpsimd", "YS0")
        c.wait("gpsimd", "YS1")
        c.sem("AGC")
        c.cnt["AGC"] += 16
        agc_v = c.cnt["AGC"]

        def _agin_copy(g, _selt=None):
            if no_branch:
                g.dma_start(out=agin.ap(), in_=yg[:, :, TT2:2 * TT2]).then_inc(
                    c.sems["AGC"], 16)
                return
            with g.register() as r:
                g.load(r, selc_sb[0:1, 0:1])
                with g.If_eq(r, 0):
                    g.dma_start(out=agin.ap(), in_=yg[:, :, TT2:2 * TT2]).then_inc(
                        c.sems["AGC"], 16)
                with g.Else():
                    g.dma_start(out=agin.ap(), in_=yg[:, :, 0:TT2]).then_inc(
                        c.sems["AGC"], 16)
        c.em("gpsimd", _agin_copy)
        c.wait("gpsimd", "AGC", agc_v)
        if no_cc:
            c.sem("CC")
            c.cnt["CC"] += 1
            c.em("gpsimd", lambda e: e.sem_inc(c.sems["CC"], 1))
        else:
            c.sem("CC")
            c.cnt["CC"] += 1
            c.em("gpsimd", lambda e: e.collective_compute(
                "AllGather", ALU.bypass,
                replica_groups=[list(range(8))],
                ins=[agin.ap().opt()], outs=[agout.ap().opt()]).then_inc(c.sems["CC"], 1))
        c.barrier()
        es_ab.close()

        ygs = sb("ygs", [128, 16, TT2], BF16)
        woT = sb("woT", [128, 16, DM], BF16)
        kq = sb("kq", [128, 8, DM], BF16)
        bout_bc = sb("bout_bc", [128, DM])
        fb_bc = sb("fb_bc", [128, DM], BF16)
        sinv = sb("sinv", [128, 3])
        scr = sb("scr", [128, DM])
        scrq = sb("scrq", [128, 4, DM])
        msk = sb("msk", [128, DM], U8)
        qf = sb("qf", [128, DM])
        reck = sb("reck", [128, 1])
        reck2 = sb("reck2", [128, 1])
        ksrc = sb("ksrc", [128, DM])
        es_set = es.enter_context(ExitStack())
        sbs = lambda n, s2, d=F32: es_set.enter_context(nc.sbuf_tensor(n, s2, d))

        kbf = sbs("kbf", [128, 8, DM], BF16)
        c.sem("L2S")
        if "l2dma" not in skip:
            c.dma("sync", bout_bc[:], _ap(b_out_r, 0, (0, 128), (1, DM)), sem="L2S")
            c.dma("sync", fb_bc[:], _ap(fbias, 0, (0, 128), (1, DM)), sem="L2S")
            c.dma("sync", sinv[:, 0:1], _ap(s_in, 0, (0, 128), (1, 1)), sem="L2S")
            c.dma("sync", sinv[:, 1:2], _ap(s_k, 0, (0, 128), (1, 1)), sem="L2S")
            c.dma("sync", reck[:], _ap(s_k, 0, (0, 128), (1, 1)), sem="L2S")
            c.dma("sync", kbf[:], kern, sem="L2S")
        ld_all = c.cnt["L2S"]
        stg_f = sbs("stg2_f", [128, 2, DI])
        stg_b = sbs("stg2_b", [128, 2, DI], BF16)
        tr_done = [0, 0]
        for ct in range(0 if "woprep" in skip else 8):
            p = ct % 2
            if tr_done[p]:
                c.wait("sync", "WT%d" % p, tr_done[p])
            ld = c.dma("sync", stg_f[:, p, :], w_out[ct * 128:(ct + 1) * 128, :],
                       sem="WL%d" % p)
            c.wait("vector", "WL%d" % p, ld)
            c.wait("vector", "WT%d" % p, tr_done[p])
            cd = c.op("vector", "tensor_copy", stg_b[:, p, :], stg_f[:, p, :], sem="VE")
            c.wait("sync", "VE", cd)
            tr_done[p] = c.dma("sync", woT[:, :, ct * 128:(ct + 1) * 128],
                               stg_b[:, p, :], transpose=True, sem="WT%d" % p)
        if "l2eng" not in skip:
            c.wait("vector", "L2S", ld_all)
            c.op("vector", "tensor_mul", sinv[:, 1:2], sinv[:, 0:1], sinv[:, 1:2])
            c.op("vector", "reciprocal", sinv[:, 2:3], sinv[:, 0:1], sem="VE")
            c.op("vector", "reciprocal", reck2[:], reck[:], sem="VE")

        def quant_tile(dst_bf, src_f32, width):
            t1, t2, hi = scrq[:, 0, 0:width], scrq[:, 1, 0:width], scrq[:, 2, 0:width]
            sq = scrq[:, 3, 0:width]
            c.op("vector", "tensor_scalar_mul", t1, src_f32, VC1)
            c.op("vector", "tensor_sub", t2, t1, src_f32)
            c.op("vector", "tensor_sub", hi, t1, t2)
            c.op("vector", "tensor_scalar", qf[:, 0:width], src_f32, MAGIC, MAGIC,
                 ALU.add, ALU.subtract)
            c.op("vector", "scalar_tensor_tensor", sq, src_f32, 1.0, src_f32,
                 ALU.mult, ALU.mult)
            c.op("vector", "tensor_single_scalar", msk[:, 0:width], sq,
                 float(2.0 ** -10), ALU.is_ge)
            c.op("vector", "copy_predicated", qf[:, 0:width], msk[:, 0:width], hi)
            c.op("vector", "tensor_scalar", dst_bf, qf[:, 0:width], 448.0, -448.0,
                 ALU.min, ALU.max)

        for kc in range(0 if "kq" in skip else 8):
            c.op("vector", "tensor_copy", scr[:], kbf[:, kc, :])
            c.op("vector", "tensor_scalar_mul", ksrc[:], scr[:], reck2[:, 0:1])
            quant_tile(kq[:, kc, :], ksrc[:], DM)
        c.op("vector", "nop", sem="VE")
        c.sem("YGL")
        c.cnt["YGL"] += 32
        ygl = c.cnt["YGL"]

        def _ygs_loads(g):
            if no_branch:
                g.dma_start(out=ygs[:, 0:8, :], in_=yg[:, :, 0:TT2]).then_inc(c.sems["YGL"], 16)
                g.dma_start(out=ygs[:, 8:16, :], in_=agout.ap()[0]).then_inc(c.sems["YGL"], 16)
                return
            with g.register() as r:
                g.load(r, selc_sb[0:1, 0:1])
                with g.If_eq(r, 0):
                    g.dma_start(out=ygs[:, 0:8, :],
                                in_=yg[:, :, 0:TT2]).then_inc(c.sems["YGL"], 16)
                with g.Else():
                    g.dma_start(out=ygs[:, 8:16, :],
                                in_=yg[:, :, TT2:2 * TT2]).then_inc(c.sems["YGL"], 16)
            with g.register() as r2:
                g.load(r2, selc_sb[0:1, 1:2])
                for pc in range(8):
                    dst = ygs[:, 0:8, :] if pc % 2 == 0 else ygs[:, 8:16, :]
                    with g.If_eq(r2, pc):
                        g.dma_start(out=dst,
                                    in_=agout.ap()[pc]).then_inc(c.sems["YGL"], 16)
        c.wait("sync", "CC")
        c.em("sync", _ygs_loads)
        c.wait("sync", "L2S", ld_all)
        c.wait("sync", "WT0")
        c.wait("sync", "WT1")
        c.wait("sync", "YGL", ygl)
        c.barrier()
        es_set.close()

        xts = sb("xts", [128, 2, DM])
        x2t = sb("x2t", [128, 2, DM])
        x2n = sb("x2n", [128, 2, DM], BF16)
        x2nf = sb("x2nf", [128, DM])
        qb = sb("qb", [128, DM], BF16)
        x2qT = sb("x2qT", [128, 2, 8, 128], BF16)
        ffnb = sb("ffnb", [128, 2, DM], BF16)
        outt = sb("outt", [128, 2, DM])
        stt = sb("stt", [128, 8])
        aps = es.enter_context(nc.psum_tensor("aps", [128, 2, DM], F32))
        fps = es.enter_context(nc.psum_tensor("fps", [128, 2, DM], F32))

        prev = {}
        for tt in range(0 if "l2" in skip else NTT):
            pb = tt % 2
            tsl = slice(tt * 128, (tt + 1) * 128)
            c.wait("sync", "VE", prev.get(("x2", tt - 2), 0))
            ldx = c.dma("sync", xts[:, pb, :],
                        _ap(xb_t, tt * 128 * DM, (DM, 128), (1, DM)), sem="XL%d" % pb)
            c.wait("tensor", "VE", prev.get(("aev", pb), 0))
            for kc in range(16):
                for hh in range(2):
                    c.op("tensor", "matmul", aps[:, pb, hh * 512:(hh + 1) * 512],
                         ygs[:, kc, tsl], woT[:, kc, hh * 512:(hh + 1) * 512],
                         start=(kc == 0), stop=(kc == 15))
            amm = c.op("tensor", "nop", sem="TE")
            c.wait("vector", "TE", amm)
            c.wait("vector", "XL%d" % pb, ldx)
            c.op("vector", "scalar_tensor_tensor", scr[:], aps[:, pb, :], 1.0,
                 bout_bc[:], ALU.mult, ALU.add)
            x2ve = c.op("vector", "tensor_add", x2t[:, pb, :], scr[:], xts[:, pb, :],
                        sem="VE")
            prev[("x2", tt)] = x2ve
            c.wait("scalar", "VE", x2ve)
            c.op("scalar", "activation", scr[:], x2t[:, pb, :], AF.Identity,
                 accum_out=stt[:, 0:1])
            sse = c.op("scalar", "activation", scr[:], x2t[:, pb, :], AF.Square,
                       accum_out=stt[:, 1:2], sem="SE")
            c.wait("vector", "SE", sse)
            c.op("vector", "tensor_scalar_mul", stt[:, 2:3], stt[:, 0:1], -1.0 / DM)
            c.op("vector", "tensor_scalar_mul", stt[:, 3:4], stt[:, 1:2], 1.0 / DM)
            c.op("vector", "tensor_tensor", stt[:, 4:5], stt[:, 2:3], stt[:, 2:3], ALU.mult)
            vve = c.op("vector", "tensor_sub", stt[:, 3:4], stt[:, 3:4], stt[:, 4:5],
                       sem="VE")
            c.wait("scalar", "VE", vve)
            c.op("scalar", "activation", stt[:, 3:4], stt[:, 3:4], AF.Ln, bias=ceps[:, 0:1])
            ivse = c.op("scalar", "activation", stt[:, 5:6], stt[:, 3:4], AF.Exp,
                        scale=-0.5, sem="SE")
            c.wait("vector", "SE", ivse)
            c.op("vector", "tensor_scalar", x2n[:, pb, :], x2t[:, pb, :],
                 stt[:, 2:3], stt[:, 5:6], ALU.add, ALU.mult)
            c.op("vector", "tensor_copy", x2nf[:], x2n[:, pb, :])
            c.op("vector", "tensor_scalar_mul", x2nf[:], x2nf[:], sinv[:, 2:3])
            quant_tile(qb[:], x2nf[:], DM)
            qve = c.op("vector", "nop", sem="VE")
            c.wait("sync", "VE", qve)
            c.wait("sync", "TE", prev.get(("fmm", pb), 0))
            trq = c.dma("sync", x2qT[:, pb, :, :], qb[:], transpose=True, sem="XT%d" % pb)
            c.wait("tensor", "XT%d" % pb, trq)
            c.wait("tensor", "VE", prev.get(("fev", pb), 0))
            for kc in range(8):
                for hh in range(2):
                    c.op("tensor", "matmul", fps[:, pb, hh * 512:(hh + 1) * 512],
                         x2qT[:, pb, kc, :], kq[:, kc, hh * 512:(hh + 1) * 512],
                         start=(kc == 0), stop=(kc == 7))
            fmm = c.op("tensor", "nop", sem="TE")
            prev[("fmm", pb)] = fmm
            c.wait("vector", "TE", fmm)
            c.op("vector", "scalar_tensor_tensor", ffnb[:, pb, :], fps[:, pb, :],
                 sinv[:, 1:2], fb_bc[:], ALU.mult, ALU.add)
            c.wait("vector", "OS%d" % pb, prev.get(("odma", tt - 2), 0))
            fev = c.op("vector", "tensor_add", outt[:, pb, :], x2t[:, pb, :],
                       ffnb[:, pb, :], sem="VE")
            prev[("fev", pb)] = fev
            prev[("aev", pb)] = fev
            c.wait("gpsimd", "VE", fev)
            prev[("odma", tt)] = c.dma("gpsimd", out_t[tsl, :], outt[:, pb, :],
                                       sem="OS%d" % pb)
        c.wait("gpsimd", "OS0")
        c.wait("gpsimd", "OS1")
        with allow_ici_all_to_all():
            c.replay(es.enter_context(nc.Block()))
    return nc


def _fused_inmaps(inputs):
    l1 = _l1_inmaps(inputs)
    x = np.ascontiguousarray(inputs["L0_attn_x"])
    W_out = np.ascontiguousarray(inputs["L0_attn_W_out"])
    b_out = np.ascontiguousarray(inputs["L0_attn_b_out"]).reshape(1, DM)
    kern = np.ascontiguousarray(inputs["L0_ffn_kernel"])
    s_in = np.ascontiguousarray(inputs["L0_ffn_input_scale"]).reshape(1, 1)
    s_k = np.ascontiguousarray(inputs["L0_ffn_kernel_scale"]).reshape(1, 1)
    fb = np.ascontiguousarray(inputs["L0_ffn_bias"]).reshape(1, DM)
    kern_r = np.ascontiguousarray(kern.reshape(8, 128, DM).transpose(1, 0, 2))
    maps = []
    for cc in range(8):
        b, th = cc // 2, cc % 2
        m = dict(l1[cc])
        m.update(dict(xb_t=np.ascontiguousarray(x[b, th * 2048:(th + 1) * 2048]),
                      w_out=W_out, b_out_r=b_out, kern=kern_r,
                      s_in=s_in, s_k=s_k, fbias=fb,
                      selc=np.array([[th, cc ^ 1, 0, 0]], dtype=np.uint32)))
        maps.append(m)
    return maps


def kernel(**inputs):
    nc = build_fused()
    r = run_bass_kernel_spmd(nc, _fused_inmaps(inputs), core_ids=list(range(8)))
    out = np.empty((B, T, DM), np.float32)
    for cc in range(8):
        b, th = cc // 2, cc % 2
        out[b, th * 2048:(th + 1) * 2048] = r.results[cc]["out_t"]
    return out


def kernel_2launch(**inputs):
    nc1 = build_l1()
    r1 = run_bass_kernel_spmd(nc1, _l1_inmaps(inputs), core_ids=list(range(8)))
    yg_l1 = [r1.results[cc]["yg"] for cc in range(8)]
    nc2 = build_l2()
    r2 = run_bass_kernel_spmd(nc2, _l2_inmaps(inputs, yg_l1), core_ids=list(range(8)))
    out = np.empty((B, T, DM), np.float32)
    for cc in range(8):
        b, th = cc // 2, cc % 2
        out[b, th * 2048:(th + 1) * 2048] = r2.results[cc]["out_t"]
    return out


# version 11
# speedup vs baseline: 1.0187x; 1.0187x over previous
import numpy as np
import ml_dtypes
from contextlib import ExitStack

import concourse.bass as bass
import concourse.mybir as mybir
from concourse.bass_utils import run_bass_kernel_spmd

F32 = mybir.dt.float32
BF16 = mybir.dt.bfloat16
U8 = mybir.dt.uint8
AF = mybir.ActivationFunctionType
ALU = mybir.AluOpType

B, T, DM, DI, N, K = 4, 4096, 1024, 2048, 16, 4
NDT = 8
TCA = 512
TCS = 512
EPS = 1e-5
VC1 = float(2.0 ** 20 + 1)
MAGIC = float(3 * 2 ** 13)


class Ctx:
    ENGS = ("sync", "vector", "scalar", "tensor", "gpsimd")

    def __init__(self, nc, es):
        self.nc = nc
        self._es = es
        self.ops = {e: [] for e in self.ENGS}
        self.sems = {}
        self.cnt = {}
        for s in ("LD", "TE", "VE", "SE", "SY"):
            self.sem(s)

    def sem(self, name):
        if name not in self.sems:
            self.sems[name] = self._es.enter_context(self.nc.semaphore(name))
            self.cnt[name] = 0
        return self.sems[name]

    def em(self, eng, fn):
        self.ops[eng].append(fn)

    def wait(self, eng, sem, val=None):
        v = self.cnt[sem] if val is None else val
        if v > 0:
            self.em(eng, lambda e, s=self.sems[sem], v=v: e.wait_ge(s, v))
        return v

    def dma(self, eng, out, in_, sem="LD", **kw):
        self.sem(sem)
        self.cnt[sem] += 16
        s = self.sems[sem]
        self.em(eng, lambda e, o=out, i=in_, kw=kw, s=s:
                e.dma_start(out=o, in_=i, **kw).then_inc(s, 16))
        return self.cnt[sem]

    def op(self, eng, meth, *a, sem=None, **kw):
        if meth != "nop":
            self.em(eng, lambda e, a=a, kw=kw: getattr(e, meth)(*a, **kw))
        if eng in ("vector", "scalar") or (eng == "tensor" and sem is not None):
            if sem is not None:
                self.sem(sem)
                self.cnt[sem] += 1
                s = self.sems[sem]
                self.em(eng, lambda e, s=s: e.drain().then_inc(s, 1))
                return self.cnt[sem]
            self.em(eng, lambda e: e.drain())
            return None
        if sem is not None:
            self.sem(sem)
            self.cnt[sem] += 1
            s = self.sems[sem]
            self.em(eng, lambda e, s=s: e.nop().then_inc(s, 1))
            return self.cnt[sem]
        return None

    def barrier(self):
        self.cnt["SY"] += 5
        tgt = self.cnt["SY"]
        for eng in self.ENGS:
            if eng in ("vector", "scalar", "tensor", "gpsimd"):
                self.em(eng, lambda e: e.drain())
            self.em(eng, lambda e, s=self.sems["SY"]: e.sem_inc(s, 1))
        for eng in self.ENGS:
            self.em(eng, lambda e, s=self.sems["SY"], v=tgt: e.wait_ge(s, v))

    def replay(self, blk):
        for eng in self.ENGS:
            ops = self.ops[eng]
            getattr(blk, eng)(lambda e, ops=ops: [f(e) for f in ops])


def _ap(t, off, *dims):
    return bass.AP(t.tensor if isinstance(t, bass.AP) else t, off, [list(d) for d in dims])



def build_l1(debug=()):
    nc = bass.Bass("TRN2", target_bir_lowering=False, debug=False)
    xb = nc.dram_tensor("xb", [T, DM], F32, kind="ExternalInput").ap()
    w_in_s = nc.dram_tensor("w_in_s", [24 * 128, DM], F32, kind="ExternalInput").ap()
    b_in_s = nc.dram_tensor("b_in_s", [128, 24], F32, kind="ExternalInput").ap()
    w_conv_s = nc.dram_tensor("w_conv_s", [128, 16 * K], F32, kind="ExternalInput").ap()
    b_conv_s = nc.dram_tensor("b_conv_s", [128, 16], F32, kind="ExternalInput").ap()
    w_x_s = nc.dram_tensor("w_x_s", [9 * 128, DI], F32, kind="ExternalInput").ap()
    b_x_s = nc.dram_tensor("b_x_s", [128, 9], F32, kind="ExternalInput").ap()
    a_log_s = nc.dram_tensor("a_log_s", [128, NDT * N], F32, kind="ExternalInput").ap()
    d_s = nc.dram_tensor("d_s", [128, NDT], F32, kind="ExternalInput").ap()
    eye_in = nc.dram_tensor("eye_in", [128, 128], BF16, kind="ExternalInput").ap()
    yg = nc.dram_tensor("yg", [128, NDT, T], BF16, kind="ExternalOutput").ap()
    dbg_t = {}
    for name, shape, dt in debug:
        dbg_t[name] = nc.dram_tensor("dbg_" + name, shape, dt, kind="ExternalOutput").ap()
    xi_h = nc.dram_tensor("xi_h", [128, NDT, T], BF16).ap()
    dl_h = nc.dram_tensor("dl_h", [128, NDT, T], F32).ap()
    rs_h = nc.dram_tensor("rs_h", [128, NDT, T], F32).ap()
    bc_h = nc.dram_tensor("bc_h", [32, T], BF16).ap()

    es = ExitStack()
    with es:
        c = Ctx(nc, es)
        sb = lambda n, s, d=F32: es.enter_context(nc.sbuf_tensor(n, s, d))
        eye = sb("eye", [128, 128], BF16)
        b_in_t = sb("b_in_t", [128, 24])
        b_conv_t = sb("b_conv_t", [128, 16])
        b_x_t = sb("b_x_t", [128, 9])
        wconv_t = sb("wconv_t", [128, 16 * K])
        a_t = sb("a_t", [128, NDT * N])
        d_t = sb("d_t", [128, NDT])
        ceps = sb("ceps", [128, 1])
        states = sb("states", [128, NDT * N])
        es_ab = es.enter_context(ExitStack())
        sba = lambda n, s, d=F32: es_ab.enter_context(nc.sbuf_tensor(n, s, d))
        winT = sba("winT", [128, 8, 24 * 128], BF16)
        wxT = sba("wxT", [128, 16, 9 * 128], BF16)
        diag = sba("diag", [128, 16 * K, 128], BF16)

        c.op("vector", "memset", ceps[:], EPS)
        c.op("vector", "memset", states[:], 0.0)
        for dst, src in ((b_in_t, b_in_s), (b_conv_t, b_conv_s), (b_x_t, b_x_s),
                         (wconv_t, w_conv_s), (a_t, a_log_s), (d_t, d_s), (eye, eye_in)):
            c.dma("sync", dst[:], src)
        ld0 = c.cnt["LD"]
        c.wait("scalar", "LD", ld0)
        c.op("scalar", "activation", a_t[:], a_t[:], AF.Exp, sem="SE")
        c.wait("vector", "SE")
        c.op("vector", "tensor_scalar_mul", a_t[:], a_t[:], -1.0)
        c.wait("vector", "LD", ld0)
        for i in range(16 * K):
            c.op("vector", "tensor_scalar_mul", diag[:, i, :], eye[:], wconv_t[:, i:i + 1])
        c.op("vector", "nop", sem="VE")

        es_stg = ExitStack()
        stg_f = es_stg.enter_context(nc.sbuf_tensor("stg_f", [128, 2, DI], F32))
        stg_b = es_stg.enter_context(nc.sbuf_tensor("stg_b", [128, 2, DI], BF16))

        tr_done = [0, 0]

        def prep_w(src, n_ct, kdim, dstT):
            for ct in range(n_ct):
                p = ct % 2
                if tr_done[p]:
                    c.wait("sync", "WT%d" % p, tr_done[p])
                ld = c.dma("sync", stg_f[:, p, 0:kdim], src[ct * 128:(ct + 1) * 128, :],
                           sem="WL%d" % p)
                c.wait("vector", "WL%d" % p, ld)
                c.wait("vector", "WT%d" % p, tr_done[p])
                cd = c.op("vector", "tensor_copy", stg_b[:, p, 0:kdim],
                          stg_f[:, p, 0:kdim], sem="VE")
                c.wait("sync", "VE", cd)
                tr_done[p] = c.dma("sync", dstT[:, :, ct * 128:(ct + 1) * 128],
                                   stg_b[:, p, 0:kdim], transpose=True, sem="WT%d" % p)
        prep_w(w_in_s, 24, DM, winT)
        prep_w(w_x_s, 9, DI, wxT)
        c.wait("sync", "WT0")
        c.wait("sync", "WT1")
        c.wait("sync", "LD")
        c.barrier()
        es_stg.close()

        xtok = sba("xtok", [128, 1, 4, DM])
        scr = sba("scr", [128, DM])
        ssum = sba("ssum", [128, 1, 4])
        ssq = sba("ssq", [128, 1, 4])
        stat = sba("stat", [128, 1, 12])
        vtmp = sba("vtmp", [128, 4])
        xntok = sba("xntok", [128, 1, 4, DM], BF16)
        xnT = sba("xnT", [128, 1, 8, TCA], BF16)
        xipre = sba("xipre", [128, 16, 3 + TCA], BF16)
        xichk = sba("xichk", [128, 1, 16, TCA], BF16)
        reschk = sba("reschk", [128, 1, NDT, TCA])
        dlchk = sba("dlchk", [128, 1, NDT, TCA])
        bcchk = sba("bcchk", [32, 1, TCA], BF16)
        sps = es_ab.enter_context(nc.psum_tensor("sps", [128, 2, TCA], F32))
        cps = es_ab.enter_context(nc.psum_tensor("cps", [128, 2, TCA], F32))
        pps = es_ab.enter_context(nc.psum_tensor("pps", [128, 2, TCA], F32))

        c.op("vector", "memset", xipre[:, :, 0:3], 0.0)
        c.op("vector", "nop", sem="VE")

        for _sm in ("RS", "DS", "BS", "XS", "YS0", "YS1", "OS0", "OS1"):
            c.sem(_sm)
        NCH = 0 if "ab" in skip else T // TCA
        xn_ve = [0, 0]
        xr_mm = [0, 0]
        ev_ve = {}
        ev_se = {}
        conv_mm = 0
        xi_st = [0, 0]
        dl_st = [0, 0]
        rs_st = [0, 0]
        bc_st = [0, 0]
        for tc in range(NCH):
            pb = 0
            xin = _ap(xb, tc * TCA * DM, (DM, 128), (128 * DM, 4), (1, DM))
            c.wait("sync", "VE", xn_ve[pb])
            ldx = c.dma("sync", xtok[:, pb, :, :], xin, sem="XL")
            c.wait("scalar", "XL", ldx)
            for j in range(4):
                c.op("scalar", "activation", scr[:], xtok[:, pb, j, :], AF.Identity,
                     accum_out=ssum[:, pb, j:j + 1])
                c.op("scalar", "activation", scr[:], xtok[:, pb, j, :], AF.Square,
                     accum_out=ssq[:, pb, j:j + 1])
            st_se = c.op("scalar", "nop", sem="SE")
            c.wait("vector", "SE", st_se)
            c.op("vector", "tensor_scalar_mul", stat[:, pb, 0:4], ssum[:, pb, :], -1.0 / DM)
            c.op("vector", "tensor_scalar_mul", stat[:, pb, 4:8], ssq[:, pb, :], 1.0 / DM)
            c.op("vector", "tensor_tensor", vtmp[:], stat[:, pb, 0:4],
                 stat[:, pb, 0:4], ALU.mult)
            vw = c.op("vector", "tensor_sub", stat[:, pb, 4:8], stat[:, pb, 4:8],
                      vtmp[:], sem="VE")
            c.wait("scalar", "VE", vw)
            c.op("scalar", "activation", stat[:, pb, 4:8], stat[:, pb, 4:8],
                 AF.Ln, bias=ceps[:, 0:1])
            iv_se = c.op("scalar", "activation", stat[:, pb, 8:12], stat[:, pb, 4:8],
                         AF.Exp, scale=-0.5, sem="SE")
            c.wait("vector", "SE", iv_se)
            c.wait("vector", "XL", ldx)
            for j in range(4):
                c.op("vector", "tensor_scalar", xntok[:, pb, j, :], xtok[:, pb, j, :],
                     stat[:, pb, j:j + 1], stat[:, pb, 8 + j:9 + j], ALU.add, ALU.mult)
            xn_ve[pb] = c.op("vector", "nop", sem="VE")
            if "xn" in dbg_t and tc == 0:
                c.wait("gpsimd", "VE", xn_ve[pb])
                c.dma("gpsimd", dbg_t["xn"], xntok[:, 0, :, :])
            c.wait("sync", "VE", xn_ve[pb])
            c.wait("sync", "TE", xr_mm[pb])
            for j in range(4):
                trx = c.dma("sync", xnT[:, pb, :, j * 128:(j + 1) * 128],
                            xntok[:, pb, j, :], transpose=True, sem="XT")
            c.wait("tensor", "XT", trx)
            c.wait("tensor", "TE", conv_mm)
            for ct in range(24):
                pp = ct % 2
                c.wait("tensor", "VE", ev_ve.get(("xr", pp), 0))
                c.wait("tensor", "SE", ev_se.get(("xr", pp), 0))
                for kc in range(8):
                    c.op("tensor", "matmul", sps[:, pp, :],
                         winT[:, kc, ct * 128:(ct + 1) * 128],
                         xnT[:, pb, kc, :], start=(kc == 0), stop=(kc == 7))
                mm = c.op("tensor", "nop", sem="TE")
                if ct < 16:
                    c.wait("vector", "TE", mm)
                    ev_ve[("xr", pp)] = c.op(
                        "vector", "tensor_scalar_add", xipre[:, ct, 3:3 + TCA],
                        sps[:, pp, :], b_in_t[:, ct:ct + 1], sem="VE")
                else:
                    ct8 = ct - 16
                    c.wait("scalar", "TE", mm)
                    c.wait("scalar", "RS", rs_st[pb])
                    rs_st[pb] = 0
                    ev_se[("xr", pp)] = c.op(
                        "scalar", "activation", reschk[:, pb, ct8, :], sps[:, pp, :],
                        AF.Silu, bias=b_in_t[:, ct:ct + 1], sem="SE")
            xr_mm[pb] = c.cnt["TE"]
            c.wait("gpsimd", "SE", ev_se[("xr", 1)])
            c.wait("gpsimd", "SE", ev_se[("xr", 0)])
            rs_st[pb] = c.dma("gpsimd", rs_h[:, :, tc * TCA:(tc + 1) * TCA],
                              reschk[:, pb, :, :], sem="RS")
            if "xipre" in dbg_t and tc == 0:
                c.wait("gpsimd", "VE", ev_ve[("xr", 1)])
                c.wait("gpsimd", "VE", ev_ve[("xr", 0)])
                c.dma("gpsimd", dbg_t["xipre"], xipre[:, :, 3:3 + TCA])
            c.wait("tensor", "VE", ev_ve[("xr", 0)])
            c.wait("tensor", "VE", ev_ve[("xr", 1)])
            c.wait("scalar", "XS", xi_st[pb])
            for dt in range(16):
                pp = dt % 2
                c.wait("tensor", "SE", ev_se.get(("cv", pp), 0))
                for k in range(K):
                    c.op("tensor", "matmul", cps[:, pp, :], diag[:, dt * K + k, :],
                         xipre[:, dt, k:k + TCA], start=(k == 0), stop=(k == K - 1))
                mm = c.op("tensor", "nop", sem="TE")
                c.wait("scalar", "TE", mm)
                ev_se[("cv", pp)] = c.op(
                    "scalar", "activation", xichk[:, pb, dt, :], cps[:, pp, :],
                    AF.Silu, bias=b_conv_t[:, dt:dt + 1], sem="SE")
            conv_mm = c.cnt["TE"]
            c.wait("vector", "TE", conv_mm)
            c.op("vector", "tensor_copy", xipre[:, :, 0:3],
                 xipre[:, :, TCA:TCA + 3], sem="VE")
            c.wait("gpsimd", "SE", ev_se[("cv", 0)])
            c.wait("gpsimd", "SE", ev_se[("cv", 1)])
            xi_st[pb] = c.dma("gpsimd", xi_h[:, :, tc * TCA:(tc + 1) * TCA],
                              xichk[:, pb, 0:NDT, :], sem="XS")
            c.wait("tensor", "SE", ev_se[("cv", 0)])
            c.wait("tensor", "SE", ev_se[("cv", 1)])
            for ct in range(9):
                pp = ct % 2
                c.wait("tensor", "SE", ev_se.get(("pj", pp), 0))
                c.wait("tensor", "VE", ev_ve.get(("pj", pp), 0))
                for kc in range(16):
                    c.op("tensor", "matmul", pps[:, pp, :],
                         wxT[:, kc, ct * 128:(ct + 1) * 128],
                         xichk[:, pb, kc, :], start=(kc == 0), stop=(kc == 15))
                mm = c.op("tensor", "nop", sem="TE")
                if ct < 8:
                    c.wait("scalar", "TE", mm)
                    c.wait("scalar", "DS", dl_st[pb])
                    dl_st[pb] = 0
                    c.op("scalar", "activation", scr[:, 0:TCA], pps[:, pp, :],
                         AF.Exp, bias=b_x_t[:, ct:ct + 1])
                    ev_se[("pj", pp)] = c.op(
                        "scalar", "activation", dlchk[:, pb, ct, :], scr[:, 0:TCA],
                        AF.Ln, bias=1.0, sem="SE")
                else:
                    c.wait("vector", "TE", mm)
                    c.wait("vector", "BS", bc_st[pb])
                    bc_st[pb] = 0
                    ev_ve[("pj", pp)] = c.op(
                        "vector", "tensor_scalar_add", bcchk[:, pb, :],
                        pps[0:32, pp, :], b_x_t[0:32, ct:ct + 1], sem="VE")
            c.wait("gpsimd", "SE", ev_se[("pj", 0)])
            c.wait("gpsimd", "SE", ev_se[("pj", 1)])
            dl_st[pb] = c.dma("gpsimd", dl_h[:, :, tc * TCA:(tc + 1) * TCA],
                              dlchk[:, pb, :, :], sem="DS")
            c.wait("gpsimd", "VE", ev_ve[("pj", 0)])
            bc_st[pb] = c.dma("gpsimd", bc_h[:, tc * TCA:(tc + 1) * TCA],
                              bcchk[:, pb, :], sem="BS")
        for sm in ("RS", "DS", "BS", "XS"):
            c.wait("gpsimd", sm)
        c.barrier()
        es_ab.close()

        bcB = sb("bcB", [128, N, TCS], BF16)
        bcC = sb("bcC", [128, N, TCS], BF16)
        dls = sb("dls", [128, 2, TCS])
        xis = sb("xis", [128, 2, TCS], BF16)
        rss = sb("rss", [128, 2, TCS])
        wt = sb("wt", [128, 2, TCS], BF16)
        ut = sb("ut", [128, 2, TCS], BF16)
        dat = sb("dat", [128, 2, TCS])
        dbt = sb("dbt", [128, 2, TCS], BF16)
        ht = sb("ht", [128, 2, TCS], BF16)
        tnc = sb("tnc", [128, 4, TCS], BF16)
        g1 = sb("g1", [128, 2, TCS])
        ygt = sb("ygt", [128, 2, TCS], BF16)
        yps = es.enter_context(nc.psum_tensor("yps", [128, 2, TCS], F32))

        NSC = 0 if "scan" in skip else T // TCS
        last_cons = [0, 0]
        last_gate = [0, 0]
        last_ygdma = [0, 0]
        last_ymm = [0, 0, 0, 0]
        last_tnc = [0, 0]
        prev_chunk_ymm = 0
        for tc in range(NSC):
            c.wait("sync", "TE", prev_chunk_ymm)
            c.wait("sync", "BS")
            bcl = 0
            for n in range(N):
                c.dma("sync", bcB[:, n, :],
                      _ap(bc_h, n * T + tc * TCS, (0, 128), (1, TCS)), sem="BCL")
                bcl = c.dma("sync", bcC[:, n, :],
                            _ap(bc_h, (16 + n) * T + tc * TCS, (0, 128), (1, TCS)), sem="BCL")
            for dt in range(NDT):
                pd = dt % 2
                sl = slice(tc * TCS, (tc + 1) * TCS)
                c.wait("sync", "VE", last_cons[pd])
                c.dma("sync", dls[:, pd, :], dl_h[:, dt, sl], sem="SL%d" % pd)
                c.dma("sync", xis[:, pd, :], xi_h[:, dt, sl], sem="SL%d" % pd)
                sld = c.dma("sync", rss[:, pd, :], rs_h[:, dt, sl], sem="SL%d" % pd)
                c.wait("vector", "SL%d" % pd, sld)
                c.op("vector", "tensor_mul", wt[:, pd, :], dls[:, pd, :], xis[:, pd, :])
                c.op("vector", "tensor_scalar_mul", ut[:, pd, :], xis[:, pd, :],
                     d_t[:, dt:dt + 1], sem="VE")
                c.wait("scalar", "SL%d" % pd, sld)
                for n in range(N):
                    pn = n % 2
                    idx = dt * N + n
                    c.wait("scalar", "VE", last_tnc[pn])
                    da_se = c.op("scalar", "activation", dat[:, pn, :], dls[:, pd, :],
                                 AF.Exp, scale=a_t[:, idx:idx + 1], sem="SE")
                    c.wait("vector", "BCL", bcl)
                    c.wait("vector", "SE", da_se)
                    c.wait("vector", "TE", last_ymm[n % 4])
                    c.op("vector", "tensor_mul", dbt[:, pn, :], wt[:, pd, :], bcB[:, n, :])
                    c.op("vector", "tensor_tensor_scan", ht[:, pn, :], dat[:, pn, :],
                         dbt[:, pn, :], states[:, idx:idx + 1], ALU.mult, ALU.add)
                    c.op("vector", "tensor_copy", states[:, idx:idx + 1],
                         ht[:, pn, TCS - 1:TCS])
                    last_tnc[pn] = c.op("vector", "tensor_mul", tnc[:, n % 4, :],
                                        ht[:, pn, :], bcC[:, n, :], sem="VE")
                    c.wait("tensor", "VE", last_tnc[pn])
                    if n == 0:
                        c.wait("tensor", "VE", last_gate[pd])
                    for h2 in range(TCS // TCA):
                        c.op("tensor", "matmul", yps[:, pd, h2 * TCA:(h2 + 1) * TCA],
                             eye[:], tnc[:, n % 4, h2 * TCA:(h2 + 1) * TCA],
                             start=(n == 0), stop=(n == N - 1))
                    last_ymm[n % 4] = c.op("tensor", "nop", sem="TE")
                c.wait("vector", "TE", last_ymm[(N - 1) % 4])
                c.wait("vector", "YS%d" % pd, last_ygdma[pd])
                c.op("vector", "scalar_tensor_tensor", g1[:, pd, :], yps[:, pd, :],
                     1.0, ut[:, pd, :], ALU.mult, ALU.add)
                gv = c.op("vector", "tensor_mul", ygt[:, pd, :], g1[:, pd, :],
                          rss[:, pd, :], sem="VE")
                last_gate[pd] = gv
                last_cons[pd] = gv
                c.wait("gpsimd", "VE", gv)
                last_ygdma[pd] = c.dma("gpsimd", yg[:, dt, sl], ygt[:, pd, :],
                                       sem="YS%d" % pd)
            prev_chunk_ymm = c.cnt["TE"]
        c.wait("gpsimd", "YS0")
        c.wait("gpsimd", "YS1")
        c.replay(es.enter_context(nc.Block()))
    return nc



def build_l2(debug=()):
    TT2 = 2048
    NTT = TT2 // 128
    nc = bass.Bass("TRN2", target_bir_lowering=False, debug=False)
    ygb = nc.dram_tensor("ygb", [128, 16, TT2], BF16, kind="ExternalInput").ap()
    xb_t = nc.dram_tensor("xb_t", [TT2, DM], F32, kind="ExternalInput").ap()
    w_out = nc.dram_tensor("w_out", [DM, DI], F32, kind="ExternalInput").ap()
    b_out_r = nc.dram_tensor("b_out_r", [1, DM], F32, kind="ExternalInput").ap()
    kern = nc.dram_tensor("kern", [128, 8, DM], BF16, kind="ExternalInput").ap()
    s_in = nc.dram_tensor("s_in", [1, 1], F32, kind="ExternalInput").ap()
    s_k = nc.dram_tensor("s_k", [1, 1], F32, kind="ExternalInput").ap()
    fbias = nc.dram_tensor("fbias", [1, DM], BF16, kind="ExternalInput").ap()
    out_t = nc.dram_tensor("out_t", [TT2, DM], F32, kind="ExternalOutput").ap()
    dbg_t = {}
    for name, shape, dt in debug:
        dbg_t[name] = nc.dram_tensor("dbg_" + name, shape, dt, kind="ExternalOutput").ap()

    es = ExitStack()
    with es:
        c = Ctx(nc, es)
        sb = lambda n, s, d=F32: es.enter_context(nc.sbuf_tensor(n, s, d))

        ygs = sb("ygs", [128, 16, TT2], BF16)
        woT = sb("woT", [128, 16, DM], BF16)
        kq = sb("kq", [128, 8, DM], BF16)
        bout_bc = sb("bout_bc", [128, DM])
        fb_bc = sb("fb_bc", [128, DM], BF16)
        sinv = sb("sinv", [128, 3])
        ceps = sb("ceps", [128, 1])
        scr = sb("scr", [128, DM])
        scrq = sb("scrq", [128, 4, DM])
        msk = sb("msk", [128, DM], U8)
        qf = sb("qf", [128, DM])
        reck = sb("reck", [128, 1])
        reck2 = sb("reck2", [128, 1])
        ksrc = sb("ksrc", [128, DM])
        es_set = es.enter_context(ExitStack())
        sbs = lambda n, s2, d=F32: es_set.enter_context(nc.sbuf_tensor(n, s2, d))

        c.op("vector", "memset", ceps[:], EPS)
        kbf = sbs("kbf", [128, 8, DM], BF16)
        c.dma("sync", ygs[:], ygb)
        c.dma("sync", bout_bc[:], _ap(b_out_r, 0, (0, 128), (1, DM)))
        c.dma("sync", fb_bc[:], _ap(fbias, 0, (0, 128), (1, DM)))
        c.dma("sync", sinv[:, 0:1], _ap(s_in, 0, (0, 128), (1, 1)))
        c.dma("sync", sinv[:, 1:2], _ap(s_k, 0, (0, 128), (1, 1)))
        c.dma("sync", reck[:], _ap(s_k, 0, (0, 128), (1, 1)))
        c.dma("sync", kbf[:], kern)
        ld_all = c.cnt["LD"]
        stg_f = sbs("stg2_f", [128, 2, DI])
        stg_b = sbs("stg2_b", [128, 2, DI], BF16)
        tr_done = [0, 0]
        for ct in range(0 if "woprep" in skip else 8):
            p = ct % 2
            if tr_done[p]:
                c.wait("sync", "WT%d" % p, tr_done[p])
            ld = c.dma("sync", stg_f[:, p, :], w_out[ct * 128:(ct + 1) * 128, :],
                       sem="WL%d" % p)
            c.wait("vector", "WL%d" % p, ld)
            c.wait("vector", "WT%d" % p, tr_done[p])
            cd = c.op("vector", "tensor_copy", stg_b[:, p, :], stg_f[:, p, :], sem="VE")
            c.wait("sync", "VE", cd)
            tr_done[p] = c.dma("sync", woT[:, :, ct * 128:(ct + 1) * 128],
                               stg_b[:, p, :], transpose=True, sem="WT%d" % p)
        c.wait("vector", "LD", ld_all)
        c.op("vector", "tensor_mul", sinv[:, 1:2], sinv[:, 0:1], sinv[:, 1:2])
        c.op("vector", "reciprocal", sinv[:, 2:3], sinv[:, 0:1], sem="VE")
        c.op("vector", "reciprocal", reck2[:], reck[:], sem="VE")

        def quant_tile(dst_bf, src_f32, width):
            t1, t2, hi = scrq[:, 0, 0:width], scrq[:, 1, 0:width], scrq[:, 2, 0:width]
            sq = scrq[:, 3, 0:width]
            c.op("vector", "tensor_scalar_mul", t1, src_f32, VC1)
            c.op("vector", "tensor_sub", t2, t1, src_f32)
            c.op("vector", "tensor_sub", hi, t1, t2)
            c.op("vector", "tensor_scalar", qf[:, 0:width], src_f32, MAGIC, MAGIC,
                 ALU.add, ALU.subtract)
            c.op("vector", "scalar_tensor_tensor", sq, src_f32, 1.0, src_f32,
                 ALU.mult, ALU.mult)
            c.op("vector", "tensor_single_scalar", msk[:, 0:width], sq,
                 float(2.0 ** -10), ALU.is_ge)
            c.op("vector", "copy_predicated", qf[:, 0:width], msk[:, 0:width], hi)
            c.op("vector", "tensor_scalar", dst_bf, qf[:, 0:width], 448.0, -448.0,
                 ALU.min, ALU.max)

        for kc in range(0 if "kq" in skip else 8):
            c.op("vector", "tensor_copy", scr[:], kbf[:, kc, :])
            c.op("vector", "tensor_scalar_mul", ksrc[:], scr[:], reck2[:, 0:1])
            quant_tile(kq[:, kc, :], ksrc[:], DM)
        c.op("vector", "nop", sem="VE")
        c.wait("sync", "LD", ld_all)
        c.wait("sync", "WT0")
        c.wait("sync", "WT1")
        c.barrier()
        es_set.close()

        xts = sb("xts", [128, 2, DM])
        x2t = sb("x2t", [128, 2, DM])
        x2n = sb("x2n", [128, 2, DM], BF16)
        x2nf = sb("x2nf", [128, DM])
        qb = sb("qb", [128, DM], BF16)
        x2qT = sb("x2qT", [128, 2, 8, 128], BF16)
        ffnb = sb("ffnb", [128, 2, DM], BF16)
        outt = sb("outt", [128, 2, DM])
        stt = sb("stt", [128, 8])
        aps = es.enter_context(nc.psum_tensor("aps", [128, 2, DM], F32))
        fps = es.enter_context(nc.psum_tensor("fps", [128, 2, DM], F32))

        prev = {}
        for tt in range(0 if "l2" in skip else NTT):
            pb = tt % 2
            tsl = slice(tt * 128, (tt + 1) * 128)
            c.wait("sync", "VE", prev.get(("x2", tt - 2), 0))
            ldx = c.dma("sync", xts[:, pb, :],
                        _ap(xb_t, tt * 128 * DM, (DM, 128), (1, DM)), sem="XL%d" % pb)
            c.wait("tensor", "VE", prev.get(("aev", pb), 0))
            for kc in range(16):
                for hh in range(2):
                    c.op("tensor", "matmul", aps[:, pb, hh * 512:(hh + 1) * 512],
                         ygs[:, kc, tsl], woT[:, kc, hh * 512:(hh + 1) * 512],
                         start=(kc == 0), stop=(kc == 15))
            amm = c.op("tensor", "nop", sem="TE")
            c.wait("vector", "TE", amm)
            c.wait("vector", "XL%d" % pb, ldx)
            c.op("vector", "scalar_tensor_tensor", scr[:], aps[:, pb, :], 1.0,
                 bout_bc[:], ALU.mult, ALU.add)
            x2ve = c.op("vector", "tensor_add", x2t[:, pb, :], scr[:], xts[:, pb, :],
                        sem="VE")
            prev[("x2", tt)] = x2ve
            c.wait("scalar", "VE", x2ve)
            c.op("scalar", "activation", scr[:], x2t[:, pb, :], AF.Identity,
                 accum_out=stt[:, 0:1])
            sse = c.op("scalar", "activation", scr[:], x2t[:, pb, :], AF.Square,
                       accum_out=stt[:, 1:2], sem="SE")
            c.wait("vector", "SE", sse)
            c.op("vector", "tensor_scalar_mul", stt[:, 2:3], stt[:, 0:1], -1.0 / DM)
            c.op("vector", "tensor_scalar_mul", stt[:, 3:4], stt[:, 1:2], 1.0 / DM)
            c.op("vector", "tensor_tensor", stt[:, 4:5], stt[:, 2:3], stt[:, 2:3], ALU.mult)
            vve = c.op("vector", "tensor_sub", stt[:, 3:4], stt[:, 3:4], stt[:, 4:5],
                       sem="VE")
            c.wait("scalar", "VE", vve)
            c.op("scalar", "activation", stt[:, 3:4], stt[:, 3:4], AF.Ln, bias=ceps[:, 0:1])
            ivse = c.op("scalar", "activation", stt[:, 5:6], stt[:, 3:4], AF.Exp,
                        scale=-0.5, sem="SE")
            c.wait("vector", "SE", ivse)
            c.op("vector", "tensor_scalar", x2n[:, pb, :], x2t[:, pb, :],
                 stt[:, 2:3], stt[:, 5:6], ALU.add, ALU.mult)
            c.op("vector", "tensor_copy", x2nf[:], x2n[:, pb, :])
            c.op("vector", "tensor_scalar_mul", x2nf[:], x2nf[:], sinv[:, 2:3])
            quant_tile(qb[:], x2nf[:], DM)
            qve = c.op("vector", "nop", sem="VE")
            c.wait("sync", "VE", qve)
            c.wait("sync", "TE", prev.get(("fmm", pb), 0))
            trq = c.dma("sync", x2qT[:, pb, :, :], qb[:], transpose=True, sem="XT%d" % pb)
            c.wait("tensor", "XT%d" % pb, trq)
            c.wait("tensor", "VE", prev.get(("fev", pb), 0))
            for kc in range(8):
                for hh in range(2):
                    c.op("tensor", "matmul", fps[:, pb, hh * 512:(hh + 1) * 512],
                         x2qT[:, pb, kc, :], kq[:, kc, hh * 512:(hh + 1) * 512],
                         start=(kc == 0), stop=(kc == 7))
            fmm = c.op("tensor", "nop", sem="TE")
            prev[("fmm", pb)] = fmm
            c.wait("vector", "TE", fmm)
            c.op("vector", "scalar_tensor_tensor", ffnb[:, pb, :], fps[:, pb, :],
                 sinv[:, 1:2], fb_bc[:], ALU.mult, ALU.add)
            c.wait("vector", "OS%d" % pb, prev.get(("odma", tt - 2), 0))
            fev = c.op("vector", "tensor_add", outt[:, pb, :], x2t[:, pb, :],
                       ffnb[:, pb, :], sem="VE")
            prev[("fev", pb)] = fev
            prev[("aev", pb)] = fev
            c.wait("gpsimd", "VE", fev)
            c.wait("gpsimd", "OS%d" % pb, prev.get(("odma", tt - 2), 0))
            prev[("odma", tt)] = c.dma("gpsimd", out_t[tsl, :], outt[:, pb, :],
                                       sem="OS%d" % pb)
        c.wait("gpsimd", "OS0")
        c.wait("gpsimd", "OS1")
        c.replay(es.enter_context(nc.Block()))
    return nc



def _l1_inmaps(inputs):
    x = np.ascontiguousarray(inputs["L0_attn_x"])
    W_in = np.ascontiguousarray(inputs["L0_attn_W_in"])
    b_in = np.ascontiguousarray(inputs["L0_attn_b_in"])
    W_conv = np.ascontiguousarray(inputs["L0_attn_W_conv"])
    b_conv = np.ascontiguousarray(inputs["L0_attn_b_conv"])
    W_x = np.ascontiguousarray(inputs["L0_attn_W_x"])
    b_x = np.ascontiguousarray(inputs["L0_attn_b_x"])
    A_log = np.ascontiguousarray(inputs["L0_attn_A_log"])
    Dv = np.ascontiguousarray(inputs["L0_attn_D"])
    eye = np.eye(128, dtype=ml_dtypes.bfloat16)
    maps = []
    for cc in range(8):
        b, dh = cc // 2, cc % 2
        own = np.arange(1024 * dh, 1024 * dh + 1024)
        oth = np.arange(1024 * (1 - dh), 1024 * (1 - dh) + 1024)
        perm = np.concatenate([own, oth])
        w_in_s = np.concatenate([W_in[perm], W_in[DI + own]], axis=0)
        b_in_cat = np.concatenate([b_in[perm], b_in[DI + own]])
        b_in_s = np.ascontiguousarray(b_in_cat.reshape(24, 128).T)
        wc = W_conv[perm, 0, :]
        w_conv_s = np.ascontiguousarray(
            wc.reshape(16, 128, K).transpose(1, 0, 2).reshape(128, 16 * K))
        b_conv_s = np.ascontiguousarray(b_conv[perm].reshape(16, 128).T)
        wx_rows = np.concatenate([W_x[own], W_x[DI:DI + 32],
                                  np.zeros((96, DI), W_x.dtype)], axis=0)
        w_x_s = np.ascontiguousarray(wx_rows[:, perm])
        bx9 = np.zeros((128, 9), np.float32)
        bx9[:, 0:8] = b_x[own].reshape(8, 128).T
        bx9[0:32, 8] = b_x[DI:DI + 32]
        a_log_s = np.ascontiguousarray(
            A_log[own].reshape(8, 128, N).transpose(1, 0, 2).reshape(128, 8 * N))
        d_s = np.ascontiguousarray(Dv[own].reshape(8, 128).T)
        maps.append(dict(xb=x[b], w_in_s=w_in_s, b_in_s=b_in_s, w_conv_s=w_conv_s,
                         b_conv_s=b_conv_s, w_x_s=w_x_s, b_x_s=bx9,
                         a_log_s=a_log_s, d_s=d_s, eye_in=eye))
    return maps


def _l2_inmaps(inputs, yg_l1):
    x = np.ascontiguousarray(inputs["L0_attn_x"])
    W_out = np.ascontiguousarray(inputs["L0_attn_W_out"])
    b_out = np.ascontiguousarray(inputs["L0_attn_b_out"]).reshape(1, DM)
    kern = np.ascontiguousarray(inputs["L0_ffn_kernel"])
    s_in = np.ascontiguousarray(inputs["L0_ffn_input_scale"]).reshape(1, 1)
    s_k = np.ascontiguousarray(inputs["L0_ffn_kernel_scale"]).reshape(1, 1)
    fb = np.ascontiguousarray(inputs["L0_ffn_bias"]).reshape(1, DM)
    kern_r = np.ascontiguousarray(
        kern.reshape(8, 128, DM).transpose(1, 0, 2))
    maps = []
    for cc in range(8):
        b, th = cc // 2, cc % 2
        ygb = np.empty((128, 16, 2048), dtype=ml_dtypes.bfloat16)
        for g in range(16):
            src = yg_l1[2 * b + g // 8]
            ygb[:, g, :] = src[:, g % 8, th * 2048:(th + 1) * 2048]
        maps.append(dict(ygb=np.ascontiguousarray(ygb),
                         xb_t=np.ascontiguousarray(x[b, th * 2048:(th + 1) * 2048]),
                         w_out=W_out, b_out_r=b_out, kern=kern_r,
                         s_in=s_in, s_k=s_k, fbias=fb))
    return maps





def build_fused(debug=(), no_cc=False, no_branch=False, skip=()):
    from concourse.aot_env import allow_ici_all_to_all
    TT2 = 2048
    NTT = TT2 // 128
    nc = bass.Bass("TRN2", target_bir_lowering=False, debug=False)
    xb = nc.dram_tensor("xb", [T, DM], F32, kind="ExternalInput").ap()
    w_in_s = nc.dram_tensor("w_in_s", [24 * 128, DM], F32, kind="ExternalInput").ap()
    b_in_s = nc.dram_tensor("b_in_s", [128, 24], F32, kind="ExternalInput").ap()
    w_conv_s = nc.dram_tensor("w_conv_s", [128, 16 * K], F32, kind="ExternalInput").ap()
    b_conv_s = nc.dram_tensor("b_conv_s", [128, 16], F32, kind="ExternalInput").ap()
    w_x_s = nc.dram_tensor("w_x_s", [9 * 128, DI], F32, kind="ExternalInput").ap()
    b_x_s = nc.dram_tensor("b_x_s", [128, 9], F32, kind="ExternalInput").ap()
    a_log_s = nc.dram_tensor("a_log_s", [128, NDT * N], F32, kind="ExternalInput").ap()
    d_s = nc.dram_tensor("d_s", [128, NDT], F32, kind="ExternalInput").ap()
    eye_in = nc.dram_tensor("eye_in", [128, 128], BF16, kind="ExternalInput").ap()
    xb_t = nc.dram_tensor("xb_t", [TT2, DM], F32, kind="ExternalInput").ap()
    w_out = nc.dram_tensor("w_out", [DM, DI], F32, kind="ExternalInput").ap()
    b_out_r = nc.dram_tensor("b_out_r", [1, DM], F32, kind="ExternalInput").ap()
    kern = nc.dram_tensor("kern", [128, 8, DM], BF16, kind="ExternalInput").ap()
    s_in = nc.dram_tensor("s_in", [1, 1], F32, kind="ExternalInput").ap()
    s_k = nc.dram_tensor("s_k", [1, 1], F32, kind="ExternalInput").ap()
    fbias = nc.dram_tensor("fbias", [1, DM], BF16, kind="ExternalInput").ap()
    out_t = nc.dram_tensor("out_t", [TT2, DM], F32, kind="ExternalOutput").ap()
    xi_h = nc.dram_tensor("xi_h", [128, NDT, T], BF16).ap()
    dl_h = nc.dram_tensor("dl_h", [128, NDT, T], F32).ap()
    rs_h = nc.dram_tensor("rs_h", [128, NDT, T], F32).ap()
    bc_h = nc.dram_tensor("bc_h", [32, T], BF16).ap()
    w_in_b = nc.dram_tensor("w_in_b", [24 * 128, DM], BF16)
    w_x_b = nc.dram_tensor("w_x_b", [9 * 128, DI], BF16)
    yg = nc.dram_tensor("yg", [128, NDT, T], BF16).ap()
    agin = nc.dram_tensor("agin", [128, NDT, TT2], BF16)
    agout = nc.dram_tensor("agout", [8, 128, NDT, TT2], BF16)
    selc = nc.dram_tensor("selc", [1, 4], mybir.dt.uint32, kind="ExternalInput").ap()
    dbg_t = {}
    for name, shape, dt in debug:
        dbg_t[name] = nc.dram_tensor("dbg_" + name, shape, dt, kind="ExternalOutput").ap()

    es = ExitStack()
    with es:
        c = Ctx(nc, es)
        sb = lambda n, s, d=F32: es.enter_context(nc.sbuf_tensor(n, s, d))
        eye = sb("eye", [128, 128], BF16)
        b_in_t = sb("b_in_t", [128, 24])
        b_conv_t = sb("b_conv_t", [128, 16])
        b_x_t = sb("b_x_t", [128, 9])
        wconv_t = sb("wconv_t", [128, 16 * K])
        a_t = sb("a_t", [128, NDT * N])
        d_t = sb("d_t", [128, NDT])
        ceps = sb("ceps", [128, 1])
        states = sb("states", [128, NDT * N])
        selc_sb = sb("selc_sb", [1, 4], mybir.dt.uint32)
        es_ab = es.enter_context(ExitStack())
        sba = lambda n, s, d=F32: es_ab.enter_context(nc.sbuf_tensor(n, s, d))
        diag = sba("diag", [128, 16 * K, 128], BF16)

        c.dma("sync", selc_sb[:], selc)
        if "l1eng" not in skip:
            c.op("vector", "memset", ceps[:], EPS)
            c.op("vector", "memset", states[:], 0.0)
        if "l1dma" not in skip:
            for dst, src in ((b_in_t, b_in_s), (b_conv_t, b_conv_s), (b_x_t, b_x_s),
                             (wconv_t, w_conv_s), (a_t, a_log_s), (d_t, d_s), (eye, eye_in)):
                c.dma("sync", dst[:], src)
        ld0 = c.cnt["LD"]
        if "l1eng" not in skip:
            c.wait("scalar", "LD", ld0)
            c.op("scalar", "activation", a_t[:], a_t[:], AF.Exp, sem="SE")
            c.wait("vector", "SE")
            c.op("vector", "tensor_scalar_mul", a_t[:], a_t[:], -1.0)
            c.wait("vector", "LD", ld0)
            for i in range(16 * K):
                c.op("vector", "tensor_scalar_mul", diag[:, i, :], eye[:], wconv_t[:, i:i + 1])
            c.op("vector", "nop", sem="VE")

        xtok = sba("xtok", [128, 1, 4, DM])
        scr2 = sba("scr2", [128, TCA])
        ssum = sba("ssum", [128, 2, 4])
        ssq = sba("ssq", [128, 2, 4])
        stat = sba("stat", [128, 2, 12])
        vtmp = sba("vtmp", [128, 4])
        xntok = sba("xntok", [128, 2, DM], BF16)
        xnT = sba("xnT", [128, 1, 8, TCA], BF16)
        xipre = sba("xipre", [128, 16, 3 + TCA], BF16)
        xichk = sba("xichk", [128, 1, 16, TCA], BF16)
        reschk = sba("reschk", [128, 1, TCA])
        dlchk = sba("dlchk", [128, 1, TCA])
        bcchk = sba("bcchk", [32, 1, TCA], BF16)
        winTs = sba("winTs", [128, 8, 2, 128], BF16)
        wxTs = sba("wxTs", [128, 16, 2, 128], BF16)
        bcB = sba("bcB", [128, N, TCS], BF16)
        bcC = sba("bcC", [128, N, TCS], BF16)
        dls = sba("dls", [128, 2, TCS])
        xis = sba("xis", [128, 2, TCS], BF16)
        rss = sba("rss", [128, 8, TCS])
        wt = sba("wt", [128, 2, TCS], BF16)
        ut = sba("ut", [128, 8, TCS], BF16)
        dat = sba("dat", [128, 2, TCS])
        dbt = sba("dbt", [128, 2, TCS], BF16)
        ht = sba("ht", [128, 2, TCS], BF16)
        tnc = sba("tnc", [128, 2, N, TCS], BF16)
        g1 = sba("g1", [128, 2, TCS])
        ygt = sba("ygt", [128, 2, TCS], BF16)
        sps = es_ab.enter_context(nc.psum_tensor("sps", [128, 2, TCA], F32))
        cps = es_ab.enter_context(nc.psum_tensor("cps", [128, 2, TCA], F32))
        pps = es_ab.enter_context(nc.psum_tensor("pps", [128, 2, TCA], F32))
        yps = es_ab.enter_context(nc.psum_tensor("yps", [128, 2, TCS], F32))

        datf = bass.AP(dat, 0, [[2 * TCS, 128], [1, 2 * TCS]])
        c.op("vector", "memset", xipre[:, :, 0:3], 0.0)
        c.op("vector", "nop", sem="VE")
        setup_ve = c.cnt["VE"]
        c.wait("scalar", "VE", setup_ve)

        ldx0 = 0
        if "ab" not in skip:
            ldx0 = c.dma("gpsimd", xtok[:, 0, :, :],
                         _ap(xb, 0, (DM, 128), (128 * DM, 4), (1, DM)), sem="XL")
        cast_in = [None] * 24
        cast_x = [None] * 9
        if "wprep" not in skip:
            cast_prev = [0, 0]
            tiles = [("in", ct) for ct in range(24)] + [("x", ct) for ct in range(9)]
            for i, (fam, ct) in enumerate(tiles):
                p = i % 2
                if cast_prev[p]:
                    c.wait("gpsimd", "WC%d" % p, cast_prev[p])
                src, dst = ((w_in_s, w_in_b.ap()) if fam == "in"
                            else (w_x_s, w_x_b.ap()))
                cast_prev[p] = c.dma("gpsimd", dst[ct * 128:(ct + 1) * 128, :],
                                     src[ct * 128:(ct + 1) * 128, :], sem="WC%d" % p)
                if fam == "in":
                    cast_in[ct] = ("WC%d" % p, cast_prev[p])
                else:
                    cast_x[ct] = ("WC%d" % p, cast_prev[p])
        for _sm in ("WI0", "WI1", "WX0", "WX1", "WC0", "WC1", "RS0",
                    "DS0", "BS", "XS", "YS0", "YS1", "SL0", "SL1",
                    "BCL", "XT"):
            c.sem(_sm)

        NCH = 0 if "ab" in skip else T // TCA
        do_scan = ("scan" not in skip) and NCH > 0
        xn_ve = [0]
        xr_mm = [0, 0]
        ev_ve = {}
        ev_se = {}
        xi_st = [0, 0]
        dl_st = [0]
        rs_st = [0]
        bc_st = [0]
        dl_list = [0] * 8
        rs_list = [0] * 8
        xnk_tr = [0, 0]
        dl_rec = {}
        wi_mm = [0, 0]
        wx_mm = [0, 0]
        wi_tr = [0, 0]
        wx_tr = [0, 0]
        pj_se = {}
        pj_guard = {"se": 0, "ve": 0}
        st_rec = {}
        sl_ld = [0, 0]
        wtu_done = [0, 0]
        dat_done = [0, 0]
        gate_done = [0] * NDT
        gate_slot = [0, 0]
        ymm_rec = [0, 0]
        ym_of = {}
        last_tts = [0, 0]
        bc_used = [0]
        bcl_v = [0]
        last_ygdma = [0, 0]
        core_done = {}

        def ab_ldx(tc):
            if tc == 0:
                return ldx0
            c.wait("sync", "VE", xn_ve[0])
            return c.dma("sync", xtok[:, 0, :, :],
                         _ap(xb, tc * TCA * DM, (DM, 128), (128 * DM, 4), (1, DM)),
                         sem="XL")

        def ab_head(tc, ldx):
            pb = tc % 2
            c.wait("scalar", "XL", ldx)
            c.wait("scalar", "VE", last_tts[0])
            c.wait("scalar", "VE", last_tts[1])
            for j in range(4):
                c.op("scalar", "activation", datf, xtok[:, 0, j, :], AF.Identity,
                     accum_out=ssum[:, pb, j:j + 1])
                c.op("scalar", "activation", datf, xtok[:, 0, j, :], AF.Square,
                     accum_out=ssq[:, pb, j:j + 1])
            st_se = c.op("scalar", "nop", sem="SE")
            c.wait("vector", "SE", st_se)
            c.op("vector", "tensor_scalar_mul", stat[:, pb, 0:4], ssum[:, pb, :],
                 -1.0 / DM)
            c.op("vector", "tensor_scalar_mul", stat[:, pb, 4:8], ssq[:, pb, :],
                 1.0 / DM)
            c.op("vector", "tensor_tensor", vtmp[:], stat[:, pb, 0:4],
                 stat[:, pb, 0:4], ALU.mult)
            vw = c.op("vector", "tensor_sub", stat[:, pb, 4:8], stat[:, pb, 4:8],
                      vtmp[:], sem="VE")
            c.wait("scalar", "VE", vw)
            c.op("scalar", "activation", stat[:, pb, 4:8], stat[:, pb, 4:8],
                 AF.Ln, bias=ceps[:, 0:1])
            iv_se = c.op("scalar", "activation", stat[:, pb, 8:12], stat[:, pb, 4:8],
                         AF.Exp, scale=-0.5, sem="SE")
            c.wait("vector", "SE", iv_se)
            c.wait("vector", "XL", ldx)
            c.wait("sync", "TE", xr_mm[(tc - 1) % 2])
            for j in range(4):
                sj = j % 2
                c.wait("vector", "XT", xnk_tr[sj])
                xv = c.op("vector", "tensor_scalar", xntok[:, sj, :], xtok[:, 0, j, :],
                          stat[:, pb, j:j + 1], stat[:, pb, 8 + j:9 + j],
                          ALU.add, ALU.mult, sem="VE")
                c.wait("sync", "VE", xv)
                xnk_tr[sj] = c.dma("sync", xnT[:, 0, :, j * 128:(j + 1) * 128],
                                   xntok[:, sj, :], transpose=True, sem="XT")
            xn_ve[0] = xv
            return xnk_tr[1]

        def scan_bc(j):
            rec = st_rec[j]
            c.wait("sync", "VE", bc_used[0])
            c.wait("sync", "BS", rec["bc"])
            for n in range(N):
                c.dma("sync", bcB[:, n, :],
                      _ap(bc_h, n * T + j * TCS, (0, 128), (1, TCS)), sem="BCL")
                bcl_v[0] = c.dma("sync", bcC[:, n, :],
                                 _ap(bc_h, (16 + n) * T + j * TCS, (0, 128), (1, TCS)),
                                 sem="BCL")

        def scan_loads(j, dt):
            pd = dt % 2
            sl = slice(j * TCS, (j + 1) * TCS)
            rec = st_rec[j]
            c.wait("sync", "VE", wtu_done[pd])
            c.wait("sync", "SE", dat_done[pd])
            c.wait("sync", "VE", gate_done[dt])
            c.wait("sync", "DS0", rec["dl"][dt])
            c.wait("sync", "XS", rec["xi"])
            c.wait("sync", "RS0", rec["rs"][dt])
            c.dma("sync", dls[:, pd, :], dl_h[:, dt, sl], sem="SL%d" % pd)
            c.dma("sync", xis[:, pd, :], xi_h[:, dt, sl], sem="SL%d" % pd)
            sl_ld[pd] = c.dma("sync", rss[:, dt, :], rs_h[:, dt, sl], sem="SL%d" % pd)

        def scan_pre(j):
            scan_bc(j)
            scan_loads(j, 0)
            scan_loads(j, 1)

        def scan_core(j, dt):
            pd = dt % 2
            c.wait("vector", "TE", ymm_rec[pd])
            c.wait("vector", "SL%d" % pd, sl_ld[pd])
            c.op("vector", "tensor_mul", wt[:, pd, :], dls[:, pd, :], xis[:, pd, :])
            wtu_done[pd] = c.op("vector", "tensor_scalar_mul", ut[:, dt, :],
                                xis[:, pd, :], d_t[:, dt:dt + 1], sem="VE")
            c.wait("scalar", "SL%d" % pd, sl_ld[pd])
            cd = 0
            da_se = 0
            for n in range(N):
                pn = n % 2
                idx = dt * N + n
                c.wait("scalar", "VE", last_tts[pn])
                da_se = c.op("scalar", "activation", dat[:, pn, :], dls[:, pd, :],
                             AF.Exp, scale=a_t[:, idx:idx + 1], sem="SE")
                c.wait("vector", "BCL", bcl_v[0])
                c.wait("vector", "SE", da_se)
                c.op("vector", "tensor_mul", dbt[:, pn, :], wt[:, pd, :], bcB[:, n, :])
                last_tts[pn] = c.op("vector", "tensor_tensor_scan", ht[:, pn, :],
                                    dat[:, pn, :], dbt[:, pn, :],
                                    states[:, idx:idx + 1], ALU.mult, ALU.add,
                                    sem="VE")
                c.op("vector", "tensor_copy", states[:, idx:idx + 1],
                     ht[:, pn, TCS - 1:TCS])
                cd = c.op("vector", "tensor_mul", tnc[:, pd, n, :],
                          ht[:, pn, :], bcC[:, n, :], sem="VE")
            dat_done[pd] = da_se
            core_done[(j, dt)] = cd
            if dt == NDT - 1:
                bc_used[0] = cd
            if 0 < dt < NDT - 2 + 1 and dt + 2 < NDT:
                scan_loads(j, dt + 2)
            elif dt == 0:
                scan_loads(j, 2)

        def scan_tail_t(j, dt):
            pd = dt % 2
            c.wait("tensor", "VE", core_done[(j, dt)])
            c.wait("tensor", "VE", gate_slot[pd])
            for n in range(N):
                c.op("tensor", "matmul", yps[:, pd, :], eye[:], tnc[:, pd, n, :],
                     start=(n == 0), stop=(n == N - 1))
            ym = c.op("tensor", "nop", sem="TE")
            ymm_rec[pd] = ym
            ym_of[(j, dt)] = ym

        def scan_tail_v(j, dt):
            pd = dt % 2
            sl = slice(j * TCS, (j + 1) * TCS)
            c.wait("vector", "TE", ym_of[(j, dt)])
            c.wait("vector", "YS%d" % pd, last_ygdma[pd])
            c.op("vector", "scalar_tensor_tensor", g1[:, pd, :], yps[:, pd, :],
                 1.0, ut[:, dt, :], ALU.mult, ALU.add)
            gv = c.op("vector", "tensor_mul", ygt[:, pd, :], g1[:, pd, :],
                      rss[:, dt, :], sem="VE")
            gate_done[dt] = gv
            gate_slot[pd] = gv
            c.wait("gpsimd", "VE", gv)
            last_ygdma[pd] = c.dma("gpsimd", yg[:, dt, sl], ygt[:, pd, :],
                                   sem="YS%d" % pd)

        slot_t = [(0, 0), (0, 1), (1, 0), (1, 1)]

        def ab_mid(tc, trx):
            pb = tc % 2
            st_rec[tc] = {"xi": 0, "dl": [0] * 8, "rs": [0] * 8, "bc": 0}
            c.wait("tensor", "XT", trx)
            for ct in range(24):
                pp = ct % 2
                c.wait("sync", "TE", wi_mm[pp])
                if tc == 0 and cast_in[ct] is not None:
                    c.wait("sync", cast_in[ct][0], cast_in[ct][1])
                wi_tr[pp] = c.dma("sync", winTs[:, :, pp, :],
                                  w_in_b.ap()[ct * 128:(ct + 1) * 128, :],
                                  transpose=True, sem="WI%d" % pp)
                c.wait("tensor", "WI%d" % pp, wi_tr[pp])
                c.wait("tensor", "VE", ev_ve.get(("xr", pp), 0))
                c.wait("tensor", "SE", ev_se.get(("xr", pp), 0))
                for kc in range(8):
                    c.op("tensor", "matmul", sps[:, pp, :], winTs[:, kc, pp, :],
                         xnT[:, 0, kc, :], start=(kc == 0), stop=(kc == 7))
                mm = c.op("tensor", "nop", sem="TE")
                wi_mm[pp] = mm
                if ct < 16:
                    c.wait("vector", "TE", mm)
                    ev_ve[("xr", pp)] = c.op(
                        "vector", "tensor_scalar_add", xipre[:, ct, 3:3 + TCA],
                        sps[:, pp, :], b_in_t[:, ct:ct + 1], sem="VE")
                else:
                    ct8 = ct - 16
                    c.wait("scalar", "TE", mm)
                    c.wait("scalar", "RS0", rs_st[0])
                    ev_se[("xr", pp)] = c.op(
                        "scalar", "activation", reschk[:, 0, :], sps[:, pp, :],
                        AF.Silu, bias=b_in_t[:, ct:ct + 1], sem="SE")
                    c.wait("gpsimd", "SE", ev_se[("xr", pp)])
                    rs_st[0] = c.dma("gpsimd",
                                     rs_h[:, ct8, tc * TCA:(tc + 1) * TCA],
                                     reschk[:, 0, :], sem="RS0")
                    rs_list[ct8] = rs_st[0]
            xr_mm[pb] = c.cnt["TE"]
            st_rec[tc]["rs"] = list(rs_list)
            c.wait("tensor", "VE", ev_ve[("xr", 0)])
            c.wait("tensor", "VE", ev_ve[("xr", 1)])
            c.wait("scalar", "XS", xi_st[pb])
            c.wait("tensor", "VE", setup_ve)
            c.wait("tensor", "SE", pj_guard["se"])
            c.wait("tensor", "VE", pj_guard["ve"])
            for dt in range(16):
                pp = dt % 2
                c.wait("tensor", "SE", ev_se.get(("cv", pp), 0))
                for k in range(K):
                    c.op("tensor", "matmul", cps[:, pp, :], diag[:, dt * K + k, :],
                         xipre[:, dt, k:k + TCA], start=(k == 0), stop=(k == K - 1))
                mm = c.op("tensor", "nop", sem="TE")
                c.wait("scalar", "TE", mm)
                ev_se[("cv", pp)] = c.op(
                    "scalar", "activation", xichk[:, 0, dt, :], cps[:, pp, :],
                    AF.Silu, bias=b_conv_t[:, dt:dt + 1], sem="SE")
            conv_mm = c.cnt["TE"]
            c.wait("gpsimd", "SE", ev_se[("cv", 0)])
            c.wait("gpsimd", "SE", ev_se[("cv", 1)])
            xi_st[pb] = c.dma("gpsimd", xi_h[:, :, tc * TCA:(tc + 1) * TCA],
                              xichk[:, 0, 0:NDT, :], sem="XS")
            st_rec[tc]["xi"] = xi_st[pb]
            c.wait("tensor", "SE", ev_se[("cv", 0)])
            c.wait("tensor", "SE", ev_se[("cv", 1)])
            pj_mm = {}
            for ct in range(9):
                pw = ct % 2
                sN = ct % 4
                use_pps, si = slot_t[sN]
                tns = pps if use_pps else cps
                c.wait("sync", "TE", wx_mm[pw])
                if tc == 0 and cast_x[ct] is not None:
                    c.wait("sync", cast_x[ct][0], cast_x[ct][1])
                wx_tr[pw] = c.dma("sync", wxTs[:, :, pw, :],
                                  w_x_b.ap()[ct * 128:(ct + 1) * 128, :],
                                  transpose=True, sem="WX%d" % pw)
                c.wait("tensor", "WX%d" % pw, wx_tr[pw])
                c.wait("tensor", "SE", pj_se.get(sN, 0))
                if sN == 0:
                    c.wait("tensor", "VE", pj_guard["ve"])
                for kc in range(16):
                    c.op("tensor", "matmul", tns[:, si, :],
                         wxTs[:, kc, pw, :], xichk[:, 0, kc, :],
                         start=(kc == 0), stop=(kc == 15))
                mm = c.op("tensor", "nop", sem="TE")
                wx_mm[pw] = mm
                pj_mm[ct] = (mm, tns, si)
                if ct < 8:
                    c.wait("scalar", "TE", mm)
                    c.wait("scalar", "DS0", dl_st[0])
                    c.op("scalar", "activation", scr2[:], tns[:, si, :],
                         AF.Exp, bias=b_x_t[:, ct:ct + 1])
                    pj_se[sN] = c.op(
                        "scalar", "activation", dlchk[:, 0, :], scr2[:],
                        AF.Ln, bias=1.0, sem="SE")
                    c.wait("gpsimd", "SE", pj_se[sN])
                    dl_st[0] = c.dma("gpsimd",
                                     dl_h[:, ct, tc * TCA:(tc + 1) * TCA],
                                     dlchk[:, 0, :], sem="DS0")
                    dl_list[ct] = dl_st[0]
            pj_guard["se"] = max(pj_se.get(0, 0), pj_se.get(1, 0))
            st_rec[tc]["dl"] = list(dl_list)
            return conv_mm, pj_mm[8][0]

        def ab_bc_halo(tc, conv_mm, bc_mm):
            c.wait("vector", "TE", bc_mm)
            c.wait("vector", "BS", bc_st[0])
            bv = c.op("vector", "tensor_scalar_add", bcchk[:, 0, :],
                      cps[0:32, 0, :], b_x_t[0:32, 8:9], sem="VE")
            pj_guard["ve"] = bv
            c.wait("gpsimd", "VE", bv)
            bc_st[0] = c.dma("gpsimd", bc_h[:, tc * TCA:(tc + 1) * TCA],
                             bcchk[:, 0, :], sem="BS")
            st_rec[tc]["bc"] = bc_st[0]
            c.wait("vector", "TE", conv_mm)
            c.op("vector", "tensor_copy", xipre[:, :, 0:3],
                 xipre[:, :, TCA:TCA + 3], sem="VE")

        for k in range(NCH + (1 if do_scan else 0)):
            j = k - 1
            em_ab = k < NCH
            em_sc = do_scan and 0 <= j < NCH
            if em_ab:
                ldx = ab_ldx(k)
            if em_sc:
                scan_pre(j)
            if em_ab:
                trx = ab_head(k, ldx)
            if em_sc:
                scan_core(j, 0)
                scan_core(j, 1)
                scan_tail_t(j, 0)
            if em_ab:
                conv_mm, bc_mm = ab_mid(k, trx)
            if em_sc:
                scan_core(j, 2)
                scan_tail_t(j, 1)
                scan_tail_v(j, 0)
                if em_ab:
                    ab_bc_halo(k, conv_mm, bc_mm)
                for dt in range(3, NDT):
                    scan_core(j, dt)
                    scan_tail_t(j, dt - 1)
                    scan_tail_v(j, dt - 2)
                scan_tail_t(j, NDT - 1)
                scan_tail_v(j, NDT - 2)
                scan_tail_v(j, NDT - 1)
            elif em_ab:
                ab_bc_halo(k, conv_mm, bc_mm)

        for sm in ("RS0", "DS0", "BS", "XS"):
            c.wait("gpsimd", sm)
        c.wait("sync", "LD")
        c.wait("gpsimd", "YS0")
        c.wait("gpsimd", "YS1")
        c.sem("AGC")
        c.cnt["AGC"] += 16
        agc_v = c.cnt["AGC"]

        def _agin_copy(g, _selt=None):
            if no_branch:
                g.dma_start(out=agin.ap(), in_=yg[:, :, TT2:2 * TT2]).then_inc(
                    c.sems["AGC"], 16)
                return
            with g.register() as r:
                g.load(r, selc_sb[0:1, 0:1])
                with g.If_eq(r, 0):
                    g.dma_start(out=agin.ap(), in_=yg[:, :, TT2:2 * TT2]).then_inc(
                        c.sems["AGC"], 16)
                with g.Else():
                    g.dma_start(out=agin.ap(), in_=yg[:, :, 0:TT2]).then_inc(
                        c.sems["AGC"], 16)
        c.em("gpsimd", _agin_copy)
        c.wait("gpsimd", "AGC", agc_v)
        if no_cc:
            c.sem("CC")
            c.cnt["CC"] += 1
            c.em("gpsimd", lambda e: e.sem_inc(c.sems["CC"], 1))
        else:
            c.sem("CC")
            c.cnt["CC"] += 1
            c.em("gpsimd", lambda e: e.collective_compute(
                "AllGather", ALU.bypass,
                replica_groups=[list(range(8))],
                ins=[agin.ap().opt()], outs=[agout.ap().opt()]).then_inc(c.sems["CC"], 1))
        c.barrier()
        es_ab.close()

        ygs = sb("ygs", [128, 16, TT2], BF16)
        woT = sb("woT", [128, 16, DM], BF16)
        kq = sb("kq", [128, 8, DM], BF16)
        bout_bc = sb("bout_bc", [128, DM])
        fb_bc = sb("fb_bc", [128, DM], BF16)
        sinv = sb("sinv", [128, 3])
        scr = sb("scr", [128, DM])
        scrq = sb("scrq", [128, 4, DM])
        msk = sb("msk", [128, DM], U8)
        qf = sb("qf", [128, DM])
        reck = sb("reck", [128, 1])
        reck2 = sb("reck2", [128, 1])
        ksrc = sb("ksrc", [128, DM])
        es_set = es.enter_context(ExitStack())
        sbs = lambda n, s2, d=F32: es_set.enter_context(nc.sbuf_tensor(n, s2, d))

        kbf = sbs("kbf", [128, 8, DM], BF16)
        c.sem("L2S")
        if "l2dma" not in skip:
            c.dma("sync", bout_bc[:], _ap(b_out_r, 0, (0, 128), (1, DM)), sem="L2S")
            c.dma("sync", fb_bc[:], _ap(fbias, 0, (0, 128), (1, DM)), sem="L2S")
            c.dma("sync", sinv[:, 0:1], _ap(s_in, 0, (0, 128), (1, 1)), sem="L2S")
            c.dma("sync", sinv[:, 1:2], _ap(s_k, 0, (0, 128), (1, 1)), sem="L2S")
            c.dma("sync", reck[:], _ap(s_k, 0, (0, 128), (1, 1)), sem="L2S")
            c.dma("sync", kbf[:], kern, sem="L2S")
        ld_all = c.cnt["L2S"]
        stg_f = sbs("stg2_f", [128, 2, DI])
        stg_b = sbs("stg2_b", [128, 2, DI], BF16)
        tr_done = [0, 0]
        for ct in range(0 if "woprep" in skip else 8):
            p = ct % 2
            if tr_done[p]:
                c.wait("sync", "WT%d" % p, tr_done[p])
            ld = c.dma("sync", stg_f[:, p, :], w_out[ct * 128:(ct + 1) * 128, :],
                       sem="WL%d" % p)
            c.wait("vector", "WL%d" % p, ld)
            c.wait("vector", "WT%d" % p, tr_done[p])
            cd = c.op("vector", "tensor_copy", stg_b[:, p, :], stg_f[:, p, :], sem="VE")
            c.wait("sync", "VE", cd)
            tr_done[p] = c.dma("sync", woT[:, :, ct * 128:(ct + 1) * 128],
                               stg_b[:, p, :], transpose=True, sem="WT%d" % p)
        if "l2eng" not in skip:
            c.wait("vector", "L2S", ld_all)
            c.op("vector", "tensor_mul", sinv[:, 1:2], sinv[:, 0:1], sinv[:, 1:2])
            c.op("vector", "reciprocal", sinv[:, 2:3], sinv[:, 0:1], sem="VE")
            c.op("vector", "reciprocal", reck2[:], reck[:], sem="VE")

        def quant_tile(dst_bf, src_f32, width):
            t1, t2, hi = scrq[:, 0, 0:width], scrq[:, 1, 0:width], scrq[:, 2, 0:width]
            sq = scrq[:, 3, 0:width]
            c.op("vector", "tensor_scalar_mul", t1, src_f32, VC1)
            c.op("vector", "tensor_sub", t2, t1, src_f32)
            c.op("vector", "tensor_sub", hi, t1, t2)
            c.op("vector", "tensor_scalar", qf[:, 0:width], src_f32, MAGIC, MAGIC,
                 ALU.add, ALU.subtract)
            c.op("vector", "scalar_tensor_tensor", sq, src_f32, 1.0, src_f32,
                 ALU.mult, ALU.mult)
            c.op("vector", "tensor_single_scalar", msk[:, 0:width], sq,
                 float(2.0 ** -10), ALU.is_ge)
            c.op("vector", "copy_predicated", qf[:, 0:width], msk[:, 0:width], hi)
            c.op("vector", "tensor_scalar", dst_bf, qf[:, 0:width], 448.0, -448.0,
                 ALU.min, ALU.max)

        for kc in range(0 if "kq" in skip else 8):
            c.op("vector", "tensor_copy", scr[:], kbf[:, kc, :])
            c.op("vector", "tensor_scalar_mul", ksrc[:], scr[:], reck2[:, 0:1])
            quant_tile(kq[:, kc, :], ksrc[:], DM)
        c.op("vector", "nop", sem="VE")
        c.sem("YGL")
        c.cnt["YGL"] += 32
        ygl = c.cnt["YGL"]

        def _ygs_loads(g):
            if no_branch:
                g.dma_start(out=ygs[:, 0:8, :], in_=yg[:, :, 0:TT2]).then_inc(c.sems["YGL"], 16)
                g.dma_start(out=ygs[:, 8:16, :], in_=agout.ap()[0]).then_inc(c.sems["YGL"], 16)
                return
            with g.register() as r:
                g.load(r, selc_sb[0:1, 0:1])
                with g.If_eq(r, 0):
                    g.dma_start(out=ygs[:, 0:8, :],
                                in_=yg[:, :, 0:TT2]).then_inc(c.sems["YGL"], 16)
                with g.Else():
                    g.dma_start(out=ygs[:, 8:16, :],
                                in_=yg[:, :, TT2:2 * TT2]).then_inc(c.sems["YGL"], 16)
            with g.register() as r2:
                g.load(r2, selc_sb[0:1, 1:2])
                for pc in range(8):
                    dst = ygs[:, 0:8, :] if pc % 2 == 0 else ygs[:, 8:16, :]
                    with g.If_eq(r2, pc):
                        g.dma_start(out=dst,
                                    in_=agout.ap()[pc]).then_inc(c.sems["YGL"], 16)
        c.wait("sync", "CC")
        c.em("sync", _ygs_loads)
        c.wait("sync", "L2S", ld_all)
        c.wait("sync", "WT0")
        c.wait("sync", "WT1")
        c.wait("sync", "YGL", ygl)
        c.barrier()
        es_set.close()

        xts = sb("xts", [128, 2, DM])
        x2t = sb("x2t", [128, 2, DM])
        x2n = sb("x2n", [128, 2, DM], BF16)
        x2nf = sb("x2nf", [128, DM])
        qb = sb("qb", [128, DM], BF16)
        x2qT = sb("x2qT", [128, 2, 8, 128], BF16)
        ffnb = sb("ffnb", [128, 2, DM], BF16)
        outt = sb("outt", [128, 2, DM])
        stt = sb("stt", [128, 8])
        aps = es.enter_context(nc.psum_tensor("aps", [128, 2, DM], F32))
        fps = es.enter_context(nc.psum_tensor("fps", [128, 2, DM], F32))

        prev = {}
        for tt in range(0 if "l2" in skip else NTT):
            pb = tt % 2
            tsl = slice(tt * 128, (tt + 1) * 128)
            c.wait("sync", "VE", prev.get(("x2", tt - 2), 0))
            ldx = c.dma("sync", xts[:, pb, :],
                        _ap(xb_t, tt * 128 * DM, (DM, 128), (1, DM)), sem="XL%d" % pb)
            c.wait("tensor", "VE", prev.get(("aev", pb), 0))
            for kc in range(16):
                for hh in range(2):
                    c.op("tensor", "matmul", aps[:, pb, hh * 512:(hh + 1) * 512],
                         ygs[:, kc, tsl], woT[:, kc, hh * 512:(hh + 1) * 512],
                         start=(kc == 0), stop=(kc == 15))
            amm = c.op("tensor", "nop", sem="TE")
            c.wait("vector", "TE", amm)
            c.wait("vector", "XL%d" % pb, ldx)
            c.op("vector", "scalar_tensor_tensor", scr[:], aps[:, pb, :], 1.0,
                 bout_bc[:], ALU.mult, ALU.add)
            x2ve = c.op("vector", "tensor_add", x2t[:, pb, :], scr[:], xts[:, pb, :],
                        sem="VE")
            prev[("x2", tt)] = x2ve
            c.wait("scalar", "VE", x2ve)
            c.op("scalar", "activation", scr[:], x2t[:, pb, :], AF.Identity,
                 accum_out=stt[:, 0:1])
            sse = c.op("scalar", "activation", scr[:], x2t[:, pb, :], AF.Square,
                       accum_out=stt[:, 1:2], sem="SE")
            c.wait("vector", "SE", sse)
            c.op("vector", "tensor_scalar_mul", stt[:, 2:3], stt[:, 0:1], -1.0 / DM)
            c.op("vector", "tensor_scalar_mul", stt[:, 3:4], stt[:, 1:2], 1.0 / DM)
            c.op("vector", "tensor_tensor", stt[:, 4:5], stt[:, 2:3], stt[:, 2:3], ALU.mult)
            vve = c.op("vector", "tensor_sub", stt[:, 3:4], stt[:, 3:4], stt[:, 4:5],
                       sem="VE")
            c.wait("scalar", "VE", vve)
            c.op("scalar", "activation", stt[:, 3:4], stt[:, 3:4], AF.Ln, bias=ceps[:, 0:1])
            ivse = c.op("scalar", "activation", stt[:, 5:6], stt[:, 3:4], AF.Exp,
                        scale=-0.5, sem="SE")
            c.wait("vector", "SE", ivse)
            c.op("vector", "tensor_scalar", x2n[:, pb, :], x2t[:, pb, :],
                 stt[:, 2:3], stt[:, 5:6], ALU.add, ALU.mult)
            c.op("vector", "tensor_copy", x2nf[:], x2n[:, pb, :])
            c.op("vector", "tensor_scalar_mul", x2nf[:], x2nf[:], sinv[:, 2:3])
            quant_tile(qb[:], x2nf[:], DM)
            qve = c.op("vector", "nop", sem="VE")
            c.wait("sync", "VE", qve)
            c.wait("sync", "TE", prev.get(("fmm", pb), 0))
            trq = c.dma("sync", x2qT[:, pb, :, :], qb[:], transpose=True, sem="XT%d" % pb)
            c.wait("tensor", "XT%d" % pb, trq)
            c.wait("tensor", "VE", prev.get(("fev", pb), 0))
            for kc in range(8):
                for hh in range(2):
                    c.op("tensor", "matmul", fps[:, pb, hh * 512:(hh + 1) * 512],
                         x2qT[:, pb, kc, :], kq[:, kc, hh * 512:(hh + 1) * 512],
                         start=(kc == 0), stop=(kc == 7))
            fmm = c.op("tensor", "nop", sem="TE")
            prev[("fmm", pb)] = fmm
            c.wait("vector", "TE", fmm)
            c.op("vector", "scalar_tensor_tensor", ffnb[:, pb, :], fps[:, pb, :],
                 sinv[:, 1:2], fb_bc[:], ALU.mult, ALU.add)
            c.wait("vector", "OS%d" % pb, prev.get(("odma", tt - 2), 0))
            fev = c.op("vector", "tensor_add", outt[:, pb, :], x2t[:, pb, :],
                       ffnb[:, pb, :], sem="VE")
            prev[("fev", pb)] = fev
            prev[("aev", pb)] = fev
            c.wait("gpsimd", "VE", fev)
            prev[("odma", tt)] = c.dma("gpsimd", out_t[tsl, :], outt[:, pb, :],
                                       sem="OS%d" % pb)
        c.wait("gpsimd", "OS0")
        c.wait("gpsimd", "OS1")
        with allow_ici_all_to_all():
            c.replay(es.enter_context(nc.Block()))
    return nc


def _fused_inmaps(inputs):
    l1 = _l1_inmaps(inputs)
    x = np.ascontiguousarray(inputs["L0_attn_x"])
    W_out = np.ascontiguousarray(inputs["L0_attn_W_out"])
    b_out = np.ascontiguousarray(inputs["L0_attn_b_out"]).reshape(1, DM)
    kern = np.ascontiguousarray(inputs["L0_ffn_kernel"])
    s_in = np.ascontiguousarray(inputs["L0_ffn_input_scale"]).reshape(1, 1)
    s_k = np.ascontiguousarray(inputs["L0_ffn_kernel_scale"]).reshape(1, 1)
    fb = np.ascontiguousarray(inputs["L0_ffn_bias"]).reshape(1, DM)
    kern_r = np.ascontiguousarray(kern.reshape(8, 128, DM).transpose(1, 0, 2))
    maps = []
    for cc in range(8):
        b, th = cc // 2, cc % 2
        m = dict(l1[cc])
        m.update(dict(xb_t=np.ascontiguousarray(x[b, th * 2048:(th + 1) * 2048]),
                      w_out=W_out, b_out_r=b_out, kern=kern_r,
                      s_in=s_in, s_k=s_k, fbias=fb,
                      selc=np.array([[th, cc ^ 1, 0, 0]], dtype=np.uint32)))
        maps.append(m)
    return maps


def kernel(**inputs):
    nc = build_fused()
    r = run_bass_kernel_spmd(nc, _fused_inmaps(inputs), core_ids=list(range(8)))
    out = np.empty((B, T, DM), np.float32)
    for cc in range(8):
        b, th = cc // 2, cc % 2
        out[b, th * 2048:(th + 1) * 2048] = r.results[cc]["out_t"]
    return out


def kernel_2launch(**inputs):
    nc1 = build_l1()
    r1 = run_bass_kernel_spmd(nc1, _l1_inmaps(inputs), core_ids=list(range(8)))
    yg_l1 = [r1.results[cc]["yg"] for cc in range(8)]
    nc2 = build_l2()
    r2 = run_bass_kernel_spmd(nc2, _l2_inmaps(inputs, yg_l1), core_ids=list(range(8)))
    out = np.empty((B, T, DM), np.float32)
    for cc in range(8):
        b, th = cc // 2, cc % 2
        out[b, th * 2048:(th + 1) * 2048] = r2.results[cc]["out_t"]
    return out


# version 12 (final)
# speedup vs baseline: 1.0812x; 1.0613x over previous
import numpy as np
import ml_dtypes
from contextlib import ExitStack

import concourse.bass as bass
import concourse.mybir as mybir
from concourse.bass_utils import run_bass_kernel_spmd

F32 = mybir.dt.float32
BF16 = mybir.dt.bfloat16
U8 = mybir.dt.uint8
AF = mybir.ActivationFunctionType
ALU = mybir.AluOpType

B, T, DM, DI, N, K = 4, 4096, 1024, 2048, 16, 4
NDT = 8
TCA = 512
TCS = 512
EPS = 1e-5
VC1 = float(2.0 ** 20 + 1)
MAGIC = float(3 * 2 ** 13)


class Ctx:
    ENGS = ("sync", "vector", "scalar", "tensor", "gpsimd")

    def __init__(self, nc, es):
        self.nc = nc
        self._es = es
        self.ops = {e: [] for e in self.ENGS}
        self.sems = {}
        self.cnt = {}
        for s in ("LD", "TE", "VE", "SE", "SY"):
            self.sem(s)

    def sem(self, name):
        if name not in self.sems:
            self.sems[name] = self._es.enter_context(self.nc.semaphore(name))
            self.cnt[name] = 0
        return self.sems[name]

    def em(self, eng, fn):
        self.ops[eng].append(fn)

    def wait(self, eng, sem, val=None):
        v = self.cnt[sem] if val is None else val
        if v > 0:
            self.em(eng, lambda e, s=self.sems[sem], v=v: e.wait_ge(s, v))
        return v

    def dma(self, eng, out, in_, sem="LD", **kw):
        self.sem(sem)
        self.cnt[sem] += 16
        s = self.sems[sem]
        self.em(eng, lambda e, o=out, i=in_, kw=kw, s=s:
                e.dma_start(out=o, in_=i, **kw).then_inc(s, 16))
        return self.cnt[sem]

    def op(self, eng, meth, *a, sem=None, **kw):
        if meth != "nop":
            self.em(eng, lambda e, a=a, kw=kw: getattr(e, meth)(*a, **kw))
        if eng in ("vector", "scalar") or (eng == "tensor" and sem is not None):
            if sem is not None:
                self.sem(sem)
                self.cnt[sem] += 1
                s = self.sems[sem]
                self.em(eng, lambda e, s=s: e.drain().then_inc(s, 1))
                return self.cnt[sem]
            self.em(eng, lambda e: e.drain())
            return None
        if sem is not None:
            self.sem(sem)
            self.cnt[sem] += 1
            s = self.sems[sem]
            self.em(eng, lambda e, s=s: e.nop().then_inc(s, 1))
            return self.cnt[sem]
        return None

    def barrier(self):
        self.cnt["SY"] += 5
        tgt = self.cnt["SY"]
        for eng in self.ENGS:
            if eng in ("vector", "scalar", "tensor", "gpsimd"):
                self.em(eng, lambda e: e.drain())
            self.em(eng, lambda e, s=self.sems["SY"]: e.sem_inc(s, 1))
        for eng in self.ENGS:
            self.em(eng, lambda e, s=self.sems["SY"], v=tgt: e.wait_ge(s, v))

    def replay(self, blk):
        for eng in self.ENGS:
            ops = self.ops[eng]
            getattr(blk, eng)(lambda e, ops=ops: [f(e) for f in ops])


def _ap(t, off, *dims):
    return bass.AP(t.tensor if isinstance(t, bass.AP) else t, off, [list(d) for d in dims])



def build_l1(debug=()):
    nc = bass.Bass("TRN2", target_bir_lowering=False, debug=False)
    xb = nc.dram_tensor("xb", [T, DM], F32, kind="ExternalInput").ap()
    w_in_s = nc.dram_tensor("w_in_s", [24 * 128, DM], F32, kind="ExternalInput").ap()
    b_in_s = nc.dram_tensor("b_in_s", [128, 24], F32, kind="ExternalInput").ap()
    w_conv_s = nc.dram_tensor("w_conv_s", [128, 16 * K], F32, kind="ExternalInput").ap()
    b_conv_s = nc.dram_tensor("b_conv_s", [128, 16], F32, kind="ExternalInput").ap()
    w_x_s = nc.dram_tensor("w_x_s", [9 * 128, DI], F32, kind="ExternalInput").ap()
    b_x_s = nc.dram_tensor("b_x_s", [128, 9], F32, kind="ExternalInput").ap()
    a_log_s = nc.dram_tensor("a_log_s", [128, NDT * N], F32, kind="ExternalInput").ap()
    d_s = nc.dram_tensor("d_s", [128, NDT], F32, kind="ExternalInput").ap()
    eye_in = nc.dram_tensor("eye_in", [128, 128], BF16, kind="ExternalInput").ap()
    yg = nc.dram_tensor("yg", [128, NDT, T], BF16, kind="ExternalOutput").ap()
    dbg_t = {}
    for name, shape, dt in debug:
        dbg_t[name] = nc.dram_tensor("dbg_" + name, shape, dt, kind="ExternalOutput").ap()
    xi_h = nc.dram_tensor("xi_h", [128, NDT, T], BF16).ap()
    dl_h = nc.dram_tensor("dl_h", [128, NDT, T], F32).ap()
    rs_h = nc.dram_tensor("rs_h", [128, NDT, T], F32).ap()
    bc_h = nc.dram_tensor("bc_h", [32, T], BF16).ap()

    es = ExitStack()
    with es:
        c = Ctx(nc, es)
        sb = lambda n, s, d=F32: es.enter_context(nc.sbuf_tensor(n, s, d))
        eye = sb("eye", [128, 128], BF16)
        b_in_t = sb("b_in_t", [128, 24])
        b_conv_t = sb("b_conv_t", [128, 16])
        b_x_t = sb("b_x_t", [128, 9])
        wconv_t = sb("wconv_t", [128, 16 * K])
        a_t = sb("a_t", [128, NDT * N])
        d_t = sb("d_t", [128, NDT])
        ceps = sb("ceps", [128, 1])
        states = sb("states", [128, NDT * N])
        es_ab = es.enter_context(ExitStack())
        sba = lambda n, s, d=F32: es_ab.enter_context(nc.sbuf_tensor(n, s, d))
        winT = sba("winT", [128, 8, 24 * 128], BF16)
        wxT = sba("wxT", [128, 16, 9 * 128], BF16)
        diag = sba("diag", [128, 16 * K, 128], BF16)

        c.op("vector", "memset", ceps[:], EPS)
        c.op("vector", "memset", states[:], 0.0)
        for dst, src in ((b_in_t, b_in_s), (b_conv_t, b_conv_s), (b_x_t, b_x_s),
                         (wconv_t, w_conv_s), (a_t, a_log_s), (d_t, d_s), (eye, eye_in)):
            c.dma("sync", dst[:], src)
        ld0 = c.cnt["LD"]
        c.wait("scalar", "LD", ld0)
        c.op("scalar", "activation", a_t[:], a_t[:], AF.Exp, sem="SE")
        c.wait("vector", "SE")
        c.op("vector", "tensor_scalar_mul", a_t[:], a_t[:], -1.0)
        c.wait("vector", "LD", ld0)
        for i in range(16 * K):
            c.op("vector", "tensor_scalar_mul", diag[:, i, :], eye[:], wconv_t[:, i:i + 1])
        c.op("vector", "nop", sem="VE")

        es_stg = ExitStack()
        stg_f = es_stg.enter_context(nc.sbuf_tensor("stg_f", [128, 2, DI], F32))
        stg_b = es_stg.enter_context(nc.sbuf_tensor("stg_b", [128, 2, DI], BF16))

        tr_done = [0, 0]

        def prep_w(src, n_ct, kdim, dstT):
            for ct in range(n_ct):
                p = ct % 2
                if tr_done[p]:
                    c.wait("sync", "WT%d" % p, tr_done[p])
                ld = c.dma("sync", stg_f[:, p, 0:kdim], src[ct * 128:(ct + 1) * 128, :],
                           sem="WL%d" % p)
                c.wait("vector", "WL%d" % p, ld)
                c.wait("vector", "WT%d" % p, tr_done[p])
                cd = c.op("vector", "tensor_copy", stg_b[:, p, 0:kdim],
                          stg_f[:, p, 0:kdim], sem="VE")
                c.wait("sync", "VE", cd)
                tr_done[p] = c.dma("sync", dstT[:, :, ct * 128:(ct + 1) * 128],
                                   stg_b[:, p, 0:kdim], transpose=True, sem="WT%d" % p)
        prep_w(w_in_s, 24, DM, winT)
        prep_w(w_x_s, 9, DI, wxT)
        c.wait("sync", "WT0")
        c.wait("sync", "WT1")
        c.wait("sync", "LD")
        c.barrier()
        es_stg.close()

        xtok = sba("xtok", [128, 1, 4, DM])
        scr = sba("scr", [128, DM])
        ssum = sba("ssum", [128, 1, 4])
        ssq = sba("ssq", [128, 1, 4])
        stat = sba("stat", [128, 1, 12])
        vtmp = sba("vtmp", [128, 4])
        xntok = sba("xntok", [128, 1, 4, DM], BF16)
        xnT = sba("xnT", [128, 1, 8, TCA], BF16)
        xipre = sba("xipre", [128, 16, 3 + TCA], BF16)
        xichk = sba("xichk", [128, 1, 16, TCA], BF16)
        reschk = sba("reschk", [128, 1, NDT, TCA])
        dlchk = sba("dlchk", [128, 1, NDT, TCA])
        bcchk = sba("bcchk", [32, 1, TCA], BF16)
        sps = es_ab.enter_context(nc.psum_tensor("sps", [128, 2, TCA], F32))
        cps = es_ab.enter_context(nc.psum_tensor("cps", [128, 2, TCA], F32))
        pps = es_ab.enter_context(nc.psum_tensor("pps", [128, 2, TCA], F32))

        c.op("vector", "memset", xipre[:, :, 0:3], 0.0)
        c.op("vector", "nop", sem="VE")

        for _sm in ("RS", "DS", "BS", "XS", "YS0", "YS1", "OS0", "OS1"):
            c.sem(_sm)
        NCH = 0 if "ab" in skip else T // TCA
        xn_ve = [0, 0]
        xr_mm = [0, 0]
        ev_ve = {}
        ev_se = {}
        conv_mm = 0
        xi_st = [0, 0]
        dl_st = [0, 0]
        rs_st = [0, 0]
        bc_st = [0, 0]
        for tc in range(NCH):
            pb = 0
            xin = _ap(xb, tc * TCA * DM, (DM, 128), (128 * DM, 4), (1, DM))
            c.wait("sync", "VE", xn_ve[pb])
            ldx = c.dma("sync", xtok[:, pb, :, :], xin, sem="XL")
            c.wait("scalar", "XL", ldx)
            for j in range(4):
                c.op("scalar", "activation", scr[:], xtok[:, pb, j, :], AF.Identity,
                     accum_out=ssum[:, pb, j:j + 1])
                c.op("scalar", "activation", scr[:], xtok[:, pb, j, :], AF.Square,
                     accum_out=ssq[:, pb, j:j + 1])
            st_se = c.op("scalar", "nop", sem="SE")
            c.wait("vector", "SE", st_se)
            c.op("vector", "tensor_scalar_mul", stat[:, pb, 0:4], ssum[:, pb, :], -1.0 / DM)
            c.op("vector", "tensor_scalar_mul", stat[:, pb, 4:8], ssq[:, pb, :], 1.0 / DM)
            c.op("vector", "tensor_tensor", vtmp[:], stat[:, pb, 0:4],
                 stat[:, pb, 0:4], ALU.mult)
            vw = c.op("vector", "tensor_sub", stat[:, pb, 4:8], stat[:, pb, 4:8],
                      vtmp[:], sem="VE")
            c.wait("scalar", "VE", vw)
            c.op("scalar", "activation", stat[:, pb, 4:8], stat[:, pb, 4:8],
                 AF.Ln, bias=ceps[:, 0:1])
            iv_se = c.op("scalar", "activation", stat[:, pb, 8:12], stat[:, pb, 4:8],
                         AF.Exp, scale=-0.5, sem="SE")
            c.wait("vector", "SE", iv_se)
            c.wait("vector", "XL", ldx)
            for j in range(4):
                c.op("vector", "tensor_scalar", xntok[:, pb, j, :], xtok[:, pb, j, :],
                     stat[:, pb, j:j + 1], stat[:, pb, 8 + j:9 + j], ALU.add, ALU.mult)
            xn_ve[pb] = c.op("vector", "nop", sem="VE")
            if "xn" in dbg_t and tc == 0:
                c.wait("gpsimd", "VE", xn_ve[pb])
                c.dma("gpsimd", dbg_t["xn"], xntok[:, 0, :, :])
            c.wait("sync", "VE", xn_ve[pb])
            c.wait("sync", "TE", xr_mm[pb])
            for j in range(4):
                trx = c.dma("sync", xnT[:, pb, :, j * 128:(j + 1) * 128],
                            xntok[:, pb, j, :], transpose=True, sem="XT")
            c.wait("tensor", "XT", trx)
            c.wait("tensor", "TE", conv_mm)
            for ct in range(24):
                pp = ct % 2
                c.wait("tensor", "VE", ev_ve.get(("xr", pp), 0))
                c.wait("tensor", "SE", ev_se.get(("xr", pp), 0))
                for kc in range(8):
                    c.op("tensor", "matmul", sps[:, pp, :],
                         winT[:, kc, ct * 128:(ct + 1) * 128],
                         xnT[:, pb, kc, :], start=(kc == 0), stop=(kc == 7))
                mm = c.op("tensor", "nop", sem="TE")
                if ct < 16:
                    c.wait("vector", "TE", mm)
                    ev_ve[("xr", pp)] = c.op(
                        "vector", "tensor_scalar_add", xipre[:, ct, 3:3 + TCA],
                        sps[:, pp, :], b_in_t[:, ct:ct + 1], sem="VE")
                else:
                    ct8 = ct - 16
                    c.wait("scalar", "TE", mm)
                    c.wait("scalar", "RS", rs_st[pb])
                    rs_st[pb] = 0
                    ev_se[("xr", pp)] = c.op(
                        "scalar", "activation", reschk[:, pb, ct8, :], sps[:, pp, :],
                        AF.Silu, bias=b_in_t[:, ct:ct + 1], sem="SE")
            xr_mm[pb] = c.cnt["TE"]
            c.wait("gpsimd", "SE", ev_se[("xr", 1)])
            c.wait("gpsimd", "SE", ev_se[("xr", 0)])
            rs_st[pb] = c.dma("gpsimd", rs_h[:, :, tc * TCA:(tc + 1) * TCA],
                              reschk[:, pb, :, :], sem="RS")
            if "xipre" in dbg_t and tc == 0:
                c.wait("gpsimd", "VE", ev_ve[("xr", 1)])
                c.wait("gpsimd", "VE", ev_ve[("xr", 0)])
                c.dma("gpsimd", dbg_t["xipre"], xipre[:, :, 3:3 + TCA])
            c.wait("tensor", "VE", ev_ve[("xr", 0)])
            c.wait("tensor", "VE", ev_ve[("xr", 1)])
            c.wait("scalar", "XS", xi_st[pb])
            for dt in range(16):
                pp = dt % 2
                c.wait("tensor", "SE", ev_se.get(("cv", pp), 0))
                for k in range(K):
                    c.op("tensor", "matmul", cps[:, pp, :], diag[:, dt * K + k, :],
                         xipre[:, dt, k:k + TCA], start=(k == 0), stop=(k == K - 1))
                mm = c.op("tensor", "nop", sem="TE")
                c.wait("scalar", "TE", mm)
                ev_se[("cv", pp)] = c.op(
                    "scalar", "activation", xichk[:, pb, dt, :], cps[:, pp, :],
                    AF.Silu, bias=b_conv_t[:, dt:dt + 1], sem="SE")
            conv_mm = c.cnt["TE"]
            c.wait("vector", "TE", conv_mm)
            c.op("vector", "tensor_copy", xipre[:, :, 0:3],
                 xipre[:, :, TCA:TCA + 3], sem="VE")
            c.wait("gpsimd", "SE", ev_se[("cv", 0)])
            c.wait("gpsimd", "SE", ev_se[("cv", 1)])
            xi_st[pb] = c.dma("gpsimd", xi_h[:, :, tc * TCA:(tc + 1) * TCA],
                              xichk[:, pb, 0:NDT, :], sem="XS")
            c.wait("tensor", "SE", ev_se[("cv", 0)])
            c.wait("tensor", "SE", ev_se[("cv", 1)])
            for ct in range(9):
                pp = ct % 2
                c.wait("tensor", "SE", ev_se.get(("pj", pp), 0))
                c.wait("tensor", "VE", ev_ve.get(("pj", pp), 0))
                for kc in range(16):
                    c.op("tensor", "matmul", pps[:, pp, :],
                         wxT[:, kc, ct * 128:(ct + 1) * 128],
                         xichk[:, pb, kc, :], start=(kc == 0), stop=(kc == 15))
                mm = c.op("tensor", "nop", sem="TE")
                if ct < 8:
                    c.wait("scalar", "TE", mm)
                    c.wait("scalar", "DS", dl_st[pb])
                    dl_st[pb] = 0
                    c.op("scalar", "activation", scr[:, 0:TCA], pps[:, pp, :],
                         AF.Exp, bias=b_x_t[:, ct:ct + 1])
                    ev_se[("pj", pp)] = c.op(
                        "scalar", "activation", dlchk[:, pb, ct, :], scr[:, 0:TCA],
                        AF.Ln, bias=1.0, sem="SE")
                else:
                    c.wait("vector", "TE", mm)
                    c.wait("vector", "BS", bc_st[pb])
                    bc_st[pb] = 0
                    ev_ve[("pj", pp)] = c.op(
                        "vector", "tensor_scalar_add", bcchk[:, pb, :],
                        pps[0:32, pp, :], b_x_t[0:32, ct:ct + 1], sem="VE")
            c.wait("gpsimd", "SE", ev_se[("pj", 0)])
            c.wait("gpsimd", "SE", ev_se[("pj", 1)])
            dl_st[pb] = c.dma("gpsimd", dl_h[:, :, tc * TCA:(tc + 1) * TCA],
                              dlchk[:, pb, :, :], sem="DS")
            c.wait("gpsimd", "VE", ev_ve[("pj", 0)])
            bc_st[pb] = c.dma("gpsimd", bc_h[:, tc * TCA:(tc + 1) * TCA],
                              bcchk[:, pb, :], sem="BS")
        for sm in ("RS", "DS", "BS", "XS"):
            c.wait("gpsimd", sm)
        c.barrier()
        es_ab.close()

        bcB = sb("bcB", [128, N, TCS], BF16)
        bcC = sb("bcC", [128, N, TCS], BF16)
        dls = sb("dls", [128, 2, TCS])
        xis = sb("xis", [128, 2, TCS], BF16)
        rss = sb("rss", [128, 2, TCS])
        wt = sb("wt", [128, 2, TCS], BF16)
        ut = sb("ut", [128, 2, TCS], BF16)
        dat = sb("dat", [128, 2, TCS])
        dbt = sb("dbt", [128, 2, TCS], BF16)
        ht = sb("ht", [128, 2, TCS], BF16)
        tnc = sb("tnc", [128, 4, TCS], BF16)
        g1 = sb("g1", [128, 2, TCS])
        ygt = sb("ygt", [128, 2, TCS], BF16)
        yps = es.enter_context(nc.psum_tensor("yps", [128, 2, TCS], F32))

        NSC = 0 if "scan" in skip else T // TCS
        last_cons = [0, 0]
        last_gate = [0, 0]
        last_ygdma = [0, 0]
        last_ymm = [0, 0, 0, 0]
        last_tnc = [0, 0]
        prev_chunk_ymm = 0
        for tc in range(NSC):
            c.wait("sync", "TE", prev_chunk_ymm)
            c.wait("sync", "BS")
            bcl = 0
            for n in range(N):
                c.dma("sync", bcB[:, n, :],
                      _ap(bc_h, n * T + tc * TCS, (0, 128), (1, TCS)), sem="BCL")
                bcl = c.dma("sync", bcC[:, n, :],
                            _ap(bc_h, (16 + n) * T + tc * TCS, (0, 128), (1, TCS)), sem="BCL")
            for dt in range(NDT):
                pd = dt % 2
                sl = slice(tc * TCS, (tc + 1) * TCS)
                c.wait("sync", "VE", last_cons[pd])
                c.dma("sync", dls[:, pd, :], dl_h[:, dt, sl], sem="SL%d" % pd)
                c.dma("sync", xis[:, pd, :], xi_h[:, dt, sl], sem="SL%d" % pd)
                sld = c.dma("sync", rss[:, pd, :], rs_h[:, dt, sl], sem="SL%d" % pd)
                c.wait("vector", "SL%d" % pd, sld)
                c.op("vector", "tensor_mul", wt[:, pd, :], dls[:, pd, :], xis[:, pd, :])
                c.op("vector", "tensor_scalar_mul", ut[:, pd, :], xis[:, pd, :],
                     d_t[:, dt:dt + 1], sem="VE")
                c.wait("scalar", "SL%d" % pd, sld)
                for n in range(N):
                    pn = n % 2
                    idx = dt * N + n
                    c.wait("scalar", "VE", last_tnc[pn])
                    da_se = c.op("scalar", "activation", dat[:, pn, :], dls[:, pd, :],
                                 AF.Exp, scale=a_t[:, idx:idx + 1], sem="SE")
                    c.wait("vector", "BCL", bcl)
                    c.wait("vector", "SE", da_se)
                    c.wait("vector", "TE", last_ymm[n % 4])
                    c.op("vector", "tensor_mul", dbt[:, pn, :], wt[:, pd, :], bcB[:, n, :])
                    c.op("vector", "tensor_tensor_scan", ht[:, pn, :], dat[:, pn, :],
                         dbt[:, pn, :], states[:, idx:idx + 1], ALU.mult, ALU.add)
                    c.op("vector", "tensor_copy", states[:, idx:idx + 1],
                         ht[:, pn, TCS - 1:TCS])
                    last_tnc[pn] = c.op("vector", "tensor_mul", tnc[:, n % 4, :],
                                        ht[:, pn, :], bcC[:, n, :], sem="VE")
                    c.wait("tensor", "VE", last_tnc[pn])
                    if n == 0:
                        c.wait("tensor", "VE", last_gate[pd])
                    for h2 in range(TCS // TCA):
                        c.op("tensor", "matmul", yps[:, pd, h2 * TCA:(h2 + 1) * TCA],
                             eye[:], tnc[:, n % 4, h2 * TCA:(h2 + 1) * TCA],
                             start=(n == 0), stop=(n == N - 1))
                    last_ymm[n % 4] = c.op("tensor", "nop", sem="TE")
                c.wait("vector", "TE", last_ymm[(N - 1) % 4])
                c.wait("vector", "YS%d" % pd, last_ygdma[pd])
                c.op("vector", "scalar_tensor_tensor", g1[:, pd, :], yps[:, pd, :],
                     1.0, ut[:, pd, :], ALU.mult, ALU.add)
                gv = c.op("vector", "tensor_mul", ygt[:, pd, :], g1[:, pd, :],
                          rss[:, pd, :], sem="VE")
                last_gate[pd] = gv
                last_cons[pd] = gv
                c.wait("gpsimd", "VE", gv)
                last_ygdma[pd] = c.dma("gpsimd", yg[:, dt, sl], ygt[:, pd, :],
                                       sem="YS%d" % pd)
            prev_chunk_ymm = c.cnt["TE"]
        c.wait("gpsimd", "YS0")
        c.wait("gpsimd", "YS1")
        c.replay(es.enter_context(nc.Block()))
    return nc



def build_l2(debug=()):
    TT2 = 2048
    NTT = TT2 // 128
    nc = bass.Bass("TRN2", target_bir_lowering=False, debug=False)
    ygb = nc.dram_tensor("ygb", [128, 16, TT2], BF16, kind="ExternalInput").ap()
    xb_t = nc.dram_tensor("xb_t", [TT2, DM], F32, kind="ExternalInput").ap()
    w_out = nc.dram_tensor("w_out", [DM, DI], F32, kind="ExternalInput").ap()
    b_out_r = nc.dram_tensor("b_out_r", [1, DM], F32, kind="ExternalInput").ap()
    kern = nc.dram_tensor("kern", [128, 8, DM], BF16, kind="ExternalInput").ap()
    s_in = nc.dram_tensor("s_in", [1, 1], F32, kind="ExternalInput").ap()
    s_k = nc.dram_tensor("s_k", [1, 1], F32, kind="ExternalInput").ap()
    fbias = nc.dram_tensor("fbias", [1, DM], BF16, kind="ExternalInput").ap()
    out_t = nc.dram_tensor("out_t", [TT2, DM], F32, kind="ExternalOutput").ap()
    dbg_t = {}
    for name, shape, dt in debug:
        dbg_t[name] = nc.dram_tensor("dbg_" + name, shape, dt, kind="ExternalOutput").ap()

    es = ExitStack()
    with es:
        c = Ctx(nc, es)
        sb = lambda n, s, d=F32: es.enter_context(nc.sbuf_tensor(n, s, d))

        ygs = sb("ygs", [128, 16, TT2], BF16)
        woT = sb("woT", [128, 16, DM], BF16)
        kq = sb("kq", [128, 8, DM], BF16)
        bout_bc = sb("bout_bc", [128, DM])
        fb_bc = sb("fb_bc", [128, DM], BF16)
        sinv = sb("sinv", [128, 3])
        ceps = sb("ceps", [128, 1])
        scr = sb("scr", [128, DM])
        scrq = sb("scrq", [128, 4, DM])
        msk = sb("msk", [128, DM], U8)
        qf = sb("qf", [128, DM])
        reck = sb("reck", [128, 1])
        reck2 = sb("reck2", [128, 1])
        ksrc = sb("ksrc", [128, DM])
        es_set = es.enter_context(ExitStack())
        sbs = lambda n, s2, d=F32: es_set.enter_context(nc.sbuf_tensor(n, s2, d))

        c.op("vector", "memset", ceps[:], EPS)
        kbf = sbs("kbf", [128, 8, DM], BF16)
        c.dma("sync", ygs[:], ygb)
        c.dma("sync", bout_bc[:], _ap(b_out_r, 0, (0, 128), (1, DM)))
        c.dma("sync", fb_bc[:], _ap(fbias, 0, (0, 128), (1, DM)))
        c.dma("sync", sinv[:, 0:1], _ap(s_in, 0, (0, 128), (1, 1)))
        c.dma("sync", sinv[:, 1:2], _ap(s_k, 0, (0, 128), (1, 1)))
        c.dma("sync", reck[:], _ap(s_k, 0, (0, 128), (1, 1)))
        c.dma("sync", kbf[:], kern)
        ld_all = c.cnt["LD"]
        stg_f = sbs("stg2_f", [128, 2, DI])
        stg_b = sbs("stg2_b", [128, 2, DI], BF16)
        tr_done = [0, 0]
        for ct in range(0 if "woprep" in skip else 8):
            p = ct % 2
            if tr_done[p]:
                c.wait("sync", "WT%d" % p, tr_done[p])
            ld = c.dma("sync", stg_f[:, p, :], w_out[ct * 128:(ct + 1) * 128, :],
                       sem="WL%d" % p)
            c.wait("vector", "WL%d" % p, ld)
            c.wait("vector", "WT%d" % p, tr_done[p])
            cd = c.op("vector", "tensor_copy", stg_b[:, p, :], stg_f[:, p, :], sem="VE")
            c.wait("sync", "VE", cd)
            tr_done[p] = c.dma("sync", woT[:, :, ct * 128:(ct + 1) * 128],
                               stg_b[:, p, :], transpose=True, sem="WT%d" % p)
        c.wait("vector", "LD", ld_all)
        c.op("vector", "tensor_mul", sinv[:, 1:2], sinv[:, 0:1], sinv[:, 1:2])
        c.op("vector", "reciprocal", sinv[:, 2:3], sinv[:, 0:1], sem="VE")
        c.op("vector", "reciprocal", reck2[:], reck[:], sem="VE")

        def quant_tile(dst_bf, src_f32, width):
            t1, t2, hi = scrq[:, 0, 0:width], scrq[:, 1, 0:width], scrq[:, 2, 0:width]
            sq = scrq[:, 3, 0:width]
            c.op("vector", "tensor_scalar_mul", t1, src_f32, VC1)
            c.op("vector", "tensor_sub", t2, t1, src_f32)
            c.op("vector", "tensor_sub", hi, t1, t2)
            c.op("vector", "tensor_scalar", qf[:, 0:width], src_f32, MAGIC, MAGIC,
                 ALU.add, ALU.subtract)
            c.op("vector", "scalar_tensor_tensor", sq, src_f32, 1.0, src_f32,
                 ALU.mult, ALU.mult)
            c.op("vector", "tensor_single_scalar", msk[:, 0:width], sq,
                 float(2.0 ** -10), ALU.is_ge)
            c.op("vector", "copy_predicated", qf[:, 0:width], msk[:, 0:width], hi)
            c.op("vector", "tensor_scalar", dst_bf, qf[:, 0:width], 448.0, -448.0,
                 ALU.min, ALU.max)

        for kc in range(0 if "kq" in skip else 8):
            c.op("vector", "tensor_copy", scr[:], kbf[:, kc, :])
            c.op("vector", "tensor_scalar_mul", ksrc[:], scr[:], reck2[:, 0:1])
            quant_tile(kq[:, kc, :], ksrc[:], DM)
        c.op("vector", "nop", sem="VE")
        c.wait("sync", "LD", ld_all)
        c.wait("sync", "WT0")
        c.wait("sync", "WT1")
        c.barrier()
        es_set.close()

        xts = sb("xts", [128, 2, DM])
        x2t = sb("x2t", [128, 2, DM])
        x2n = sb("x2n", [128, 2, DM], BF16)
        x2nf = sb("x2nf", [128, DM])
        qb = sb("qb", [128, DM], BF16)
        x2qT = sb("x2qT", [128, 2, 8, 128], BF16)
        ffnb = sb("ffnb", [128, 2, DM], BF16)
        outt = sb("outt", [128, 2, DM])
        stt = sb("stt", [128, 8])
        aps = es.enter_context(nc.psum_tensor("aps", [128, 2, DM], F32))
        fps = es.enter_context(nc.psum_tensor("fps", [128, 2, DM], F32))

        prev = {}
        for tt in range(0 if "l2" in skip else NTT):
            pb = tt % 2
            tsl = slice(tt * 128, (tt + 1) * 128)
            c.wait("sync", "VE", prev.get(("x2", tt - 2), 0))
            ldx = c.dma("sync", xts[:, pb, :],
                        _ap(xb_t, tt * 128 * DM, (DM, 128), (1, DM)), sem="XL%d" % pb)
            c.wait("tensor", "VE", prev.get(("aev", pb), 0))
            for kc in range(16):
                for hh in range(2):
                    c.op("tensor", "matmul", aps[:, pb, hh * 512:(hh + 1) * 512],
                         ygs[:, kc, tsl], woT[:, kc, hh * 512:(hh + 1) * 512],
                         start=(kc == 0), stop=(kc == 15))
            amm = c.op("tensor", "nop", sem="TE")
            c.wait("vector", "TE", amm)
            c.wait("vector", "XL%d" % pb, ldx)
            c.op("vector", "scalar_tensor_tensor", scr[:], aps[:, pb, :], 1.0,
                 bout_bc[:], ALU.mult, ALU.add)
            x2ve = c.op("vector", "tensor_add", x2t[:, pb, :], scr[:], xts[:, pb, :],
                        sem="VE")
            prev[("x2", tt)] = x2ve
            c.wait("scalar", "VE", x2ve)
            c.op("scalar", "activation", scr[:], x2t[:, pb, :], AF.Identity,
                 accum_out=stt[:, 0:1])
            sse = c.op("scalar", "activation", scr[:], x2t[:, pb, :], AF.Square,
                       accum_out=stt[:, 1:2], sem="SE")
            c.wait("vector", "SE", sse)
            c.op("vector", "tensor_scalar_mul", stt[:, 2:3], stt[:, 0:1], -1.0 / DM)
            c.op("vector", "tensor_scalar_mul", stt[:, 3:4], stt[:, 1:2], 1.0 / DM)
            c.op("vector", "tensor_tensor", stt[:, 4:5], stt[:, 2:3], stt[:, 2:3], ALU.mult)
            vve = c.op("vector", "tensor_sub", stt[:, 3:4], stt[:, 3:4], stt[:, 4:5],
                       sem="VE")
            c.wait("scalar", "VE", vve)
            c.op("scalar", "activation", stt[:, 3:4], stt[:, 3:4], AF.Ln, bias=ceps[:, 0:1])
            ivse = c.op("scalar", "activation", stt[:, 5:6], stt[:, 3:4], AF.Exp,
                        scale=-0.5, sem="SE")
            c.wait("vector", "SE", ivse)
            c.op("vector", "tensor_scalar", x2n[:, pb, :], x2t[:, pb, :],
                 stt[:, 2:3], stt[:, 5:6], ALU.add, ALU.mult)
            c.op("vector", "tensor_copy", x2nf[:], x2n[:, pb, :])
            c.op("vector", "tensor_scalar_mul", x2nf[:], x2nf[:], sinv[:, 2:3])
            quant_tile(qb[:], x2nf[:], DM)
            qve = c.op("vector", "nop", sem="VE")
            c.wait("sync", "VE", qve)
            c.wait("sync", "TE", prev.get(("fmm", pb), 0))
            trq = c.dma("sync", x2qT[:, pb, :, :], qb[:], transpose=True, sem="XT%d" % pb)
            c.wait("tensor", "XT%d" % pb, trq)
            c.wait("tensor", "VE", prev.get(("fev", pb), 0))
            for kc in range(8):
                for hh in range(2):
                    c.op("tensor", "matmul", fps[:, pb, hh * 512:(hh + 1) * 512],
                         x2qT[:, pb, kc, :], kq[:, kc, hh * 512:(hh + 1) * 512],
                         start=(kc == 0), stop=(kc == 7))
            fmm = c.op("tensor", "nop", sem="TE")
            prev[("fmm", pb)] = fmm
            c.wait("vector", "TE", fmm)
            c.op("vector", "scalar_tensor_tensor", ffnb[:, pb, :], fps[:, pb, :],
                 sinv[:, 1:2], fb_bc[:], ALU.mult, ALU.add)
            c.wait("vector", "OS%d" % pb, prev.get(("odma", tt - 2), 0))
            fev = c.op("vector", "tensor_add", outt[:, pb, :], x2t[:, pb, :],
                       ffnb[:, pb, :], sem="VE")
            prev[("fev", pb)] = fev
            prev[("aev", pb)] = fev
            c.wait("gpsimd", "VE", fev)
            c.wait("gpsimd", "OS%d" % pb, prev.get(("odma", tt - 2), 0))
            prev[("odma", tt)] = c.dma("gpsimd", out_t[tsl, :], outt[:, pb, :],
                                       sem="OS%d" % pb)
        c.wait("gpsimd", "OS0")
        c.wait("gpsimd", "OS1")
        c.replay(es.enter_context(nc.Block()))
    return nc



def _l1_inmaps(inputs):
    x = np.ascontiguousarray(inputs["L0_attn_x"])
    W_in = np.ascontiguousarray(inputs["L0_attn_W_in"])
    b_in = np.ascontiguousarray(inputs["L0_attn_b_in"])
    W_conv = np.ascontiguousarray(inputs["L0_attn_W_conv"])
    b_conv = np.ascontiguousarray(inputs["L0_attn_b_conv"])
    W_x = np.ascontiguousarray(inputs["L0_attn_W_x"])
    b_x = np.ascontiguousarray(inputs["L0_attn_b_x"])
    A_log = np.ascontiguousarray(inputs["L0_attn_A_log"])
    Dv = np.ascontiguousarray(inputs["L0_attn_D"])
    eye = np.eye(128, dtype=ml_dtypes.bfloat16)
    maps = []
    for cc in range(8):
        b, dh = cc // 2, cc % 2
        own = np.arange(1024 * dh, 1024 * dh + 1024)
        oth = np.arange(1024 * (1 - dh), 1024 * (1 - dh) + 1024)
        perm = np.concatenate([own, oth])
        w_in_s = np.concatenate([W_in[perm], W_in[DI + own]], axis=0)
        b_in_cat = np.concatenate([b_in[perm], b_in[DI + own]])
        b_in_s = np.ascontiguousarray(b_in_cat.reshape(24, 128).T)
        wc = W_conv[perm, 0, :]
        w_conv_s = np.ascontiguousarray(
            wc.reshape(16, 128, K).transpose(1, 0, 2).reshape(128, 16 * K))
        b_conv_s = np.ascontiguousarray(b_conv[perm].reshape(16, 128).T)
        wx_rows = np.concatenate([W_x[own], W_x[DI:DI + 32],
                                  np.zeros((96, DI), W_x.dtype)], axis=0)
        w_x_s = np.ascontiguousarray(wx_rows[:, perm])
        bx9 = np.zeros((128, 9), np.float32)
        bx9[:, 0:8] = b_x[own].reshape(8, 128).T
        bx9[0:32, 8] = b_x[DI:DI + 32]
        a_log_s = np.ascontiguousarray(
            A_log[own].reshape(8, 128, N).transpose(1, 0, 2).reshape(128, 8 * N))
        d_s = np.ascontiguousarray(Dv[own].reshape(8, 128).T)
        maps.append(dict(xb=x[b], w_in_s=w_in_s, b_in_s=b_in_s, w_conv_s=w_conv_s,
                         b_conv_s=b_conv_s, w_x_s=w_x_s, b_x_s=bx9,
                         a_log_s=a_log_s, d_s=d_s, eye_in=eye))
    return maps


def _l2_inmaps(inputs, yg_l1):
    x = np.ascontiguousarray(inputs["L0_attn_x"])
    W_out = np.ascontiguousarray(inputs["L0_attn_W_out"])
    b_out = np.ascontiguousarray(inputs["L0_attn_b_out"]).reshape(1, DM)
    kern = np.ascontiguousarray(inputs["L0_ffn_kernel"])
    s_in = np.ascontiguousarray(inputs["L0_ffn_input_scale"]).reshape(1, 1)
    s_k = np.ascontiguousarray(inputs["L0_ffn_kernel_scale"]).reshape(1, 1)
    fb = np.ascontiguousarray(inputs["L0_ffn_bias"]).reshape(1, DM)
    kern_r = np.ascontiguousarray(
        kern.reshape(8, 128, DM).transpose(1, 0, 2))
    maps = []
    for cc in range(8):
        b, th = cc // 2, cc % 2
        ygb = np.empty((128, 16, 2048), dtype=ml_dtypes.bfloat16)
        for g in range(16):
            src = yg_l1[2 * b + g // 8]
            ygb[:, g, :] = src[:, g % 8, th * 2048:(th + 1) * 2048]
        maps.append(dict(ygb=np.ascontiguousarray(ygb),
                         xb_t=np.ascontiguousarray(x[b, th * 2048:(th + 1) * 2048]),
                         w_out=W_out, b_out_r=b_out, kern=kern_r,
                         s_in=s_in, s_k=s_k, fbias=fb))
    return maps





def build_fused(debug=(), no_cc=False, no_branch=False, skip=()):
    from concourse.aot_env import allow_ici_all_to_all
    TT2 = 2048
    NTT = TT2 // 128
    nc = bass.Bass("TRN2", target_bir_lowering=False, debug=False)
    xb = nc.dram_tensor("xb", [T, DM], F32, kind="ExternalInput").ap()
    w_inT = nc.dram_tensor("w_inT", [DM, 24 * 128], F32, kind="ExternalInput")
    b_in_s = nc.dram_tensor("b_in_s", [128, 24], F32, kind="ExternalInput").ap()
    w_conv_s = nc.dram_tensor("w_conv_s", [128, 16 * K], F32, kind="ExternalInput").ap()
    b_conv_s = nc.dram_tensor("b_conv_s", [128, 16], F32, kind="ExternalInput").ap()
    w_xT = nc.dram_tensor("w_xT", [DI, 9 * 128], F32, kind="ExternalInput")
    b_x_s = nc.dram_tensor("b_x_s", [128, 9], F32, kind="ExternalInput").ap()
    a_log_s = nc.dram_tensor("a_log_s", [128, NDT * N], F32, kind="ExternalInput").ap()
    d_s = nc.dram_tensor("d_s", [128, NDT], F32, kind="ExternalInput").ap()
    eye_in = nc.dram_tensor("eye_in", [128, 128], BF16, kind="ExternalInput").ap()
    xb_t = nc.dram_tensor("xb_t", [TT2, DM], F32, kind="ExternalInput").ap()
    w_out = nc.dram_tensor("w_out", [DM, DI], F32, kind="ExternalInput").ap()
    b_out_r = nc.dram_tensor("b_out_r", [1, DM], F32, kind="ExternalInput").ap()
    kern = nc.dram_tensor("kern", [128, 8, DM], BF16, kind="ExternalInput").ap()
    s_in = nc.dram_tensor("s_in", [1, 1], F32, kind="ExternalInput").ap()
    s_k = nc.dram_tensor("s_k", [1, 1], F32, kind="ExternalInput").ap()
    fbias = nc.dram_tensor("fbias", [1, DM], BF16, kind="ExternalInput").ap()
    out_t = nc.dram_tensor("out_t", [TT2, DM], F32, kind="ExternalOutput").ap()
    xi_h = nc.dram_tensor("xi_h", [128, NDT, T], BF16).ap()
    dl_h = nc.dram_tensor("dl_h", [128, NDT, T], F32).ap()
    rs_h = nc.dram_tensor("rs_h", [128, NDT, T], F32).ap()
    bc_h = nc.dram_tensor("bc_h", [32, T], BF16).ap()
    w_inT_b = nc.dram_tensor("w_inT_b", [DM, 24 * 128], BF16)
    w_xT_b = nc.dram_tensor("w_xT_b", [DI, 9 * 128], BF16)
    flsh = nc.dram_tensor("flsh", [1, 1], F32)
    yg = nc.dram_tensor("yg", [128, NDT, T], BF16).ap()
    agin = nc.dram_tensor("agin", [128, NDT, TT2], BF16)
    agout = nc.dram_tensor("agout", [8, 128, NDT, TT2], BF16)
    selc = nc.dram_tensor("selc", [1, 4], mybir.dt.uint32, kind="ExternalInput").ap()
    dbg_t = {}
    for name, shape, dt in debug:
        dbg_t[name] = nc.dram_tensor("dbg_" + name, shape, dt, kind="ExternalOutput").ap()

    es = ExitStack()
    with es:
        c = Ctx(nc, es)
        sb = lambda n, s, d=F32: es.enter_context(nc.sbuf_tensor(n, s, d))
        eye = sb("eye", [128, 128], BF16)
        b_in_t = sb("b_in_t", [128, 24])
        b_conv_t = sb("b_conv_t", [128, 16])
        b_x_t = sb("b_x_t", [128, 9])
        wconv_t = sb("wconv_t", [128, 16 * K])
        a_t = sb("a_t", [128, NDT * N])
        d_t = sb("d_t", [128, NDT])
        ceps = sb("ceps", [128, 1])
        states = sb("states", [128, NDT * N])
        selc_sb = sb("selc_sb", [1, 4], mybir.dt.uint32)
        es_ab = es.enter_context(ExitStack())
        sba = lambda n, s, d=F32: es_ab.enter_context(nc.sbuf_tensor(n, s, d))
        diag = sba("diag", [128, 16 * K, 128], BF16)

        c.dma("sync", selc_sb[:], selc)
        if "l1eng" not in skip:
            c.op("vector", "memset", ceps[:], EPS)
            c.op("vector", "memset", states[:], 0.0)
        if "l1dma" not in skip:
            for dst, src in ((b_in_t, b_in_s), (b_conv_t, b_conv_s), (b_x_t, b_x_s),
                             (wconv_t, w_conv_s), (a_t, a_log_s), (d_t, d_s), (eye, eye_in)):
                c.dma("sync", dst[:], src)
        ld0 = c.cnt["LD"]
        if "l1eng" not in skip:
            c.wait("scalar", "LD", ld0)
            c.op("scalar", "activation", a_t[:], a_t[:], AF.Exp, sem="SE")
            c.wait("vector", "SE")
            c.op("vector", "tensor_scalar_mul", a_t[:], a_t[:], -1.0)
            c.wait("vector", "LD", ld0)
            for i in range(16 * K):
                c.op("vector", "tensor_scalar_mul", diag[:, i, :], eye[:], wconv_t[:, i:i + 1])
            c.op("vector", "nop", sem="VE")

        xtok = sba("xtok", [128, 2, DM])
        scr2 = sba("scr2", [128, TCA])
        ssum = sba("ssum", [128, 2, 4])
        ssq = sba("ssq", [128, 2, 4])
        stat = sba("stat", [128, 2, 12])
        vtmp = sba("vtmp", [128, 4])
        xntok = sba("xntok", [128, 2, DM], BF16)
        xnT = sba("xnT", [128, 1, 8, TCA], BF16)
        xipre = sba("xipre", [128, 16, 3 + TCA], BF16)
        xichk = sba("xichk", [128, 1, 16, TCA], BF16)
        reschk = sba("reschk", [128, 1, TCA])
        dlchk = sba("dlchk", [128, 1, TCA])
        bcchk = sba("bcchk", [32, 1, TCA], BF16)
        winTs = sba("winTs", [128, 8, 4, 128], BF16)
        wxTs = sba("wxTs", [128, 16, 3, 128], BF16)
        bcB = sba("bcB", [128, N, TCS], BF16)
        bcC = sba("bcC", [128, N, TCS], BF16)
        dls = sba("dls", [128, 2, TCS])
        xis = sba("xis", [128, 2, TCS], BF16)
        rss = sba("rss", [128, 8, TCS])
        wt = sba("wt", [128, 2, TCS], BF16)
        ut = sba("ut", [128, 8, TCS], BF16)
        dat = sba("dat", [128, 2, TCS])
        dbt = sba("dbt", [128, 2, TCS], BF16)
        ht = sba("ht", [128, 2, TCS])
        tnc = sba("tnc", [128, 2, N, TCS], BF16)
        g1 = sba("g1", [128, 1, TCS])
        ygt = sba("ygt", [128, 1, TCS], BF16)
        sps = es_ab.enter_context(nc.psum_tensor("sps", [128, 4, TCA], F32))
        pps = es_ab.enter_context(nc.psum_tensor("pps", [128, 2, TCA], F32))
        yps = es_ab.enter_context(nc.psum_tensor("yps", [128, 2, TCS], F32))

        datf = bass.AP(dat, 0, [[2 * TCS, 128], [1, 2 * TCS]])
        cone = sba("cone", [128, 1])
        czero = sba("czero", [128, 1])
        c.op("vector", "memset", cone[:], 1.0)
        c.op("vector", "memset", czero[:], 0.0)
        c.op("vector", "memset", xipre[:, :, 0:3], 0.0)
        c.op("vector", "nop", sem="VE")
        setup_ve = c.cnt["VE"]
        c.wait("scalar", "VE", setup_ve)

        ldx0 = 0
        if "ab" not in skip:
            ldx0 = c.dma("gpsimd", xtok[:, 0, :],
                         _ap(xb, 0, (DM, 128), (1, DM)), sem="XL")
        cast_in = [None] * 24
        cast_x = [None] * 9
        if "wprep" not in skip:
            cast_prev = [0, 0]
            tiles = [("in", ct) for ct in range(24)] + [("x", ct) for ct in range(9)]
            for i, (fam, ct) in enumerate(tiles):
                p = i % 2
                if cast_prev[p]:
                    c.wait("gpsimd", "WC%d" % p, cast_prev[p])
                if fam == "in":
                    srcap = _ap(w_inT, ct * 128, (24 * 128, DM), (1, 128))
                    dstap = _ap(w_inT_b, ct * 128, (24 * 128, DM), (1, 128))
                else:
                    srcap = _ap(w_xT, ct * 128, (9 * 128, DI), (1, 128))
                    dstap = _ap(w_xT_b, ct * 128, (9 * 128, DI), (1, 128))
                cast_prev[p] = c.dma("gpsimd", dstap, srcap, sem="WC%d" % p)
                if fam == "in":
                    cast_in[ct] = ("WC%d" % p, cast_prev[p])
                else:
                    cast_x[ct] = ("WC%d" % p, cast_prev[p])
        for _sm in ("WI0", "WI1", "WI2", "WI3", "WX0", "WX1", "WX2",
                    "WC0", "WC1", "RS0", "DS0", "BS", "XS", "YS0", "YS1",
                    "SL0", "SL1", "BCL", "XT", "TX"):
            c.sem(_sm)

        NCH = 0 if "ab" in skip else T // TCA
        do_scan = ("scan" not in skip) and NCH > 0
        xn_ve = [0]
        xr_mm = [0, 0]
        ev_ve = {}
        ev_se = {}
        xi_st = [0, 0]
        dl_st = [0]
        rs_st = [0]
        bc_st = [0]
        dl_list = [0] * 8
        rs_list = [0] * 8
        xnk_tr = [0, 0]
        dl_rec = {}
        wi_mm = [0, 0, 0, 0]
        wx_mm = [0, 0, 0]
        wi_ld = {}
        wx_ld = {}
        st_done = [0, 0, 0, 0]
        xn_j_ve = [0, 0]
        psum_se = {}
        psum_ve = {}
        xi_evac = {}
        st_rec = {}
        pending = []

        def drain(nmax):
            for _ in range(nmax):
                if pending:
                    pending.pop(0)()

        def drain_all():
            while pending:
                pending.pop(0)()
        sl_ld = [0, 0]
        wtu_done = [0, 0]
        dat_done = [0, 0]
        gate_done = [0] * NDT
        gate_slot = [0, 0]
        ymm_rec = [0, 0]
        ym_of = {}
        last_tts = [0, 0]
        bc_used = [0]
        bcl_v = [0]
        last_ygdma = [0, 0]
        core_done = {}

        def ab_ldx(tc):
            return ldx0 if tc == 0 else 0

        def ab_head(tc, ldx):
            pb = tc % 2
            for _lt in last_tts:
                c.wait("scalar", "VE", _lt)
            ldj = [ldx, 0, 0, 0]
            for j in range(4):
                sj = j % 2
                if not (tc == 0 and j == 0):
                    if j >= 2:
                        c.wait("sync", "SE", st_done[j - 2])
                    c.wait("sync", "VE", xn_j_ve[sj])
                    ldj[j] = c.dma("sync", xtok[:, sj, :],
                                   _ap(xb, (tc * TCA + j * 128) * DM,
                                       (DM, 128), (1, DM)), sem="XL")
                c.wait("scalar", "XL", ldj[j])
                c.op("scalar", "activation", datf, xtok[:, sj, :], AF.Identity,
                     accum_out=ssum[:, pb, j:j + 1])
                st_done[j] = c.op("scalar", "activation", datf, xtok[:, sj, :],
                                  AF.Square, accum_out=ssq[:, pb, j:j + 1],
                                  sem="SE")
            st_se = st_done[3]
            c.wait("vector", "SE", st_se)
            c.op("vector", "tensor_scalar_mul", stat[:, pb, 0:4], ssum[:, pb, :],
                 -1.0 / DM)
            c.op("vector", "tensor_scalar_mul", stat[:, pb, 4:8], ssq[:, pb, :],
                 1.0 / DM)
            c.op("vector", "tensor_tensor", vtmp[:], stat[:, pb, 0:4],
                 stat[:, pb, 0:4], ALU.mult)
            vw = c.op("vector", "tensor_sub", stat[:, pb, 4:8], stat[:, pb, 4:8],
                      vtmp[:], sem="VE")
            c.wait("scalar", "VE", vw)
            c.op("scalar", "activation", stat[:, pb, 4:8], stat[:, pb, 4:8],
                 AF.Ln, bias=ceps[:, 0:1])
            iv_se = c.op("scalar", "activation", stat[:, pb, 8:12], stat[:, pb, 4:8],
                         AF.Exp, scale=-0.5, sem="SE")
            c.wait("vector", "SE", iv_se)
            c.wait("sync", "TE", xr_mm[(tc - 1) % 2])
            trx = 0
            for j in range(4):
                sj = j % 2
                c.wait("sync", "SE", st_done[3])
                c.wait("sync", "VE", xn_j_ve[sj])
                ld2 = c.dma("sync", xtok[:, sj, :],
                            _ap(xb, (tc * TCA + j * 128) * DM, (DM, 128), (1, DM)),
                            sem="XL")
                c.wait("vector", "XL", ld2)
                c.wait("vector", "XT", xnk_tr[sj])
                xv = c.op("vector", "tensor_scalar", xntok[:, sj, :], xtok[:, sj, :],
                          stat[:, pb, j:j + 1], stat[:, pb, 8 + j:9 + j],
                          ALU.add, ALU.mult, sem="VE")
                xn_j_ve[sj] = xv
                c.wait("sync", "VE", xv)
                xnk_tr[sj] = tr_flush("sync", xnT[:, 0, :, j * 128:(j + 1) * 128],
                                      xntok[:, sj, :], "XT")
            xn_ve[0] = xv
            return xnk_tr[1]

        def scan_bc(j):
            rec = st_rec[j]
            c.wait("sync", "VE", bc_used[0])
            c.wait("sync", "BS", rec["bc"])
            for n in range(N):
                c.dma("sync", bcB[:, n, :],
                      _ap(bc_h, n * T + j * TCS, (0, 128), (1, TCS)), sem="BCL")
                bcl_v[0] = c.dma("sync", bcC[:, n, :],
                                 _ap(bc_h, (16 + n) * T + j * TCS, (0, 128), (1, TCS)),
                                 sem="BCL")

        def scan_loads(j, dt):
            pd = dt % 2
            sl = slice(j * TCS, (j + 1) * TCS)
            rec = st_rec[j]
            c.wait("sync", "VE", wtu_done[pd])
            c.wait("sync", "SE", dat_done[pd])
            c.wait("sync", "VE", gate_done[dt])
            c.wait("sync", "DS0", rec["dl"][dt])
            c.wait("sync", "XS", rec["xi"])
            c.wait("sync", "RS0", rec["rs"][dt])
            c.dma("sync", dls[:, pd, :], dl_h[:, dt, sl], sem="SL%d" % pd)
            c.dma("sync", xis[:, pd, :], xi_h[:, dt, sl], sem="SL%d" % pd)
            sl_ld[pd] = c.dma("sync", rss[:, dt, :], rs_h[:, dt, sl], sem="SL%d" % pd)

        def scan_pre(j):
            scan_bc(j)
            scan_loads(j, 0)
            scan_loads(j, 1)

        def scan_core(j, dt):
            pd = dt % 2
            c.wait("vector", "TE", ymm_rec[pd])
            c.wait("vector", "SL%d" % pd, sl_ld[pd])
            c.op("vector", "tensor_mul", wt[:, pd, :], dls[:, pd, :], xis[:, pd, :])
            wtu_done[pd] = c.op("vector", "tensor_scalar_mul", ut[:, dt, :],
                                xis[:, pd, :], d_t[:, dt:dt + 1], sem="VE")
            c.wait("scalar", "SL%d" % pd, sl_ld[pd])
            cd = 0
            da_se = 0
            for n in range(N):
                pn = n % 2
                p4 = n % 2
                idx = dt * N + n
                c.wait("scalar", "VE", last_tts[p4])
                da_se = c.op("scalar", "activation", dat[:, p4, :], dls[:, pd, :],
                             AF.Exp, scale=a_t[:, idx:idx + 1], bias=czero[:, 0:1],
                             sem="SE")
                drain(2 if len(pending) > 40 else 1)
                c.wait("vector", "BCL", bcl_v[0])
                c.wait("vector", "SE", da_se)
                c.op("vector", "tensor_mul", dbt[:, pn, :], wt[:, pd, :], bcB[:, n, :])
                last_tts[p4] = c.op("vector", "tensor_tensor_scan", ht[:, pn, :],
                                    dat[:, p4, :], dbt[:, pn, :],
                                    states[:, idx:idx + 1], ALU.mult, ALU.add,
                                    sem="VE")
                c.op("vector", "tensor_copy", states[:, idx:idx + 1],
                     ht[:, pn, TCS - 1:TCS])
                cd = c.op("vector", "tensor_mul", tnc[:, pd, n, :],
                          ht[:, pn, :], bcC[:, n, :], sem="VE")
            dat_done[pd] = da_se
            core_done[(j, dt)] = cd
            if dt == NDT - 1:
                bc_used[0] = cd
            if 0 < dt < NDT - 2 + 1 and dt + 2 < NDT:
                scan_loads(j, dt + 2)
            elif dt == 0:
                scan_loads(j, 2)

        def scan_tail_t(j, dt):
            pd = dt % 2
            c.wait("tensor", "VE", core_done[(j, dt)])
            c.wait("tensor", "VE", gate_slot[pd])
            for n in range(N):
                c.op("tensor", "matmul", yps[:, pd, :], eye[:], tnc[:, pd, n, :],
                     start=(n == 0), stop=(n == N - 1))
            ym = c.op("tensor", "nop", sem="TE")
            ymm_rec[pd] = ym
            ym_of[(j, dt)] = ym

        def scan_tail_v(j, dt):
            pd = dt % 2
            sl = slice(j * TCS, (j + 1) * TCS)
            c.wait("vector", "TE", ym_of[(j, dt)])
            c.wait("vector", "YS0", last_ygdma[0])
            c.wait("vector", "YS1", last_ygdma[1])
            c.op("vector", "scalar_tensor_tensor", g1[:, 0, :], yps[:, pd, :],
                 1.0, ut[:, dt, :], ALU.mult, ALU.add)
            gv = c.op("vector", "tensor_mul", ygt[:, 0, :], g1[:, 0, :],
                      rss[:, dt, :], sem="VE")
            gate_done[dt] = gv
            gate_slot[pd] = gv
            c.wait("gpsimd", "VE", gv)
            last_ygdma[pd] = c.dma("gpsimd", yg[:, dt, sl], ygt[:, 0, :],
                                   sem="YS%d" % pd)

        slot_t = [(0, 0), (0, 1), (1, 0), (1, 1)]

        def tr_flush(queue, dst, src, sem):
            c.dma(queue, dst, src, transpose=True, sem="TX")
            return c.dma(queue, flsh.ap(), cone[0:1, 0:1], sem=sem)

        mm_at = {}
        cv_evac = {}
        conv_last = [0]
        pj_last = [0]
        halo_ve = [0]

        def ab_build(tc, trx):
            pb = tc % 2
            st_rec[tc] = {"xi": 0, "dl": [0] * 8, "rs": [0] * 8, "bc": 0}

            def wl_in(ct):
                sw = ct % 4
                c.wait("sync", "TE", wi_mm[sw])
                if tc == 0 and cast_in[ct] is not None:
                    c.wait("sync", cast_in[ct][0], cast_in[ct][1])
                wi_ld[ct] = c.dma("sync", winTs[:, :, sw, :],
                                  _ap(w_inT_b, ct * 128, (24 * 128, 128),
                                      (128 * 24 * 128, 8), (1, 128)),
                                  sem="WI%d" % sw)

            def xr_tile(ct):
                sw = ct % 4
                s4 = ct % 4
                c.wait("tensor", "XT", trx)
                c.wait("tensor", "WI%d" % sw, wi_ld[ct])
                c.wait("tensor", "SE", psum_se.get(("sps", s4), 0))
                c.wait("tensor", "VE", psum_ve.get(("sps", s4), 0))
                for kc in range(8):
                    c.op("tensor", "matmul", sps[:, s4, :], winTs[:, kc, sw, :],
                         xnT[:, 0, kc, :], start=(kc == 0), stop=(kc == 7))
                mm = c.op("tensor", "nop", sem="TE")
                wi_mm[sw] = mm
                mm_at[("xr", ct)] = mm
                if ct == 23:
                    xr_mm[pb] = mm

            def xr_evac(ct):
                s4 = ct % 4
                mmv = mm_at[("xr", ct)]
                if ct < 16:
                    c.wait("vector", "TE", mmv)
                    ev = c.op("vector", "tensor_scalar_add", xipre[:, ct, 3:3 + TCA],
                              sps[:, s4, :], b_in_t[:, ct:ct + 1], sem="VE")
                    xi_evac[ct] = ev
                    psum_ve[("sps", s4)] = ev
                    return
                c.wait("scalar", "TE", mmv)
                if False:
                    pass
                else:
                    ct8 = ct - 16
                    c.wait("scalar", "RS0", rs_st[0])
                    ev = c.op("scalar", "activation", reschk[:, 0, :], sps[:, s4, :],
                              AF.Silu, scale=cone[:, 0:1],
                              bias=b_in_t[:, ct:ct + 1], sem="SE")
                    c.wait("gpsimd", "SE", ev)
                    rs_st[0] = c.dma("gpsimd",
                                     rs_h[:, ct8, tc * TCA:(tc + 1) * TCA],
                                     reschk[:, 0, :], sem="RS0")
                    st_rec[tc]["rs"][ct8] = rs_st[0]
                psum_se[("sps", s4)] = ev

            def conv_tile(dt):
                s4 = dt % 4
                c.wait("tensor", "SE", psum_se.get(("sps", s4), 0))
                c.wait("tensor", "VE", psum_ve.get(("sps", s4), 0))
                c.wait("tensor", "VE", xi_evac[dt])
                c.wait("tensor", "VE", halo_ve[0])
                c.wait("tensor", "VE", setup_ve)
                for k2 in range(K):
                    c.op("tensor", "matmul", sps[:, s4, :], diag[:, dt * K + k2, :],
                         xipre[:, dt, k2:k2 + TCA], start=(k2 == 0), stop=(k2 == K - 1))
                mm_at[("cv", dt)] = c.op("tensor", "nop", sem="TE")
                if dt == 15:
                    conv_last[0] = mm_at[("cv", dt)]

            def conv_evac(dt):
                s4 = dt % 4
                c.wait("scalar", "TE", mm_at[("cv", dt)])
                if dt == 0:
                    c.wait("scalar", "XS", xi_st[pb])
                    c.wait("scalar", "TE", pj_last[0])
                ev = c.op("scalar", "activation", xichk[:, 0, dt, :], sps[:, s4, :],
                          AF.Silu, scale=cone[:, 0:1],
                          bias=b_conv_t[:, dt:dt + 1], sem="SE")
                psum_se[("sps", s4)] = ev
                cv_evac[dt] = ev
                if dt == NDT - 1:
                    c.wait("gpsimd", "SE", ev)
                    xi_st[pb] = c.dma("gpsimd", xi_h[:, :, tc * TCA:(tc + 1) * TCA],
                                      xichk[:, 0, 0:NDT, :], sem="XS")
                    st_rec[tc]["xi"] = xi_st[pb]

            def wl_x(ct):
                sw = ct % 3
                c.wait("sync", "TE", wx_mm[sw])
                if tc == 0 and cast_x[ct] is not None:
                    c.wait("sync", cast_x[ct][0], cast_x[ct][1])
                wx_ld[ct] = c.dma("sync", wxTs[:, :, sw, :],
                                  _ap(w_xT_b, ct * 128, (9 * 128, 128),
                                      (128 * 9 * 128, 16), (1, 128)),
                                  sem="WX%d" % sw)

            def proj_tile(ct):
                sw = ct % 3
                pw = ct % 2
                c.wait("tensor", "WX%d" % sw, wx_ld[ct])
                c.wait("tensor", "SE", psum_se.get(("pps", pw), 0))
                c.wait("tensor", "VE", psum_ve.get(("pps", pw), 0))
                c.wait("tensor", "SE", cv_evac[15])
                for kc in range(16):
                    c.op("tensor", "matmul", pps[:, pw, :], wxTs[:, kc, sw, :],
                         xichk[:, 0, kc, :], start=(kc == 0), stop=(kc == 15))
                mm = c.op("tensor", "nop", sem="TE")
                mm_at[("pj", ct)] = mm
                wx_mm[sw] = mm
                if ct == 8:
                    pj_last[0] = mm

            def pj_evac(ct):
                pw = ct % 2
                c.wait("scalar", "TE", mm_at[("pj", ct)])
                c.wait("scalar", "DS0", dl_st[0])
                c.op("scalar", "activation", scr2[:], pps[:, pw, :], AF.Exp,
                     scale=cone[:, 0:1], bias=b_x_t[:, ct:ct + 1])
                ev = c.op("scalar", "activation", dlchk[:, 0, :], scr2[:], AF.Ln,
                          scale=cone[:, 0:1], bias=cone[:, 0:1], sem="SE")
                psum_se[("pps", pw)] = ev
                c.wait("gpsimd", "SE", ev)
                dl_st[0] = c.dma("gpsimd", dl_h[:, ct, tc * TCA:(tc + 1) * TCA],
                                 dlchk[:, 0, :], sem="DS0")
                st_rec[tc]["dl"][ct] = dl_st[0]

            def bc_halo():
                c.wait("vector", "TE", mm_at[("pj", 8)])
                c.wait("vector", "BS", bc_st[0])
                c.wait("vector", "BCL", bcl_v[0])
                bv = c.op("vector", "tensor_scalar_add", bcchk[:, 0, :],
                          pps[0:32, 0, :], b_x_t[0:32, 8:9], sem="VE")
                psum_ve[("pps", 0)] = bv
                c.wait("gpsimd", "VE", bv)
                bc_st[0] = c.dma("gpsimd", bc_h[:, tc * TCA:(tc + 1) * TCA],
                                 bcchk[:, 0, :], sem="BS")
                st_rec[tc]["bc"] = bc_st[0]
                c.wait("vector", "TE", conv_last[0])
                halo_ve[0] = c.op("vector", "tensor_copy", xipre[:, :, 0:3],
                                  xipre[:, :, TCA:TCA + 3], sem="VE")

            for ct in range(3):
                pending.append(lambda ct=ct: wl_in(ct))
            for ct in range(24):
                pending.append(lambda ct=ct: xr_tile(ct))
                if ct + 3 < 24:
                    pending.append(lambda ct=ct: wl_in(ct + 3))
                pending.append(lambda ct=ct: xr_evac(ct))
            for dt in range(16):
                pending.append(lambda dt=dt: conv_tile(dt))
                if dt == 0:
                    pending.append(lambda: wl_x(0))
                if dt == 8:
                    pending.append(lambda: wl_x(1))
                pending.append(lambda dt=dt: conv_evac(dt))
            for ct in range(9):
                pending.append(lambda ct=ct: proj_tile(ct))
                if ct + 2 < 9:
                    pending.append(lambda ct=ct: wl_x(ct + 2))
                if ct < 8:
                    pending.append(lambda ct=ct: pj_evac(ct))
            pending.append(bc_halo)

        for k in range(NCH + (1 if do_scan else 0)):
            j = k - 1
            em_ab = k < NCH
            em_sc = do_scan and 0 <= j < NCH
            if em_ab:
                ldx = ab_ldx(k)
            if em_sc:
                scan_pre(j)
            if em_ab:
                trx = ab_head(k, ldx)
            if em_sc:
                scan_core(j, 0)
                scan_core(j, 1)
                scan_tail_t(j, 0)
            if em_ab:
                ab_build(k, trx)
            if em_sc:
                scan_core(j, 2)
                scan_tail_t(j, 1)
                scan_tail_v(j, 0)
                for dt in range(3, NDT):
                    scan_core(j, dt)
                    scan_tail_t(j, dt - 1)
                    scan_tail_v(j, dt - 2)
                scan_tail_t(j, NDT - 1)
                scan_tail_v(j, NDT - 2)
                scan_tail_v(j, NDT - 1)
            drain_all()

        for sm in ("RS0", "DS0", "BS", "XS"):
            c.wait("gpsimd", sm)
        c.wait("sync", "TX")
        c.wait("sync", "LD")
        c.wait("gpsimd", "YS0")
        c.wait("gpsimd", "YS1")
        c.sem("AGC")
        c.cnt["AGC"] += 16
        agc_v = c.cnt["AGC"]

        def _agin_copy(g, _selt=None):
            if no_branch:
                g.dma_start(out=agin.ap(), in_=yg[:, :, TT2:2 * TT2]).then_inc(
                    c.sems["AGC"], 16)
                return
            with g.register() as r:
                g.load(r, selc_sb[0:1, 0:1])
                with g.If_eq(r, 0):
                    g.dma_start(out=agin.ap(), in_=yg[:, :, TT2:2 * TT2]).then_inc(
                        c.sems["AGC"], 16)
                with g.Else():
                    g.dma_start(out=agin.ap(), in_=yg[:, :, 0:TT2]).then_inc(
                        c.sems["AGC"], 16)
        c.em("gpsimd", _agin_copy)
        c.wait("gpsimd", "AGC", agc_v)
        if no_cc:
            c.sem("CC")
            c.cnt["CC"] += 1
            c.em("gpsimd", lambda e: e.sem_inc(c.sems["CC"], 1))
        else:
            c.sem("CC")
            c.cnt["CC"] += 1
            c.em("gpsimd", lambda e: e.collective_compute(
                "AllGather", ALU.bypass,
                replica_groups=[list(range(8))],
                ins=[agin.ap().opt()], outs=[agout.ap().opt()]).then_inc(c.sems["CC"], 1))
        c.barrier()
        es_ab.close()

        ygs = sb("ygs", [128, 16, TT2], BF16)
        woT = sb("woT", [128, 16, DM], BF16)
        kq = sb("kq", [128, 8, DM], BF16)
        bout_bc = sb("bout_bc", [128, DM])
        fb_bc = sb("fb_bc", [128, DM], BF16)
        sinv = sb("sinv", [128, 3])
        scr = sb("scr", [128, DM])
        scrq = sb("scrq", [128, 4, DM])
        msk = sb("msk", [128, DM], U8)
        qf = sb("qf", [128, DM])
        reck = sb("reck", [128, 1])
        reck2 = sb("reck2", [128, 1])
        ksrc = sb("ksrc", [128, DM])
        es_set = es.enter_context(ExitStack())
        sbs = lambda n, s2, d=F32: es_set.enter_context(nc.sbuf_tensor(n, s2, d))

        kbf = sbs("kbf", [128, 8, DM], BF16)
        c.sem("L2S")
        if "l2dma" not in skip:
            c.dma("sync", bout_bc[:], _ap(b_out_r, 0, (0, 128), (1, DM)), sem="L2S")
            c.dma("sync", fb_bc[:], _ap(fbias, 0, (0, 128), (1, DM)), sem="L2S")
            c.dma("sync", sinv[:, 0:1], _ap(s_in, 0, (0, 128), (1, 1)), sem="L2S")
            c.dma("sync", sinv[:, 1:2], _ap(s_k, 0, (0, 128), (1, 1)), sem="L2S")
            c.dma("sync", reck[:], _ap(s_k, 0, (0, 128), (1, 1)), sem="L2S")
            c.dma("sync", kbf[:], kern, sem="L2S")
        ld_all = c.cnt["L2S"]
        stg_f = sbs("stg2_f", [128, 2, DI])
        stg_b = sbs("stg2_b", [128, 2, DI], BF16)
        tr_done = [0, 0]
        for ct in range(0 if "woprep" in skip else 8):
            p = ct % 2
            if tr_done[p]:
                c.wait("sync", "WT%d" % p, tr_done[p])
            ld = c.dma("sync", stg_f[:, p, :], w_out[ct * 128:(ct + 1) * 128, :],
                       sem="WL%d" % p)
            c.wait("vector", "WL%d" % p, ld)
            c.wait("vector", "WT%d" % p, tr_done[p])
            cd = c.op("vector", "tensor_copy", stg_b[:, p, :], stg_f[:, p, :], sem="VE")
            c.wait("sync", "VE", cd)
            tr_done[p] = c.dma("sync", woT[:, :, ct * 128:(ct + 1) * 128],
                               stg_b[:, p, :], transpose=True, sem="WT%d" % p)
        if "l2eng" not in skip:
            c.wait("vector", "L2S", ld_all)
            c.op("vector", "tensor_mul", sinv[:, 1:2], sinv[:, 0:1], sinv[:, 1:2])
            c.op("vector", "reciprocal", sinv[:, 2:3], sinv[:, 0:1], sem="VE")
            c.op("vector", "reciprocal", reck2[:], reck[:], sem="VE")

        def quant_tile(dst_bf, src_f32, width):
            t1, t2, hi = scrq[:, 0, 0:width], scrq[:, 1, 0:width], scrq[:, 2, 0:width]
            sq = scrq[:, 3, 0:width]
            c.op("vector", "tensor_scalar_mul", t1, src_f32, VC1)
            c.op("vector", "tensor_sub", t2, t1, src_f32)
            c.op("vector", "tensor_sub", hi, t1, t2)
            c.op("vector", "tensor_scalar", qf[:, 0:width], src_f32, MAGIC, MAGIC,
                 ALU.add, ALU.subtract)
            c.op("vector", "scalar_tensor_tensor", sq, src_f32, 1.0, src_f32,
                 ALU.mult, ALU.mult)
            c.op("vector", "tensor_single_scalar", msk[:, 0:width], sq,
                 float(2.0 ** -10), ALU.is_ge)
            c.op("vector", "copy_predicated", qf[:, 0:width], msk[:, 0:width], hi)
            c.op("vector", "tensor_scalar", dst_bf, qf[:, 0:width], 448.0, -448.0,
                 ALU.min, ALU.max)

        for kc in range(0 if "kq" in skip else 8):
            c.op("vector", "tensor_copy", scr[:], kbf[:, kc, :])
            c.op("vector", "tensor_scalar_mul", ksrc[:], scr[:], reck2[:, 0:1])
            quant_tile(kq[:, kc, :], ksrc[:], DM)
        c.op("vector", "nop", sem="VE")
        c.sem("YGL")
        c.cnt["YGL"] += 32
        ygl = c.cnt["YGL"]

        def _ygs_loads(g):
            if no_branch:
                g.dma_start(out=ygs[:, 0:8, :], in_=yg[:, :, 0:TT2]).then_inc(c.sems["YGL"], 16)
                g.dma_start(out=ygs[:, 8:16, :], in_=agout.ap()[0]).then_inc(c.sems["YGL"], 16)
                return
            with g.register() as r:
                g.load(r, selc_sb[0:1, 0:1])
                with g.If_eq(r, 0):
                    g.dma_start(out=ygs[:, 0:8, :],
                                in_=yg[:, :, 0:TT2]).then_inc(c.sems["YGL"], 16)
                with g.Else():
                    g.dma_start(out=ygs[:, 8:16, :],
                                in_=yg[:, :, TT2:2 * TT2]).then_inc(c.sems["YGL"], 16)
            with g.register() as r2:
                g.load(r2, selc_sb[0:1, 1:2])
                for pc in range(8):
                    dst = ygs[:, 0:8, :] if pc % 2 == 0 else ygs[:, 8:16, :]
                    with g.If_eq(r2, pc):
                        g.dma_start(out=dst,
                                    in_=agout.ap()[pc]).then_inc(c.sems["YGL"], 16)
        c.wait("sync", "CC")
        c.em("sync", _ygs_loads)
        c.wait("sync", "L2S", ld_all)
        c.wait("sync", "WT0")
        c.wait("sync", "WT1")
        c.wait("sync", "YGL", ygl)
        c.barrier()
        es_set.close()

        xts = sb("xts", [128, 2, DM])
        x2t = sb("x2t", [128, 2, DM])
        x2n = sb("x2n", [128, 2, DM], BF16)
        x2nf = sb("x2nf", [128, DM])
        qb = sb("qb", [128, DM], BF16)
        x2qT = sb("x2qT", [128, 2, 8, 128], BF16)
        ffnb = sb("ffnb", [128, 2, DM], BF16)
        outt = sb("outt", [128, 2, DM])
        stt = sb("stt", [128, 8])
        aps = es.enter_context(nc.psum_tensor("aps", [128, 2, DM], F32))
        fps = es.enter_context(nc.psum_tensor("fps", [128, 2, DM], F32))

        prev = {}
        for tt in range(0 if "l2" in skip else NTT):
            pb = tt % 2
            tsl = slice(tt * 128, (tt + 1) * 128)
            c.wait("sync", "VE", prev.get(("x2", tt - 2), 0))
            ldx = c.dma("sync", xts[:, pb, :],
                        _ap(xb_t, tt * 128 * DM, (DM, 128), (1, DM)), sem="XL%d" % pb)
            c.wait("tensor", "VE", prev.get(("aev", pb), 0))
            for kc in range(16):
                for hh in range(2):
                    c.op("tensor", "matmul", aps[:, pb, hh * 512:(hh + 1) * 512],
                         ygs[:, kc, tsl], woT[:, kc, hh * 512:(hh + 1) * 512],
                         start=(kc == 0), stop=(kc == 15))
            amm = c.op("tensor", "nop", sem="TE")
            c.wait("vector", "TE", amm)
            c.wait("vector", "XL%d" % pb, ldx)
            c.op("vector", "scalar_tensor_tensor", scr[:], aps[:, pb, :], 1.0,
                 bout_bc[:], ALU.mult, ALU.add)
            x2ve = c.op("vector", "tensor_add", x2t[:, pb, :], scr[:], xts[:, pb, :],
                        sem="VE")
            prev[("x2", tt)] = x2ve
            c.wait("scalar", "VE", x2ve)
            c.op("scalar", "activation", scr[:], x2t[:, pb, :], AF.Identity,
                 accum_out=stt[:, 0:1])
            sse = c.op("scalar", "activation", scr[:], x2t[:, pb, :], AF.Square,
                       accum_out=stt[:, 1:2], sem="SE")
            c.wait("vector", "SE", sse)
            c.op("vector", "tensor_scalar_mul", stt[:, 2:3], stt[:, 0:1], -1.0 / DM)
            c.op("vector", "tensor_scalar_mul", stt[:, 3:4], stt[:, 1:2], 1.0 / DM)
            c.op("vector", "tensor_tensor", stt[:, 4:5], stt[:, 2:3], stt[:, 2:3], ALU.mult)
            vve = c.op("vector", "tensor_sub", stt[:, 3:4], stt[:, 3:4], stt[:, 4:5],
                       sem="VE")
            c.wait("scalar", "VE", vve)
            c.op("scalar", "activation", stt[:, 3:4], stt[:, 3:4], AF.Ln, bias=ceps[:, 0:1])
            ivse = c.op("scalar", "activation", stt[:, 5:6], stt[:, 3:4], AF.Exp,
                        scale=-0.5, sem="SE")
            c.wait("vector", "SE", ivse)
            c.op("vector", "tensor_scalar", x2n[:, pb, :], x2t[:, pb, :],
                 stt[:, 2:3], stt[:, 5:6], ALU.add, ALU.mult)
            c.op("vector", "tensor_copy", x2nf[:], x2n[:, pb, :])
            c.op("vector", "tensor_scalar_mul", x2nf[:], x2nf[:], sinv[:, 2:3])
            quant_tile(qb[:], x2nf[:], DM)
            qve = c.op("vector", "nop", sem="VE")
            c.wait("sync", "VE", qve)
            c.wait("sync", "TE", prev.get(("fmm", pb), 0))
            trq = c.dma("sync", x2qT[:, pb, :, :], qb[:], transpose=True, sem="XT%d" % pb)
            c.wait("tensor", "XT%d" % pb, trq)
            c.wait("tensor", "VE", prev.get(("fev", pb), 0))
            for kc in range(8):
                for hh in range(2):
                    c.op("tensor", "matmul", fps[:, pb, hh * 512:(hh + 1) * 512],
                         x2qT[:, pb, kc, :], kq[:, kc, hh * 512:(hh + 1) * 512],
                         start=(kc == 0), stop=(kc == 7))
            fmm = c.op("tensor", "nop", sem="TE")
            prev[("fmm", pb)] = fmm
            c.wait("vector", "TE", fmm)
            c.op("vector", "scalar_tensor_tensor", ffnb[:, pb, :], fps[:, pb, :],
                 sinv[:, 1:2], fb_bc[:], ALU.mult, ALU.add)
            c.wait("vector", "OS%d" % pb, prev.get(("odma", tt - 2), 0))
            fev = c.op("vector", "tensor_add", outt[:, pb, :], x2t[:, pb, :],
                       ffnb[:, pb, :], sem="VE")
            prev[("fev", pb)] = fev
            prev[("aev", pb)] = fev
            c.wait("gpsimd", "VE", fev)
            prev[("odma", tt)] = c.dma("gpsimd", out_t[tsl, :], outt[:, pb, :],
                                       sem="OS%d" % pb)
        c.wait("gpsimd", "OS0")
        c.wait("gpsimd", "OS1")
        with allow_ici_all_to_all():
            c.replay(es.enter_context(nc.Block()))
    return nc


def _fused_inmaps(inputs):
    l1 = _l1_inmaps(inputs)
    x = np.ascontiguousarray(inputs["L0_attn_x"])
    W_out = np.ascontiguousarray(inputs["L0_attn_W_out"])
    b_out = np.ascontiguousarray(inputs["L0_attn_b_out"]).reshape(1, DM)
    kern = np.ascontiguousarray(inputs["L0_ffn_kernel"])
    s_in = np.ascontiguousarray(inputs["L0_ffn_input_scale"]).reshape(1, 1)
    s_k = np.ascontiguousarray(inputs["L0_ffn_kernel_scale"]).reshape(1, 1)
    fb = np.ascontiguousarray(inputs["L0_ffn_bias"]).reshape(1, DM)
    kern_r = np.ascontiguousarray(kern.reshape(8, 128, DM).transpose(1, 0, 2))
    maps = []
    for cc in range(8):
        b, th = cc // 2, cc % 2
        m = dict(l1[cc])
        m["w_inT"] = np.ascontiguousarray(m.pop("w_in_s").T)
        m["w_xT"] = np.ascontiguousarray(m.pop("w_x_s").T)
        m.update(dict(xb_t=np.ascontiguousarray(x[b, th * 2048:(th + 1) * 2048]),
                      w_out=W_out, b_out_r=b_out, kern=kern_r,
                      s_in=s_in, s_k=s_k, fbias=fb,
                      selc=np.array([[th, cc ^ 1, 0, 0]], dtype=np.uint32)))
        maps.append(m)
    return maps


def kernel(**inputs):
    nc = build_fused()
    r = run_bass_kernel_spmd(nc, _fused_inmaps(inputs), core_ids=list(range(8)))
    out = np.empty((B, T, DM), np.float32)
    for cc in range(8):
        b, th = cc // 2, cc % 2
        out[b, th * 2048:(th + 1) * 2048] = r.results[cc]["out_t"]
    return out


def kernel_2launch(**inputs):
    nc1 = build_l1()
    r1 = run_bass_kernel_spmd(nc1, _l1_inmaps(inputs), core_ids=list(range(8)))
    yg_l1 = [r1.results[cc]["yg"] for cc in range(8)]
    nc2 = build_l2()
    r2 = run_bass_kernel_spmd(nc2, _l2_inmaps(inputs, yg_l1), core_ids=list(range(8)))
    out = np.empty((B, T, DM), np.float32)
    for cc in range(8):
        b, th = cc // 2, cc % 2
        out[b, th * 2048:(th + 1) * 2048] = r2.results[cc]["out_t"]
    return out
